# version 13
# baseline (speedup 1.0000x reference)
from contextlib import ExitStack

import numpy as np
import ml_dtypes

import concourse.bass as bass
import concourse.mybir as mybir
from concourse.bass_utils import run_bass_kernel_spmd

F32 = mybir.dt.float32
BF16 = mybir.dt.bfloat16
I32 = mybir.dt.int32
AX = mybir.AxisListType.X
OP = mybir.AluOpType
AF = mybir.ActivationFunctionType

P = 128
D = 1024
NCH = 8
N_CORES = 8

DMA_ENGINES = ("qs", "qa", "qg")
ALL_ENGINES = ("pe", "dve", "sc", "gp", "qs", "qa", "qg")
PHYS = {"pe": "tensor", "dve": "vector", "sc": "scalar", "gp": "gpsimd",
        "qs": "sync", "qa": "scalar", "qg": "gpsimd"}


NSUB = 8


class Op:
    __slots__ = ("engine", "fn", "reads", "writes", "idx", "eng_pos",
                 "need_inc", "waits", "tag", "semname", "qidx")

    def __init__(self, engine, fn, reads, writes, idx, tag):
        self.engine = engine
        self.fn = fn
        self.reads = reads
        self.writes = writes
        self.idx = idx
        self.eng_pos = 0
        self.need_inc = False
        self.waits = {}
        self.tag = tag
        self.semname = engine
        self.qidx = 0


class Prog:
    def __init__(self, paranoid=False, paranoid_tags=(),
                 paranoid_engines=("dve", "sc", "gp")):
        self.ops = []
        self.qcount = {q: 0 for q in DMA_ENGINES}
        self.paranoid = paranoid
        self.paranoid_tags = tuple(paranoid_tags)
        self.paranoid_engines = tuple(paranoid_engines)

    def _ptag(self, o):
        return any(o.tag.startswith(p) for p in self.paranoid_tags)

    def op(self, engine, fn, reads=(), writes=(), tag=""):
        assert engine in ALL_ENGINES, engine
        o = Op(engine, fn, tuple(reads), tuple(writes), len(self.ops), tag)
        if engine in DMA_ENGINES:
            o.qidx = self.qcount[engine]
            self.qcount[engine] += 1
            o.semname = f"{engine}{o.qidx % NSUB}"
        self.ops.append(o)
        return o

    def resolve(self):
        last_write = {}
        readers = {}
        deps_of = [set() for _ in self.ops]
        for o in self.ops:
            deps = deps_of[o.idx]
            for k in o.reads:
                w = last_write.get(k)
                if w is not None:
                    deps.add(w.idx)
            for k in o.writes:
                w = last_write.get(k)
                if w is not None:
                    deps.add(w.idx)
                for r in readers.get(k, ()):
                    deps.add(r.idx)
            for k in o.writes:
                last_write[k] = o
                readers[k] = []
            for k in o.reads:
                readers.setdefault(k, []).append(o)
        for o in self.ops:
            if o.engine in DMA_ENGINES:
                o.need_inc = True
            for d in deps_of[o.idx]:
                dop = self.ops[d]
                if (self.paranoid or dop.engine != o.engine
                        or dop.engine in DMA_ENGINES or self._ptag(o)
                        or o.engine in self.paranoid_engines):
                    dop.need_inc = True
        counts = {}
        for o in self.ops:
            if o.need_inc:
                counts[o.semname] = counts.get(o.semname, 0) + (
                    16 if o.engine in DMA_ENGINES else 1)
            o.eng_pos = counts.get(o.semname, 0)
        for o in self.ops:
            if o.engine in DMA_ENGINES and o.qidx >= NSUB:
                o.waits[o.semname] = max(
                    o.waits.get(o.semname, 0), o.eng_pos - 16)
            for d in deps_of[o.idx]:
                dop = self.ops[d]
                if (not self.paranoid and not self._ptag(o)
                        and o.engine not in self.paranoid_engines
                        and dop.engine == o.engine
                        and dop.engine not in DMA_ENGINES):
                    continue
                if not dop.need_inc:
                    continue
                cur = o.waits.get(dop.semname, 0)
                o.waits[dop.semname] = max(cur, dop.eng_pos)
        seen = {}
        for o in self.ops:
            s = seen.setdefault(o.engine, {})
            for e in list(o.waits):
                v = o.waits[e]
                if s.get(e, -1) >= v:
                    del o.waits[e]
                else:
                    s[e] = v

    def emit(self, nc, sems):
        self.resolve()
        by_phys = {}
        for o in self.ops:
            by_phys.setdefault(PHYS[o.engine], []).append(o)

        def run_stream(eng_obj, ops_list):
            for o in ops_list:
                for e, v in sorted(o.waits.items()):
                    eng_obj.wait_ge(sems[e], v)
                last = o.fn()
                if o.need_inc:
                    assert last is not None, f"op {o.tag} returned None"
                    last.then_inc(sems[o.semname],
                                  16 if o.engine in DMA_ENGINES else 1)

        with nc.Block() as block:
            if "sync" in by_phys:
                @block.sync
                def _(eng):
                    run_stream(eng, by_phys["sync"])
            if "tensor" in by_phys:
                @block.tensor
                def _(eng):
                    run_stream(eng, by_phys["tensor"])
            if "vector" in by_phys:
                @block.vector
                def _(eng):
                    run_stream(eng, by_phys["vector"])
            if "scalar" in by_phys:
                @block.scalar
                def _(eng):
                    run_stream(eng, by_phys["scalar"])
            if "gpsimd" in by_phys:
                @block.gpsimd
                def _(eng):
                    run_stream(eng, by_phys["gpsimd"])


def build_program(B=4, T=2048, n_blocks=4, paranoid=False, debug_taps=(),
                  paranoid_tags=()):
    NT1 = T // 128
    NT5 = T // 512
    NQ = T // 512

    nc = bass.Bass("TRN2", target_bir_lowering=False, debug=False,
                   detect_race_conditions=paranoid,
                   dynamic_dma_scratch_size=64)

    x_in = nc.dram_tensor("x", [B, T, D], F32, kind="ExternalInput").ap()
    y_out = nc.dram_tensor("y", [B, T, D], F32, kind="ExternalOutput").ap()

    def win(name, shape):
        return nc.dram_tensor(name, shape, F32, kind="ExternalInput").ap()

    Wq = win("L0_attn_Wq", [D, D]); bq = win("L0_attn_bq", [D])
    Wk = win("L0_attn_Wk", [D, D]); bk = win("L0_attn_bk", [D])
    Wv = win("L0_attn_Wv", [D, D]); bv = win("L0_attn_bv", [D])
    Wo = win("L0_attn_Wo", [D, D]); bo = win("L0_attn_bo", [D])
    Wf = win("L0_ffn_W", [D, D]); bfv = win("L0_ffn_b", [D])
    lora_down_h = win("L0_ffn_lora_down", [4, 100])
    lora_up_h = win("L0_ffn_lora_up", [50, 4])
    down_aux_h = win("L0_ffn_down_aux", [100, D])
    up_aux_h = win("L0_ffn_up_aux", [D, 50])
    Wqkv = win("L1_attn_w_qkv", [3 * D, D]); bqkv = win("L1_attn_b_qkv", [3 * D])
    Wg1 = win("L1_attn_w_g1", [64, D]); bg1v = win("L1_attn_b_g1", [64])
    Wg2 = win("L1_attn_w_g2", [D, 64]); bg2v = win("L1_attn_b_g2", [D])
    Wout = win("L1_attn_w_out", [D, D]); boutv = win("L1_attn_b_out", [D])
    ln_gv = win("L1_attn_ln_g", [D]); ln_bv = win("L1_attn_ln_b", [D])
    W1 = win("L1_ffn_w1", [4 * D, D])
    convw_h = win("L1_ffn_conv_w", [4 * D, 1, 3])
    convb_h = win("L1_ffn_conv_b", [4 * D])
    W2 = win("L1_ffn_w2", [D, 4 * D])

    c_ident = nc.dram_tensor("c_ident", [P, P], F32, kind="ExternalInput").ap()
    c_identb = nc.dram_tensor("c_identb", [P, P], BF16, kind="ExternalInput").ap()
    c_ones_col = nc.dram_tensor("c_ones_col", [1, P], F32, kind="ExternalInput").ap()
    c_onesT = nc.dram_tensor("c_onesT", [P, P], F32, kind="ExternalInput").ap()
    c_ones128 = nc.dram_tensor("c_ones128", [P, 1], F32, kind="ExternalInput").ap()
    c_sel16 = nc.dram_tensor("c_sel16", [16, D], F32, kind="ExternalInput").ap()

    wqT_d = nc.dram_tensor("wqT_d", [D, D], F32, kind="Internal").ap()
    wkT_d = nc.dram_tensor("wkT_d", [D, D], F32, kind="Internal").ap()
    wvT_d = nc.dram_tensor("wvT_d", [D, D], F32, kind="Internal").ap()
    woT_d = nc.dram_tensor("woT_d", [D, D], F32, kind="Internal").ap()
    weffT_d = nc.dram_tensor("weffT_d", [D, D], F32, kind="Internal").ap()
    qkvT_d = nc.dram_tensor("qkvT_d", [D, 3 * D], F32, kind="Internal").ap()
    woutT_d = nc.dram_tensor("woutT_d", [D, D], F32, kind="Internal").ap()
    g1T_d = nc.dram_tensor("g1T_d", [D, 64], F32, kind="Internal").ap()
    g2T_d = nc.dram_tensor("g2T_d", [64, D], F32, kind="Internal").ap()
    t1T_d = nc.dram_tensor("t1T_d", [D, 4 * D], BF16, kind="Internal").ap()
    t2T_d = nc.dram_tensor("t2T_d", [4 * D, D], BF16, kind="Internal").ap()

    pg = Prog(paranoid=paranoid, paranoid_tags=paranoid_tags)

    with ExitStack() as ctx:
        E = ctx.enter_context

        def sb(name, shape, dt=F32):
            return E(nc.sbuf_tensor(name, shape, dt))

        x_all = sb("x_all", [P, NCH * T])
        bigB = sb("bigB", [P, 16384])
        bigC = sb("bigC", [P, 8192])
        qpool = sb("qpool", [P, 4608])
        band = sb("band", [P, 1536])
        nm_row = sb("nm_row", [65, T])
        rows = sb("rows", [P, 4096])
        kv_acc = sb("kv_acc", [P, 1024])
        norm_lhs = sb("norm_lhs", [P, 16 * NCH])
        scr128 = sb("scr128", [P, P])
        kvblk = scr128
        rstd_tm = sb("rstd_tm", [P, NT1])
        var_tm = sb("var_tm", [P, NT1])
        amax_q = sb("amax_q", [P, 8])
        den_sb = sb("den_sb", [P, 8])
        s_w = sb("s_w", [P, 2])
        swr = sb("swr", [P, 1])
        red_acc = sb("red_acc", [P, 1])
        wsum_tmp = sb("wsum_tmp", [P, 24])
        t1s = sb("t1s", [P, 2 * D], BF16)
        t2s = sb("t2s", [P, 2 * D], BF16)
        qcarry = sb("qcarry", [P, 16], BF16)
        icarry = sb("icarry", [P, 1])
        ident = sb("ident", [P, P])
        identb = sb("identb", [P, P], BF16)
        onesT = sb("onesT", [P, P])
        bq_c = sb("bq_c", [P, 8]); bo_c = sb("bo_c", [P, 8])
        bf_c = sb("bf_c", [P, 8]); bout_c = sb("bout_c", [P, 8])
        bq1_c = sb("bq1_c", [P, 8]); bg2_c = sb("bg2_c", [P, 8])
        lng_c = sb("lng_c", [P, 8]); lnb_c = sb("lnb_c", [P, 8])
        bg1_c = sb("bg1_c", [64, 1])
        convw_c = sb("convw_c", [P, 96])
        convb_c = sb("convb_c", [P, 32])
        negone = sb("negone", [P, 1])
        cst2 = sb("cst2", [P, 2])
        nc.const_aps.register(F32, 0.5, cst2[:, 0:1])
        nc.const_aps.register(F32, -0.5, cst2[:, 1:2])
        nc.const_aps.register(F32, -1.0, negone[:, 0:1])

        pb = [E(nc.psum_tensor(f"pb{i}", [P, 512], F32)) for i in range(8)]
        PB = [f"pb{i}" for i in range(8)]

        sems = {}
        for e in ("pe", "dve", "sc", "gp"):
            sems[e] = E(nc.semaphore(f"sem_{e}"))
        for q in DMA_ENGINES:
            for j in range(NSUB):
                sems[f"{q}{j}"] = E(nc.semaphore(f"sem_{q}{j}"))
        TEN, VEC, SCA, GPS, SYN = (nc.tensor, nc.vector, nc.scalar,
                                   nc.gpsimd, nc.sync)

        def xc(c):
            return x_all[:, c * T:(c + 1) * T]

        XK = [f"x{c}" for c in range(NCH)]

        def xstr(tau):
            return x_all[:, 0:NCH * T].rearrange(
                "p (c t) -> p c t", c=NCH)[:, :, tau * P:(tau + 1) * P]

        def bB(i):
            return bigB[:, i * 1024:(i + 1) * 1024]

        KB = [f"bB{i}" for i in range(16)]

        def bC(i):
            return bigC[:, i * 1024:(i + 1) * 1024]

        KC = [f"bC{i}" for i in range(8)]

        def qp(i):
            return qpool[:, i * 1024:min((i + 1) * 1024, 4608)]

        KQ = [f"qp{i}" for i in range(5)]

        bkbc = None
        bvbc = None
        wres = [bB(c) for c in range(8)]
        wres2 = [bB(8 + c) for c in range(8)]
        WK1 = [f"bB{c}" for c in range(8)]
        WK2 = [f"bB{8 + c}" for c in range(8)]
        hga = bigB[:, 0:32 * 512]
        bkbc = bigC[:, 3 * 1024:4 * 1024]
        bvbc = bigC[:, 4 * 1024:5 * 1024]
        norm16t = band[0:16, 1024:1536]
        sel16 = band[0:16, 0:D]
        ghv = band[64:128, 1024:1536]

        ROWS = {}
        for i, nmn in enumerate(["wq", "weff", "q1", "g1", "bk", "bv",
                                 "bk1", "bv1", "wk", "wv", "k1", "v1"]):
            ROWS[nmn] = (32 * (i % 3), (i // 3) * 1024)

        def row(nmn, lo, n):
            p0, f0 = ROWS[nmn]
            return rows[p0:p0 + 1, f0 + lo:f0 + lo + n]

        def rowbase(nmn):
            return ROWS[nmn][0]

        def nmr_at(nmn, lo, n):
            b = ROWS[nmn][0]
            return nm_row[b:b + 1, lo:lo + n]

        def ones_at(nmn):
            b = ROWS[nmn][0]
            return onesT[b:b + 1, 0:P]

        def dma(q, out, in_, reads, writes, tag="", slow=False):
            eng = {"qs": SYN, "qa": SCA, "qg": GPS}[q]

            def _go(out=out, in_=in_, eng=eng, slow=slow, tag=tag):
                try:
                    return eng.dma_start(out=out, in_=in_,
                                         allow_slow_non_contiguous=slow)
                except Exception as e:
                    raise RuntimeError(f"dma op {tag!r} failed: {e}") from e

            pg.op(q, _go, reads=reads, writes=writes, tag=tag)

        def vop(fn, reads, writes, tag="", eng="dve"):
            pg.op(eng, fn, reads=reads, writes=writes, tag=tag)

        dma("qs", ident[:], c_ident, [], ["ident"], "c0")
        dma("qs", identb[:], c_identb, [], ["identb"], "c1")
        dma("qs", onesT[:], c_onesT, [], ["onesT"], "c3")
        dma("qs", sel16, c_sel16, [], ["sel16"], "c5")
        vop(lambda: VEC.memset(negone[:, :], -1.0), [], ["negone"], "c6")
        vop(lambda: VEC.memset(cst2[:, 0:1], 0.5), [], ["cst2"], "c7")
        vop(lambda: VEC.memset(cst2[:, 1:2], -0.5), [], ["cst2"], "c8")

        def load_bias_col(dst, dst_key, src_vec, n, tag):
            tmp = bB(0)
            dma("qs", tmp[0:n, 0:P], src_vec.rearrange("(a b) -> a b", b=P),
                [], ["bB0"], f"{tag}l")
            pg.op("pe", lambda tmp=tmp, n=n: TEN.transpose(
                    pb[0][0:P, 0:n], tmp[0:n, 0:P], ident[0:n, 0:n]),
                  reads=["bB0", "ident"], writes=["pb0"], tag=f"{tag}t")
            vop(lambda dst=dst, n=n: VEC.tensor_copy(dst[:, 0:n],
                                                     pb[0][:, 0:n]),
                ["pb0"], [dst_key, "pb0"], f"{tag}e")

        load_bias_col(bq_c, "bq_c", bq, 8, "bq")
        load_bias_col(bo_c, "bo_c", bo, 8, "bo")
        load_bias_col(bf_c, "bf_c", bfv, 8, "bf")
        load_bias_col(bout_c, "bout_c", boutv, 8, "bout")
        load_bias_col(bq1_c, "bq1_c", bqkv[0:D], 8, "bq1")
        load_bias_col(bg2_c, "bg2_c", bg2v, 8, "bg2")
        load_bias_col(lng_c, "lng_c", ln_gv, 8, "lng")
        load_bias_col(lnb_c, "lnb_c", ln_bv, 8, "lnb")
        dma("qs", bB(0)[0:1, 0:64], bg1v.rearrange("(a b) -> a b", a=1),
            [], ["bB0"], "bg1l")
        pg.op("pe", lambda: TEN.transpose(pb[0][0:64, 0:1], bB(0)[0:1, 0:64],
                                          ident[0:1, 0:1]),
              reads=["bB0", "ident"], writes=["pb0"], tag="bg1t")
        vop(lambda: VEC.tensor_copy(bg1_c[:, :], pb[0][0:64, 0:1]),
            ["pb0"], ["bg1_c", "pb0"], "bg1e")
        for nmn, vec in (("bk", bk), ("bv", bv), ("bk1", bqkv[D:2 * D]),
                         ("bv1", bqkv[2 * D:3 * D])):
            dma("qs", row(nmn, 0, D), vec.rearrange("(a b) -> a b", a=1),
                [], ["rows"], f"r_{nmn}")
        for j in range(32):
            dma("qs", convw_c[:, 3 * j:3 * j + 3],
                convw_h.rearrange("a b c -> (b a) c")[j * P:(j + 1) * P, :],
                [], ["convw_c"], f"cw{j}")
        dma("qs", convb_c[:, :], convb_h.rearrange("(a b) -> b a", b=P),
            [], ["convb_c"], "cb", slow=True)

        lora_s = bigB[0:100, 2 * 1024:2 * 1024 + 4]
        lora_ut = bigB[0:4, 2 * 1024 + 16:2 * 1024 + 66]
        msum = bigB[0:50, 2 * 1024 + 80:2 * 1024 + 81]
        daux = bigB[0:100, 3 * 1024:4 * 1024]
        lora_a = bigB[0:4, 4 * 1024:5 * 1024]
        lora_m = bigB[0:50, 5 * 1024:6 * 1024]
        upaux_t = bigB[0:50, 6 * 1024:7 * 1024]

        dma("qs", bB(1)[0:50, 0:4], lora_up_h, [], ["bB1"], "lu")
        dma("qs", bB(1)[0:4, 100:200], lora_down_h, [], ["bB1"], "ldn")
        dma("qs", daux, down_aux_h, [], ["bB3"], "dax")
        pg.op("pe", lambda: TEN.transpose(pb[0][0:100, 0:4],
                                          bB(1)[0:4, 100:200],
                                          ident[0:4, 0:4]),
              reads=["bB1", "ident"], writes=["pb0"], tag="ldT")
        vop(lambda: VEC.tensor_copy(lora_s, pb[0][0:100, 0:4]),
            ["pb0"], ["bB2", "pb0"], "ldTe")
        pg.op("pe", lambda: TEN.transpose(pb[0][0:4, 0:50], bB(1)[0:50, 0:4],
                                          ident[0:50, 0:50]),
              reads=["bB1", "ident"], writes=["pb0"], tag="luT")
        vop(lambda: VEC.tensor_copy(lora_ut, pb[0][0:4, 0:50]),
            ["pb0"], ["bB2", "pb0"], "luTe")
        for ic in range(NCH):
            dma("qs", bB(1)[0:P, 0:50], up_aux_h[ic * P:(ic + 1) * P, :],
                [], ["bB1"], f"ua{ic}")
            pg.op("pe", lambda: TEN.transpose(pb[1][0:50, 0:P],
                                              bB(1)[0:P, 0:50], ident[:, :]),
                  reads=["bB1", "ident"], writes=["pb1"], tag=f"uaT{ic}")
            vop(lambda ic=ic: VEC.tensor_copy(
                    upaux_t[:, ic * P:(ic + 1) * P], pb[1][0:50, 0:P]),
                ["pb1"], ["bB6", "pb1"], f"uaTe{ic}")
        for h in range(2):
            pg.op("pe", lambda h=h: TEN.matmul(
                    pb[0][0:4, :], lora_s, daux[:, h * 512:(h + 1) * 512],
                    start=True, stop=True),
                  reads=["bB2", "bB3"], writes=["pb0"], tag=f"lA{h}")
            vop(lambda h=h: VEC.tensor_copy(
                    lora_a[:, h * 512:(h + 1) * 512], pb[0][0:4, :]),
                ["pb0"], ["bB4", "pb0"], f"lAe{h}")
        for h in range(2):
            pg.op("pe", lambda h=h: TEN.matmul(
                    pb[0][0:50, :], lora_ut, lora_a[:, h * 512:(h + 1) * 512],
                    start=True, stop=True),
                  reads=["bB2", "bB4"], writes=["pb0"], tag=f"lM{h}")
            vop(lambda h=h: VEC.tensor_copy(
                    lora_m[:, h * 512:(h + 1) * 512], pb[0][0:50, :]),
                ["pb0"], ["bB5", "pb0"], f"lMe{h}")
        vop(lambda: VEC.reduce_sum(msum, lora_m, AX),
            ["bB5"], ["bB2"], "lMs")

        def prep_w(W_hbm, wT_d, key, n_out, wsum_names, lora, tag):
            noc = n_out // P
            for oc in range(noc):
                nb = oc % 2
                wn = bB(nb)
                dma("qa", wn[:, :], W_hbm[oc * P:(oc + 1) * P, :],
                    [], [KB[nb]], f"{tag}l{oc}")
                if wsum_names:
                    vop(lambda wn=wn, oc=oc: VEC.reduce_sum(
                            wsum_tmp[:, oc:oc + 1], wn[:, :], AX),
                        [KB[nb]], ["wsum_tmp"], f"{tag}w{oc}")
                for ic in range(NCH):
                    bank = 2 + (ic % 4)
                    pg.op("pe", lambda wn=wn, ic=ic, bank=bank:
                          TEN.transpose(pb[bank][:, 0:P],
                                        wn[:, ic * P:(ic + 1) * P],
                                        ident[:, :]),
                          reads=[KB[nb], "ident"], writes=[PB[bank]],
                          tag=f"{tag}T{oc}_{ic}")
                    if lora:
                        pg.op("pe", lambda ic=ic, oc=oc, bank=bank:
                              TEN.matmul(pb[bank][:, 0:P],
                                         lora_m[:, ic * P:(ic + 1) * P],
                                         upaux_t[:, oc * P:(oc + 1) * P],
                                         start=False, stop=True,
                                         skip_group_check=True),
                              reads=["bB5", "bB6"], writes=[PB[bank]],
                              tag=f"{tag}L{oc}_{ic}")
                    eng = "sc" if ic % 2 == 0 else "dve"
                    fn = SCA.copy if eng == "sc" else VEC.tensor_copy
                    vop(lambda fn=fn, ic=ic, bank=bank: fn(
                            bigC[:, 4096 + (ic % 4) * P:
                                 4096 + (ic % 4) * P + P],
                            pb[bank][:, 0:P]),
                        [PB[bank]], ["bC4", PB[bank]], f"{tag}e{oc}_{ic}",
                        eng=eng)
                    if ic % 4 == 3:
                        lo = (ic - 3) * P
                        dma("qs", wT_d[lo:lo + 512,
                                       oc * P:(oc + 1) * P].rearrange(
                                "(a p) b -> p a b", p=P),
                            bigC[:, 4096:4096 + 512],
                            ["bC4"], [key], f"{tag}s{oc}_{ic}")
            if wsum_names:
                for gi, wname in enumerate(wsum_names):
                    pg.op("pe", lambda gi=gi: TEN.transpose(
                            pb[6][0:8, 0:P], wsum_tmp[:, gi * 8:gi * 8 + 8],
                            ident[:, :]),
                          reads=["wsum_tmp", "ident"], writes=[PB[6]],
                          tag=f"{tag}wT{gi}")
                    vop(lambda gi=gi: VEC.tensor_copy(
                            scr128[0:8, 0:P], pb[6][0:8, 0:P]),
                        [PB[6]], ["scr128", PB[6]], f"{tag}we{gi}")
                    for rr in range(8):
                        dma("qs", row(wname, rr * P, P),
                            scr128[rr:rr + 1, 0:P],
                            ["scr128"], ["rows"], f"{tag}wf{gi}_{rr}")

        prep_w(Wq, wqT_d, "wqT_d", D, ["wq"], False, "pq")
        prep_w(Wk, wkT_d, "wkT_d", D, ["wk"], False, "pk")
        prep_w(Wv, wvT_d, "wvT_d", D, ["wv"], False, "pv")
        prep_w(Wo, woT_d, "woT_d", D, [], False, "po")
        prep_w(Wf, weffT_d, "weffT_d", D, ["weff"], True, "pf")
        for h in range(2):
            pg.op("pe", lambda h=h: TEN.matmul(
                    pb[7][0:1, :], msum,
                    upaux_t[:, h * 512:(h + 1) * 512], start=True, stop=True),
                  reads=["bB2", "bB6"], writes=[PB[7]], tag=f"pfw{h}")
            vop(lambda h=h: VEC.tensor_add(
                    row("weff", h * 512, 512), row("weff", h * 512, 512),
                    pb[7][0:1, :]),
                [PB[7], "rows"], ["rows", PB[7]], f"pfwa{h}")
        prep_w(Wqkv, qkvT_d, "qkvT_d", 3 * D, ["q1", "k1", "v1"], False, "pj")
        prep_w(Wout, woutT_d, "woutT_d", D, [], False, "pu")
        dma("qs", bB(0)[0:64, 0:D], Wg1, [], ["bB0"], "g1l")
        for ic in range(NCH):
            pg.op("pe", lambda ic=ic: TEN.transpose(
                    pb[2][0:P, 0:64], bB(0)[0:64, ic * P:(ic + 1) * P],
                    ident[0:64, 0:64]),
                  reads=["bB0", "ident"], writes=[PB[2]], tag=f"g1T{ic}")
            vop(lambda: VEC.tensor_copy(bB(1)[:, 0:64], pb[2][:, 0:64]),
                [PB[2]], ["bB1", PB[2]], f"g1e{ic}")
            dma("qs", g1T_d[ic * P:(ic + 1) * P, :], bB(1)[:, 0:64],
                ["bB1"], ["g1T_d"], f"g1s{ic}")
        vop(lambda: VEC.reduce_sum(scr128[0:64, 0:1], bB(0)[0:64, 0:D], AX),
            ["bB0"], ["scr128"], "g1w")
        pg.op("pe", lambda: TEN.transpose(pb[2][0:1, 0:64],
                                          scr128[0:64, 0:1],
                                          ident[0:64, 0:64]),
              reads=["scr128", "ident"], writes=[PB[2]], tag="g1wT")
        vop(lambda: VEC.tensor_copy(row("g1", 0, 64), pb[2][0:1, 0:64]),
            [PB[2]], ["rows", PB[2]], "g1we")
        for ic in range(NCH):
            dma("qs", bB(0)[0:P, 0:64], Wg2[ic * P:(ic + 1) * P, :],
                [], ["bB0"], f"g2l{ic}")
            pg.op("pe", lambda: TEN.transpose(pb[3][0:64, 0:P],
                                              bB(0)[0:P, 0:64], ident[:, :]),
                  reads=["bB0", "ident"], writes=[PB[3]], tag=f"g2T{ic}")
            vop(lambda: VEC.tensor_copy(bB(1)[0:64, 0:P], pb[3][0:64, 0:P]),
                [PB[3]], ["bB1", PB[3]], f"g2e{ic}")
            dma("qs", g2T_d[:, ic * P:(ic + 1) * P], bB(1)[0:64, 0:P],
                ["bB1"], ["g2T_d"], f"g2s{ic}")

        q_t = bB(2); q_d = bB(3); q_g = bB(4)
        q_i = bB(5).bitcast(I32)[:, 0:D]
        q_b = bB(6).bitcast(BF16)[:, 0:D]
        q_o = bB(7).bitcast(BF16)[:, 0:D]

        def prep_tern(W_hbm, tT_d, key, n_out, n_in, sw_col, tag):
            noc = n_out // P
            nig = n_in // D
            first = True
            for oc in range(noc):
                for ig in range(nig):
                    nb = (oc * nig + ig) % 2
                    wn = bB(nb)
                    dma("qa", wn[:, :],
                        W_hbm[oc * P:(oc + 1) * P, ig * D:(ig + 1) * D],
                        [], [KB[nb]], f"{tag}ml{oc}_{ig}")
                    vop(lambda wn=wn: VEC.reduce_sum(
                            scr128[:, 1:2], wn[:, :], AX,
                            apply_absolute_value=True),
                        [KB[nb]], ["scr128"], f"{tag}mr{oc}_{ig}")
                    if first:
                        vop(lambda: VEC.tensor_copy(red_acc[:, :],
                                                    scr128[:, 1:2]),
                            ["scr128"], ["red_acc"], f"{tag}mi")
                        first = False
                    else:
                        vop(lambda: VEC.tensor_add(
                                red_acc[:, :], red_acc[:, :], scr128[:, 1:2]),
                            ["scr128", "red_acc"], ["red_acc"],
                            f"{tag}ma{oc}_{ig}")
            pg.op("pe", lambda: TEN.matmul(pb[4][0:1, 0:1], red_acc[:, :],
                                           onesT[:, 0:1], start=True,
                                           stop=True),
                  reads=["red_acc", "onesT"], writes=[PB[4]],
                  tag=f"{tag}ms")
            vop(lambda n_out=n_out, n_in=n_in: VEC.tensor_scalar(
                    scr128[0:1, 0:1], pb[4][0:1, 0:1],
                    1.0 / (n_out * n_in), 1e-5, OP.mult, OP.max),
                [PB[4]], ["scr128", PB[4]], f"{tag}mc")
            vop(lambda: VEC.reciprocal(scr128[0:1, 1:2], scr128[0:1, 0:1]),
                ["scr128"], ["scr128"], f"{tag}mrc")
            pg.op("pe", lambda sw_col=sw_col: TEN.matmul(
                    pb[4][:, sw_col:sw_col + 1], onesT[0:1, 0:P],
                    scr128[0:1, 1:2], start=True, stop=True),
                  reads=["scr128", "onesT"], writes=[PB[4]],
                  tag=f"{tag}mp")
            vop(lambda sw_col=sw_col: VEC.tensor_copy(
                    s_w[:, sw_col:sw_col + 1], pb[4][:, sw_col:sw_col + 1]),
                [PB[4]], ["s_w", PB[4]], f"{tag}me")
            for oc in range(noc):
                for ig in range(nig):
                    nb = (oc * nig + ig) % 2
                    wn = bB(nb)
                    dma("qa", wn[:, :],
                        W_hbm[oc * P:(oc + 1) * P, ig * D:(ig + 1) * D],
                        [], [KB[nb]], f"{tag}tl{oc}_{ig}")
                    vop(lambda wn=wn, sw_col=sw_col: GPS.tensor_tensor(
                            q_t[:, :], wn[:, :],
                            s_w[:, sw_col:sw_col + 1].broadcast_to((P, D)),
                            OP.mult),
                        [KB[nb], "s_w"], ["bB2"], f"{tag}q1_{oc}_{ig}",
                        eng="gp")
                    vop(lambda: GPS.tensor_copy(q_i, q_t[:, :]),
                        ["bB2"], ["bB5"], f"{tag}q2_{oc}_{ig}", eng="gp")
                    vop(lambda: GPS.tensor_tensor(
                            q_d[:, :], q_t[:, :], q_i, OP.subtract),
                        ["bB2", "bB5"], ["bB3"], f"{tag}q3_{oc}_{ig}",
                        eng="gp")
                    pg.op("sc", lambda: SCA.activation(
                            q_g[:, :], q_d[:, :], AF.Sign, bias=-0.5,
                            scale=1.0),
                          reads=["bB3", "cst2"], writes=["bB4"],
                          tag=f"{tag}q4_{oc}_{ig}")
                    pg.op("sc", lambda: SCA.activation(
                            q_g[:, :], q_g[:, :], AF.Relu),
                          reads=["bB4"], writes=["bB4"],
                          tag=f"{tag}q4b_{oc}_{ig}")
                    pg.op("sc", lambda: SCA.activation(
                            q_d[:, :], q_d[:, :], AF.Sign, bias=0.5,
                            scale=1.0),
                          reads=["bB3", "cst2"], writes=["bB3"],
                          tag=f"{tag}q5_{oc}_{ig}")
                    pg.op("sc", lambda: SCA.activation(
                            q_d[:, :], q_d[:, :], AF.Relu, bias=0.0,
                            scale=-1.0),
                          reads=["bB3", "negone"], writes=["bB3"],
                          tag=f"{tag}q5b_{oc}_{ig}")
                    vop(lambda: VEC.scalar_tensor_tensor(
                            q_t[:, :], q_g[:, :], 1.0, q_i, OP.mult, OP.add),
                        ["bB4", "bB5"], ["bB2"], f"{tag}q6_{oc}_{ig}")
                    vop(lambda: VEC.scalar_tensor_tensor(
                            q_t[:, :], q_d[:, :], -1.0, q_t[:, :],
                            OP.mult, OP.add),
                        ["bB3", "bB2"], ["bB2"], f"{tag}q7_{oc}_{ig}")
                    vop(lambda: VEC.tensor_scalar(
                            q_b, q_t[:, :], -1.0, 1.0, OP.max, OP.min),
                        ["bB2"], ["bB6"], f"{tag}q8_{oc}_{ig}")
                    for ic in range(NCH):
                        bank = 2 + (ic % 2)
                        pg.op("pe", lambda ic=ic, bank=bank: TEN.transpose(
                                pb[bank][:, 0:P].bitcast(BF16)[:, 0:P],
                                q_b[:, ic * P:(ic + 1) * P], identb[:, :]),
                              reads=["bB6", "identb"], writes=[PB[bank]],
                              tag=f"{tag}T{oc}_{ig}_{ic}")
                        vop(lambda ic=ic, bank=bank: VEC.tensor_copy(
                                q_o[:, ic * P:(ic + 1) * P],
                                pb[bank][:, 0:P].bitcast(BF16)[:, 0:P]),
                            [PB[bank]], ["bB7", PB[bank]],
                            f"{tag}te{oc}_{ig}_{ic}")
                    dma("qs", tT_d[ig * D:(ig + 1) * D,
                                   oc * P:(oc + 1) * P].rearrange(
                            "(a p) b -> p a b", p=P),
                        q_o,
                        ["bB7"], [key], f"{tag}ts{oc}_{ig}")

        prep_tern(W1, t1T_d, "t1T_d", 4 * D, D, 0, "t1")
        prep_tern(W2, t2T_d, "t2T_d", D, 4 * D, 1, "t2")
        vop(lambda: VEC.reciprocal(s_w[:, 0:1], s_w[:, 0:1]),
            ["s_w"], ["s_w"], "isw1")
        vop(lambda: VEC.reciprocal(s_w[:, 1:2], s_w[:, 1:2]),
            ["s_w"], ["s_w"], "isw2")
        vop(lambda: VEC.tensor_scalar_mul(swr[:, :], s_w[:, 1:2],
                                          1.0 / 127.0),
            ["s_w"], ["swr"], "iswr")
        for k3 in range(3):
            vop(lambda k3=k3: VEC.tensor_scalar_mul(
                    convw_c[:, k3:96:3], convw_c[:, k3:96:3], s_w[:, 0:1]),
                ["convw_c", "s_w"], ["convw_c"], f"cwf{k3}")

        def load_wres(dst, keys, src_d, src_key, col_lo, ncols, tag):
            for c in range(NCH):
                dma("qa", dst[c][:, 0:ncols],
                    src_d[c * P:(c + 1) * P, col_lo:col_lo + ncols],
                    [src_key], [keys[c]], f"{tag}{c}")

        def rep_tiles(src_col_fn, idxs, bank, dst, dst_keys, reads_extra,
                      tag):
            n = len(idxs)
            for j, tau in enumerate(idxs):
                pg.op("pe", lambda j=j, tau=tau, bank=bank: TEN.matmul(
                        pb[bank][:, j * P:(j + 1) * P],
                        src_col_fn(tau).broadcast_to((P, P)),
                        ident[:, :], start=(j == 0), stop=(j == n - 1),
                        skip_group_check=True),
                      reads=list(reads_extra) + ["ident"],
                      writes=[PB[bank]], tag=f"{tag}m{j}")
            vop(lambda n=n, bank=bank, dst=dst: VEC.tensor_copy(
                    dst[:, 0:n * P], pb[bank][:, 0:n * P]),
                [PB[bank]], list(dst_keys) + [PB[bank]], f"{tag}e")

        def ones_rep(row_ap, n, bank, dst, dst_keys, tag):
            pg.op("pe", lambda row_ap=row_ap, n=n, bank=bank: TEN.matmul(
                    pb[bank][:, 0:n], onesT[0:1, 0:P], row_ap,
                    start=True, stop=True),
                  reads=["onesT", "nm_row", "rows"], writes=[PB[bank]],
                  tag=f"{tag}m")
            vop(lambda n=n, bank=bank, dst=dst: VEC.tensor_copy(
                    dst[:, 0:n], pb[bank][:, 0:n]),
                [PB[bank]], list(dst_keys) + [PB[bank]], f"{tag}e")

        def make_bbc(dst, dst_key, rowname, tag):
            for h in range(2):
                pg.op("pe", lambda h=h, rowname=rowname: TEN.matmul(
                        pb[6 + h][:, :], ones_at(rowname),
                        row(rowname, h * 512, 512), start=True, stop=True),
                      reads=["onesT", "rows"], writes=[PB[6 + h]],
                      tag=f"{tag}{h}")
                vop(lambda h=h, dst=dst: VEC.tensor_copy(
                        dst[:, h * 512:(h + 1) * 512], pb[6 + h][:, :]),
                    [PB[6 + h]], [dst_key, PB[6 + h]], f"{tag}e{h}")

        def emit_ln(tag):
            sq = [bigB[:, 0:2048], bigB[:, 2048:4096]]
            sqk = [["bB0", "bB1"], ["bB2", "bB3"]]
            for c in range(NCH):
                b_ = c % 2
                pg.op("sc", lambda c=c, b_=b_: SCA.activation(
                        sq[b_][:, 0:T], xc(c), AF.Square),
                      reads=[XK[c]], writes=sqk[b_], tag=f"{tag}sq{c}")
                for t5 in range(NT5):
                    pg.op("pe", lambda c=c, t5=t5: TEN.matmul(
                            pb[t5][:, :], onesT[:, :],
                            xc(c)[:, t5 * 512:(t5 + 1) * 512],
                            start=(c == 0), stop=(c == NCH - 1)),
                          reads=[XK[c], "onesT"], writes=[PB[t5]],
                          tag=f"{tag}s{c}_{t5}")
                    pg.op("pe", lambda c=c, t5=t5, b_=b_: TEN.matmul(
                            pb[4 + t5][:, :], onesT[:, :],
                            sq[b_][:, t5 * 512:(t5 + 1) * 512],
                            start=(c == 0), stop=(c == NCH - 1)),
                          reads=sqk[b_] + ["onesT"], writes=[PB[4 + t5]],
                          tag=f"{tag}ss{c}_{t5}")
            for t5 in range(NT5):
                sl = slice(t5 * 512, (t5 + 1) * 512)
                for rb in (0, 32, 64):
                    vop(lambda t5=t5, sl=sl, rb=rb: VEC.tensor_single_scalar(
                            nm_row[rb:rb + 1, sl], pb[t5][rb:rb + 1, :],
                            -1.0 / D, OP.mult),
                        [PB[t5]], ["nm_row"], f"{tag}nm{t5}_{rb}")
                vop(lambda t5=t5: VEC.tensor_scalar_mul(
                        qp(0)[:, 0:512], pb[t5][:, :], 1.0 / D),
                    [PB[t5]], ["qp0", PB[t5]], f"{tag}m{t5}")
                vop(lambda: VEC.tensor_mul(qp(0)[:, 0:512], qp(0)[:, 0:512],
                                           qp(0)[:, 0:512]),
                    ["qp0"], ["qp0"], f"{tag}m2{t5}")
                vop(lambda t5=t5: VEC.scalar_tensor_tensor(
                        qp(0)[:, 0:512], pb[4 + t5][:, :], 1.0 / D,
                        qp(0)[:, 0:512], OP.mult, OP.subtract),
                    [PB[4 + t5], "qp0"], ["qp0", PB[4 + t5]], f"{tag}v{t5}")
                for j in range(4):
                    for jj in range(4):
                        vop(lambda j=j, jj=jj: VEC.transpose(
                                scr128[jj * 32:(jj + 1) * 32, 0:32],
                                qp(0)[0:32, j * P + jj * 32:
                                      j * P + (jj + 1) * 32]),
                            ["qp0"], ["scr128"], f"{tag}vt{t5}_{j}_{jj}")
                    vop(lambda t5=t5, j=j: VEC.tensor_copy(
                            var_tm[:, t5 * 4 + j:t5 * 4 + j + 1],
                            scr128[:, 0:1]),
                        ["scr128"], ["var_tm"], f"{tag}vc{t5}_{j}")
            vop(lambda: VEC.tensor_scalar_add(var_tm[:, :], var_tm[:, :],
                                              1e-5),
                ["var_tm"], ["var_tm"], f"{tag}ve")
            vop(lambda: VEC.reciprocal(var_tm[:, :], var_tm[:, :]),
                ["var_tm"], ["var_tm"], f"{tag}vr")
            pg.op("sc", lambda: SCA.activation(rstd_tm[:, :], var_tm[:, :],
                                               AF.Sqrt),
                  reads=["var_tm"], writes=["rstd_tm"], tag=f"{tag}vs")

        def emit_passA(s, is_l1):
            ktile = bC(0)[:, 0:1024]
            vt = bC(1)[:, 0:1024]
            vkeys = ["bC1"]
            wk_key, wv_key = ("k1", "v1") if is_l1 else ("wk", "wv")
            vop(lambda: VEC.memset(kv_acc[:, :], 0.0), [], ["kv_acc"],
                f"A{s}z")
            for t1 in range(NT1):
                tsl = slice(t1 * P, (t1 + 1) * P)
                for half in range(2):
                    hsl = slice(half * 512, (half + 1) * 512)
                    for c in range(NCH):
                        pg.op("pe", lambda c=c, tsl=tsl, hsl=hsl, half=half:
                              TEN.matmul(pb[half][:, :], xc(c)[:, tsl],
                                         wres[c][:, hsl],
                                         start=(c == 0), stop=False),
                              reads=[XK[c], WK1[c]], writes=[PB[half]],
                              tag=f"A{s}k{t1}_{half}_{c}")
                    pg.op("pe", lambda tsl=tsl, half=half, wk_key=wk_key:
                          TEN.matmul(
                            pb[half][:, :],
                            nm_row[rowbase(wk_key):rowbase(wk_key) + 1, tsl],
                            row(wk_key, half * 512, 512),
                            start=False, stop=True),
                          reads=["nm_row", "rows"], writes=[PB[half]],
                          tag=f"A{s}km{t1}_{half}")
                    for c in range(NCH):
                        pg.op("pe", lambda c=c, tsl=tsl, hsl=hsl, half=half:
                              TEN.matmul(pb[2 + half][:, :], xc(c)[:, tsl],
                                         wres2[c][:, hsl],
                                         start=(c == 0), stop=False),
                              reads=[XK[c], WK2[c]], writes=[PB[2 + half]],
                              tag=f"A{s}v{t1}_{half}_{c}")
                    pg.op("pe", lambda tsl=tsl, half=half, wv_key=wv_key:
                          TEN.matmul(
                            pb[2 + half][:, :],
                            nm_row[rowbase(wv_key):rowbase(wv_key) + 1, tsl],
                            row(wv_key, half * 512, 512),
                            start=False, stop=True),
                          reads=["nm_row", "rows"], writes=[PB[2 + half]],
                          tag=f"A{s}vm{t1}_{half}")
                for half in range(2):
                    hsl = slice(half * 512, (half + 1) * 512)
                    if is_l1:
                        vop(lambda t1=t1, half=half: VEC.scalar_tensor_tensor(
                                qp(1)[:, 0:512], pb[half][:, :],
                                rstd_tm[:, t1:t1 + 1],
                                bkbc[:, half * 512:(half + 1) * 512],
                                OP.mult, OP.add),
                            [PB[half], "rstd_tm", "bC3"],
                            ["qp1", PB[half]], f"A{s}ke{t1}_{half}")
                        pg.op("sc", lambda half=half, hsl=hsl, ktile=ktile:
                              SCA.activation(ktile[:, hsl], qp(1)[:, 0:512],
                                             AF.Exp),
                              reads=["qp1"], writes=["bC0"],
                              tag=f"A{s}kx{t1}_{half}")
                    else:
                        vop(lambda t1=t1, half=half, hsl=hsl, ktile=ktile:
                            VEC.scalar_tensor_tensor(
                                ktile[:, hsl], pb[half][:, :],
                                rstd_tm[:, t1:t1 + 1],
                                bkbc[:, half * 512:(half + 1) * 512],
                                OP.mult, OP.add),
                            [PB[half], "rstd_tm", "bC3"],
                            ["bC0", PB[half]], f"A{s}ke{t1}_{half}")
                        vop(lambda hsl=hsl, ktile=ktile:
                            VEC.tensor_scalar_min(
                                qp(1)[:, 0:512], ktile[:, hsl], 0.0),
                            ["bC0"], ["qp1"], f"A{s}kmn{t1}_{half}")
                        pg.op("sc", lambda: SCA.activation(
                                qp(1)[:, 0:512], qp(1)[:, 0:512], AF.Exp),
                              reads=["qp1"], writes=["qp1"],
                              tag=f"A{s}kx{t1}_{half}")
                        vop(lambda hsl=hsl, ktile=ktile:
                            VEC.scalar_tensor_tensor(
                                ktile[:, hsl], ktile[:, hsl], 0.0,
                                qp(1)[:, 0:512], OP.max, OP.add),
                            ["bC0", "qp1"], ["bC0"], f"A{s}kp{t1}_{half}")
                    vop(lambda t1=t1, half=half, hsl=hsl, vt=vt:
                        VEC.scalar_tensor_tensor(
                            vt[:, hsl], pb[2 + half][:, :],
                            rstd_tm[:, t1:t1 + 1],
                            bvbc[:, half * 512:(half + 1) * 512],
                            OP.mult, OP.add),
                        [PB[2 + half], "rstd_tm", "bC4"],
                        vkeys + [PB[2 + half]], f"A{s}ve{t1}_{half}")
                for h in range(16):
                    p_ = h // 2
                    bank = 4 + p_ // 4
                    out_ap = pb[bank][64 * (h % 2):64 * (h % 2) + 64,
                                      (p_ % 4) * 64:(p_ % 4) * 64 + 64]
                    rhs = vt[:, h * 64:(h + 1) * 64]
                    pg.op("pe", lambda h=h, out_ap=out_ap, rhs=rhs,
                          ktile=ktile: TEN.matmul(
                            out_ap, ktile[:, h * 64:(h + 1) * 64],
                            rhs, start=(h % 8 <= 1), stop=(h % 8 >= 6),
                            skip_group_check=True),
                          reads=["bC0"] + vkeys, writes=[PB[4 + p_ // 4]],
                          tag=f"A{s}kv{t1}_{h}")
                for p_ in range(8):
                    pg.op("pe", lambda p_=p_, t1=t1, ktile=ktile:
                          TEN.matmul(
                            pb[6][:, p_:p_ + 1],
                            ktile[:, p_ * P:(p_ + 1) * P],
                            onesT[:, 0:1],
                            start=(t1 == 0 and p_ == 0),
                            stop=(t1 == NT1 - 1 and p_ == 7),
                            skip_group_check=True),
                          reads=["bC0", "onesT"], writes=[PB[6]],
                          tag=f"A{s}dn{t1}_{p_}")
                for b_ in range(2):
                    vop(lambda b_=b_: VEC.tensor_add(
                            kv_acc[:, b_ * 256:(b_ + 1) * 256],
                            kv_acc[:, b_ * 256:(b_ + 1) * 256],
                            pb[4 + b_][:, 0:256]),
                        [PB[4 + b_], "kv_acc"], ["kv_acc", PB[4 + b_]],
                        f"A{s}a{t1}_{b_}")
            vop(lambda: VEC.tensor_copy(den_sb[:, :], pb[6][:, 0:8]),
                [PB[6]], ["den_sb", PB[6]], f"A{s}de")
            if is_l1:
                vop(lambda: VEC.reciprocal(den_sb[:, :], den_sb[:, :]),
                    ["den_sb"], ["den_sb"], f"A{s}dr")
                for p_ in range(8):
                    vop(lambda p_=p_: VEC.tensor_scalar_mul(
                            kv_acc[:, p_ * 64:(p_ + 1) * 64],
                            kv_acc[:, p_ * 64:(p_ + 1) * 64],
                            den_sb[:, p_:p_ + 1]),
                        ["kv_acc", "den_sb"], ["kv_acc"], f"A{s}sc{p_}")

        def emit_load_x(s):
            for t1 in range(NT1):
                nb = t1 % 2
                land = qp(2 + nb)
                dma("qs", land[:, 0:D], x_in[s, t1 * P:(t1 + 1) * P, :],
                    [], [KQ[2 + nb]], f"lx{s}_{t1}")
                for c in range(NCH):
                    bank = c % 4
                    pg.op("pe", lambda land=land, c=c, bank=bank:
                          TEN.transpose(pb[bank][:, 0:P],
                                        land[:, c * P:(c + 1) * P],
                                        ident[:, :]),
                          reads=[KQ[2 + nb], "ident"], writes=[PB[bank]],
                          tag=f"lxT{s}_{t1}_{c}")
                    eng = "sc" if c % 2 == 0 else "dve"
                    fn = SCA.copy if eng == "sc" else VEC.tensor_copy
                    vop(lambda fn=fn, c=c, t1=t1, bank=bank: fn(
                            xc(c)[:, t1 * P:(t1 + 1) * P], pb[bank][:, 0:P]),
                        [PB[bank]], [XK[c], PB[bank]], f"lxE{s}_{t1}_{c}",
                        eng=eng)

        def emit_store_x(s):
            for t1 in range(NT1):
                nb = t1 % 2
                land = qp(2 + nb)
                for c in range(NCH):
                    bank = 4 + c % 4
                    pg.op("pe", lambda c=c, t1=t1, bank=bank: TEN.transpose(
                            pb[bank][:, 0:P], xc(c)[:, t1 * P:(t1 + 1) * P],
                            ident[:, :]),
                          reads=[XK[c], "ident"], writes=[PB[bank]],
                          tag=f"sx{s}_{t1}_{c}")
                    eng = "sc" if c % 2 == 0 else "dve"
                    fn = SCA.copy if eng == "sc" else VEC.tensor_copy
                    vop(lambda fn=fn, land=land, c=c, bank=bank: fn(
                            land[:, c * P:(c + 1) * P], pb[bank][:, 0:P]),
                        [PB[bank]], [KQ[2 + nb], PB[bank]],
                        f"sxE{s}_{t1}_{c}", eng=eng)
                dma("qs", y_out[s, t1 * P:(t1 + 1) * P, :], land[:, 0:D],
                    [KQ[2 + nb]], [f"y{s}"], f"sy{s}_{t1}")

        def rtile(t5, dst, dst_keys, tag):
            rep_tiles(lambda tau: rstd_tm[:, tau:tau + 1],
                      [4 * t5 + j for j in range(4)], 7, dst, dst_keys,
                      ["rstd_tm"], tag)

        def phv(c):
            return bC(4 + c // 2)[:, (c % 2) * 512:(c % 2) * 512 + 512]

        def phk(c):
            return KC[4 + c // 2]

        def emit_l0_attn(s):
            emit_ln(f"n0_{s}")
            make_bbc(bkbc, "bC3", "bk", f"bb0{s}")
            make_bbc(bvbc, "bC4", "bv", f"bv0{s}")
            load_wres(wres, WK1, wkT_d, "wkT_d", 0, D, f"wk{s}_")
            load_wres(wres2, WK2, wvT_d, "wvT_d", 0, D, f"wv{s}_")
            emit_passA(s, is_l1=False)
            vop(lambda: VEC.memset(norm_lhs[:, :], 0.0), [], ["norm_lhs"],
                f"B{s}nz")
            for c in range(NCH):
                vop(lambda c=c: VEC.tensor_copy(
                        norm_lhs[0:64, c * 16 + 2 * c:c * 16 + 2 * c + 1],
                        den_sb[0:64, c:c + 1]),
                    ["den_sb"], ["norm_lhs"], f"B{s}n0{c}")
                vop(lambda c=c: VEC.tensor_copy(
                        norm_lhs[64:128,
                                 c * 16 + 2 * c + 1:c * 16 + 2 * c + 2],
                        den_sb[64:128, c:c + 1]),
                    ["den_sb"], ["norm_lhs"], f"B{s}n1{c}")
            load_wres(wres, WK1, wqT_d, "wqT_d", 0, D, f"wq{s}_")
            load_wres(wres2, WK2, woT_d, "woT_d", 0, D, f"wo{s}_")
            for t5 in range(NT5):
                sl = slice(t5 * 512, (t5 + 1) * 512)
                rtile(t5, qp(0)[:, 0:512], ["qp0"], f"B{s}r{t5}")
                for oc in range(NCH):
                    bank = oc
                    for c in range(NCH):
                        pg.op("pe", lambda oc=oc, c=c, sl=sl, bank=bank:
                              TEN.matmul(pb[bank][:, :],
                                         wres[c][:, oc * P:(oc + 1) * P],
                                         xc(c)[:, sl],
                                         start=(c == 0), stop=False),
                              reads=[WK1[c], XK[c]], writes=[PB[bank]],
                              tag=f"B{s}q{t5}_{oc}_{c}")
                    pg.op("pe", lambda oc=oc, sl=sl, bank=bank: TEN.matmul(
                            pb[bank][:, :], row("wq", oc * P, P),
                            nm_row[rowbase("wq"):rowbase("wq") + 1,
                                   sl],
                            start=False, stop=True),
                          reads=["rows", "nm_row"], writes=[PB[bank]],
                          tag=f"B{s}qm{t5}_{oc}")
                for oc in range(NCH):
                    bank = oc
                    vop(lambda bank=bank: VEC.tensor_mul(
                            qp(1)[:, 0:512], pb[bank][:, :],
                            qp(0)[:, 0:512]),
                        [PB[bank], "qp0"], ["qp1", PB[bank]],
                        f"B{s}e0{t5}_{oc}")
                    vop(lambda oc=oc: VEC.tensor_scalar(
                            qp(2)[:, 0:512], qp(1)[:, 0:512],
                            bq_c[:, oc:oc + 1], 0.0, OP.add, OP.min),
                        ["qp1", "bq_c"], ["qp2"], f"B{s}e1{t5}_{oc}")
                    pg.op("sc", lambda: SCA.activation(
                            qp(2)[:, 0:512], qp(2)[:, 0:512], AF.Exp),
                          reads=["qp2"], writes=["qp2"],
                          tag=f"B{s}e2{t5}_{oc}")
                    pg.op("sc", lambda oc=oc: SCA.activation(
                            qp(1)[:, 0:512], qp(1)[:, 0:512], AF.Identity,
                            bias=bq_c[:, oc:oc + 1], scale=1.0),
                          reads=["qp1", "bq_c"], writes=["qp1"],
                          tag=f"B{s}e3{t5}_{oc}")
                    vop(lambda oc=oc: VEC.scalar_tensor_tensor(
                            phv(oc), qp(1)[:, 0:512], 0.0, qp(2)[:, 0:512],
                            OP.max, OP.add),
                        ["qp1", "qp2"], [phk(oc)], f"B{s}e4{t5}_{oc}")
                for c in range(NCH):
                    pg.op("pe", lambda c=c: TEN.matmul(
                            pb[0][0:16, :], norm_lhs[:, c * 16:(c + 1) * 16],
                            phv(c), start=(c == 0), stop=(c == NCH - 1)),
                          reads=["norm_lhs", phk(c)],
                          writes=["pb0"], tag=f"B{s}nm{t5}_{c}")
                vop(lambda: VEC.tensor_single_scalar(
                        norm16t, pb[0][0:16, :], 1e-6, OP.max),
                    ["pb0"], ["norm16t", "pb0"], f"B{s}nx{t5}")
                vop(lambda: VEC.reciprocal(norm16t, norm16t),
                    ["norm16t"], ["norm16t"], f"B{s}nr{t5}")
                for c in range(NCH):
                    vop(lambda: VEC.memset(kvblk[:, :], 0.0), [], ["kvblk"],
                        f"B{s}bz{t5}_{c}")
                    vop(lambda c=c: VEC.tensor_copy(
                            kvblk[0:64, 0:64],
                            kv_acc[0:64, c * 64:c * 64 + 64]),
                        ["kv_acc"], ["kvblk"], f"B{s}b0{t5}_{c}")
                    vop(lambda c=c: VEC.tensor_copy(
                            kvblk[64:128, 64:128],
                            kv_acc[64:128, c * 64:c * 64 + 64]),
                        ["kv_acc"], ["kvblk"], f"B{s}b1{t5}_{c}")
                    pg.op("pe", lambda c=c: TEN.matmul(
                            pb[1][:, :], kvblk[:, :], phv(c),
                            start=True, stop=True),
                          reads=["kvblk", phk(c)], writes=["pb1"],
                          tag=f"B{s}o{t5}_{c}")
                    pg.op("pe", lambda c=c: TEN.matmul(
                            pb[2][:, :], sel16[:, c * P:(c + 1) * P],
                            norm16t, start=True, stop=True),
                          reads=["sel16", "norm16t"], writes=["pb2"],
                          tag=f"B{s}p{t5}_{c}")
                    pg.op("sc", lambda: SCA.copy(qp(1)[:, 0:512],
                                                 pb[2][:, :]),
                          reads=["pb2"], writes=["qp1", "pb2"],
                          tag=f"B{s}pe{t5}_{c}")
                    vop(lambda c=c: VEC.tensor_mul(
                            phv(c), pb[1][:, :], qp(1)[:, 0:512]),
                        ["pb1", "qp1"], [phk(c), "pb1"],
                        f"B{s}d{t5}_{c}")
                for oc in range(NCH):
                    bank = 3 + oc % 5
                    for c in range(NCH):
                        pg.op("pe", lambda oc=oc, c=c, bank=bank:
                              TEN.matmul(pb[bank][:, :],
                                         wres2[c][:, oc * P:(oc + 1) * P],
                                         phv(c), start=(c == 0),
                                         stop=(c == NCH - 1)),
                              reads=[WK2[c], phk(c)],
                              writes=[PB[bank]], tag=f"B{s}w{t5}_{oc}_{c}")
                    vop(lambda oc=oc, sl=sl, bank=bank:
                        VEC.scalar_tensor_tensor(
                            xc(oc)[:, sl], pb[bank][:, :],
                            bo_c[:, oc:oc + 1], xc(oc)[:, sl],
                            OP.add, OP.add),
                        [PB[bank], "bo_c", XK[oc]],
                        [XK[oc], PB[bank]], f"B{s}x{t5}_{oc}")

        def emit_l0_ffn(s):
            emit_ln(f"n1_{s}")
            load_wres(wres, WK1, weffT_d, "weffT_d", 0, D, f"wf{s}_")
            for t5 in range(NT5):
                sl = slice(t5 * 512, (t5 + 1) * 512)
                rtile(t5, qp(0)[:, 0:512], ["qp0"], f"F{s}r{t5}")
                for oc in range(NCH):
                    bank = oc % 6
                    for c in range(NCH):
                        pg.op("pe", lambda oc=oc, c=c, sl=sl, bank=bank:
                              TEN.matmul(pb[bank][:, :],
                                         wres[c][:, oc * P:(oc + 1) * P],
                                         xc(c)[:, sl],
                                         start=(c == 0), stop=False),
                              reads=[WK1[c], XK[c]], writes=[PB[bank]],
                              tag=f"F{s}m{t5}_{oc}_{c}")
                    pg.op("pe", lambda oc=oc, sl=sl, bank=bank: TEN.matmul(
                            pb[bank][:, :], row("weff", oc * P, P),
                            nm_row[rowbase("weff"):rowbase("weff") + 1,
                                   sl],
                            start=False, stop=True),
                          reads=["rows", "nm_row"], writes=[PB[bank]],
                          tag=f"F{s}t{t5}_{oc}")
                    vop(lambda oc=oc, bank=bank: VEC.tensor_mul(
                            phv(oc), pb[bank][:, :], qp(0)[:, 0:512]),
                        [PB[bank], "qp0"], [phk(oc), PB[bank]],
                        f"F{s}e{t5}_{oc}")
                for oc in range(NCH):
                    vop(lambda oc=oc, sl=sl: VEC.scalar_tensor_tensor(
                            xc(oc)[:, sl], phv(oc),
                            bf_c[:, oc:oc + 1], xc(oc)[:, sl],
                            OP.add, OP.add),
                        [phk(oc), "bf_c", XK[oc]], [XK[oc]],
                        f"F{s}x{t5}_{oc}")

        def emit_l1_attn(s):
            emit_ln(f"n2_{s}")
            make_bbc(bkbc, "bC3", "bk1", f"bb1{s}")
            make_bbc(bvbc, "bC4", "bv1", f"bv1{s}")
            load_wres(wres, WK1, qkvT_d, "qkvT_d", D, D, f"jk{s}_")
            load_wres(wres2, WK2, qkvT_d, "qkvT_d", 2 * D, D, f"jv{s}_")
            emit_passA(s, is_l1=True)
            l1tap = "l1" in debug_taps and s == 0
            if l1tap:
                dbg_kv = nc.dram_tensor("dbg_kv", [P, 1024], F32,
                                        kind="ExternalOutput").ap()
                dbg_den = nc.dram_tensor("dbg_den", [P, 8], F32,
                                         kind="ExternalOutput").ap()
                dbg_q = nc.dram_tensor("dbg_q", [P, 4096], F32,
                                       kind="ExternalOutput").ap()
                dbg_o = nc.dram_tensor("dbg_o", [P, 4096], F32,
                                       kind="ExternalOutput").ap()
                dbg_n = nc.dram_tensor("dbg_n", [P, 4096], F32,
                                       kind="ExternalOutput").ap()
                dbg_gh = nc.dram_tensor("dbg_gh", [64, 512], F32,
                                        kind="ExternalOutput").ap()
                dbg_gate = nc.dram_tensor("dbg_gate", [P, 512], F32,
                                          kind="ExternalOutput").ap()
                dma("qs", dbg_kv, kv_acc[:, :], ["kv_acc"], ["dKV"], "dKV")
                dma("qs", dbg_den, den_sb[:, :], ["den_sb"], ["dDE"], "dDE")
            load_wres(wres, WK1, qkvT_d, "qkvT_d", 0, D, f"jq{s}_")
            load_wres(wres2, WK2, woutT_d, "woutT_d", 0, D, f"ju{s}_")
            for c in range(NCH):
                dma("qa", bC(0)[:, c * 64:(c + 1) * 64],
                    g1T_d[c * P:(c + 1) * P, :], ["g1T_d"], ["bC0"],
                    f"G1{s}_{c}")
            dma("qa", bC(1)[64:128, 0:D], g2T_d, ["g2T_d"], ["bC1"],
                f"G2{s}")
            for t5 in range(NT5):
                sl = slice(t5 * 512, (t5 + 1) * 512)
                rtile(t5, qp(0)[:, 0:512], ["qp0"], f"C{s}r{t5}")
                for oc in range(NCH):
                    bank = oc
                    for c in range(NCH):
                        pg.op("pe", lambda oc=oc, c=c, sl=sl, bank=bank:
                              TEN.matmul(pb[bank][:, :],
                                         wres[c][:, oc * P:(oc + 1) * P],
                                         xc(c)[:, sl],
                                         start=(c == 0), stop=False),
                              reads=[WK1[c], XK[c]], writes=[PB[bank]],
                              tag=f"C{s}q{t5}_{oc}_{c}")
                    pg.op("pe", lambda oc=oc, sl=sl, bank=bank: TEN.matmul(
                            pb[bank][:, :], row("q1", oc * P, P),
                            nm_row[rowbase("q1"):rowbase("q1") + 1,
                                   sl],
                            start=False, stop=True),
                          reads=["rows", "nm_row"], writes=[PB[bank]],
                          tag=f"C{s}qm{t5}_{oc}")
                for oc in range(NCH):
                    bank = oc
                    vop(lambda bank=bank: VEC.tensor_mul(
                            qp(1)[:, 0:512], pb[bank][:, :],
                            qp(0)[:, 0:512]),
                        [PB[bank], "qp0"], ["qp1", PB[bank]],
                        f"C{s}e0{t5}_{oc}")
                    pg.op("sc", lambda oc=oc: SCA.activation(
                            phv(oc), qp(1)[:, 0:512], AF.Silu,
                            bias=bq1_c[:, oc:oc + 1], scale=1.0),
                          reads=["qp1", "bq1_c"], writes=[phk(oc)],
                          tag=f"C{s}e1{t5}_{oc}")
                if l1tap and t5 == 0:
                    dma("qs", dbg_q, bigC[:, 4096:8192],
                        [phk(c_) for c_ in range(NCH)], ["dQ"], "dQ")
                for c in range(NCH):
                    vop(lambda: VEC.memset(kvblk[:, :], 0.0), [], ["kvblk"],
                        f"C{s}bz{t5}_{c}")
                    vop(lambda c=c: VEC.tensor_copy(
                            kvblk[0:64, 0:64],
                            kv_acc[0:64, c * 64:c * 64 + 64]),
                        ["kv_acc"], ["kvblk"], f"C{s}b0{t5}_{c}")
                    vop(lambda c=c: VEC.tensor_copy(
                            kvblk[64:128, 64:128],
                            kv_acc[64:128, c * 64:c * 64 + 64]),
                        ["kv_acc"], ["kvblk"], f"C{s}b1{t5}_{c}")
                    pg.op("pe", lambda c=c: TEN.matmul(
                            pb[0][:, :], kvblk[:, :], phv(c),
                            start=True, stop=True),
                          reads=["kvblk", phk(c)], writes=["pb0"],
                          tag=f"C{s}o{t5}_{c}")
                    vop(lambda c=c: VEC.tensor_copy(phv(c), pb[0][:, :]),
                        ["pb0"], [phk(c), "pb0"], f"C{s}oe{t5}_{c}")
                if l1tap and t5 == 0:
                    dma("qs", dbg_o, bigC[:, 4096:8192],
                        [phk(c_) for c_ in range(NCH)], ["dO"], "dO")
                for c in range(NCH):
                    pg.op("sc", lambda c=c: SCA.activation(
                            qp(1)[:, 0:512], phv(c), AF.Square),
                          reads=[phk(c)], writes=["qp1"],
                          tag=f"C{s}s{t5}_{c}")
                    pg.op("pe", lambda c=c: TEN.matmul(
                            pb[2][:, :], onesT[:, :], phv(c),
                            start=(c == 0), stop=(c == NCH - 1)),
                          reads=[phk(c), "onesT"], writes=["pb2"],
                          tag=f"C{s}sm{t5}_{c}")
                    pg.op("pe", lambda c=c: TEN.matmul(
                            pb[3][:, :], onesT[:, :], qp(1)[:, 0:512],
                            start=(c == 0), stop=(c == NCH - 1)),
                          reads=["qp1", "onesT"], writes=["pb3"],
                          tag=f"C{s}sq{t5}_{c}")
                vop(lambda: VEC.tensor_scalar_mul(qp(2)[:, 0:512],
                                                  pb[2][:, :], 1.0 / D),
                    ["pb2"], ["qp2", "pb2"], f"C{s}lm{t5}")
                vop(lambda: VEC.tensor_mul(qp(1)[:, 0:512], qp(2)[:, 0:512],
                                           qp(2)[:, 0:512]),
                    ["qp2"], ["qp1"], f"C{s}l2{t5}")
                vop(lambda: VEC.scalar_tensor_tensor(
                        qp(1)[:, 0:512], pb[3][:, :], 1.0 / D,
                        qp(1)[:, 0:512], OP.mult, OP.subtract),
                    ["pb3", "qp1"], ["qp1", "pb3"], f"C{s}lv{t5}")
                vop(lambda: VEC.tensor_scalar_add(qp(1)[:, 0:512],
                                                  qp(1)[:, 0:512], 1e-5),
                    ["qp1"], ["qp1"], f"C{s}le{t5}")
                vop(lambda: VEC.reciprocal(qp(1)[:, 0:512], qp(1)[:, 0:512]),
                    ["qp1"], ["qp1"], f"C{s}lr{t5}")
                pg.op("sc", lambda: SCA.activation(
                        qp(1)[:, 0:512], qp(1)[:, 0:512], AF.Sqrt),
                      reads=["qp1"], writes=["qp1"], tag=f"C{s}ls{t5}")
                vop(lambda: VEC.scalar_tensor_tensor(
                        qp(2)[:, 0:512], qp(2)[:, 0:512], -1.0,
                        qp(1)[:, 0:512], OP.mult, OP.mult),
                    ["qp2", "qp1"], ["qp2"], f"C{s}ln{t5}")
                for c in range(NCH):
                    vop(lambda c=c: GPS.tensor_mul(
                            qp(3)[:, 0:512], phv(c), qp(1)[:, 0:512]),
                        [phk(c), "qp1"], ["qp3"], f"C{s}a0{t5}_{c}",
                        eng="gp")
                    vop(lambda: GPS.tensor_add(
                            qp(3)[:, 0:512], qp(3)[:, 0:512],
                            qp(2)[:, 0:512]),
                        ["qp3", "qp2"], ["qp3"], f"C{s}a1{t5}_{c}",
                        eng="gp")
                    vop(lambda c=c: VEC.tensor_scalar(
                            phv(c), qp(3)[:, 0:512], lng_c[:, c:c + 1],
                            lnb_c[:, c:c + 1], OP.mult, OP.add),
                        ["qp3", "lng_c", "lnb_c"], [phk(c)],
                        f"C{s}a2{t5}_{c}")
                if l1tap and t5 == 0:
                    dma("qs", dbg_n, bigC[:, 4096:8192],
                        [phk(c_) for c_ in range(NCH)], ["dN"], "dN")
                for c in range(NCH):
                    pg.op("pe", lambda c=c, sl=sl: TEN.matmul(
                            pb[4][0:64, :], bC(0)[:, c * 64:(c + 1) * 64],
                            xc(c)[:, sl], start=(c == 0), stop=False),
                          reads=["bC0", XK[c]], writes=["pb4"],
                          tag=f"C{s}g{t5}_{c}")
                pg.op("pe", lambda sl=sl: TEN.matmul(
                        pb[4][0:64, :], row("g1", 0, 64),
                        nm_row[rowbase("g1"):rowbase("g1") + 1, sl],
                        start=False, stop=True),
                      reads=["rows", "nm_row"], writes=["pb4"],
                      tag=f"C{s}gm{t5}")
                vop(lambda: VEC.tensor_mul(ghv, pb[4][0:64, :],
                                           qp(0)[0:64, 0:512]),
                    ["pb4", "qp0"], ["gh", "pb4"], f"C{s}ge{t5}")
                vop(lambda: VEC.tensor_scalar_add(ghv, ghv, bg1_c[:, 0:1]),
                    ["gh", "bg1_c"], ["gh"], f"C{s}gb{t5}")
                if l1tap and t5 == 0:
                    dma("qs", dbg_gh, ghv, ["gh"], ["dGH"], "dGH")
                for oc in range(NCH):
                    pg.op("pe", lambda oc=oc: TEN.matmul(
                            pb[5][:, :], bC(1)[64:128, oc * P:(oc + 1) * P],
                            ghv, start=True, stop=True),
                          reads=["bC1", "gh"], writes=["pb5"],
                          tag=f"C{s}g2{t5}_{oc}")
                    pg.op("sc", lambda oc=oc: SCA.activation(
                            qp(3)[:, 0:512], pb[5][:, :], AF.Sigmoid,
                            bias=bg2_c[:, oc:oc + 1], scale=1.0),
                          reads=["pb5", "bg2_c"], writes=["qp3", "pb5"],
                          tag=f"C{s}g3{t5}_{oc}")
                    if l1tap and t5 == 0 and oc == 0:
                        dma("qs", dbg_gate, qp(3)[:, 0:512], ["qp3"],
                            ["dGA"], "dGA")
                    vop(lambda oc=oc: VEC.tensor_mul(
                            phv(oc), phv(oc), qp(3)[:, 0:512]),
                        [phk(oc), "qp3"], [phk(oc)], f"C{s}g4{t5}_{oc}")
                for oc in range(NCH):
                    bank = oc % 6
                    for c in range(NCH):
                        pg.op("pe", lambda oc=oc, c=c, bank=bank:
                              TEN.matmul(pb[bank][:, :],
                                         wres2[c][:, oc * P:(oc + 1) * P],
                                         phv(c), start=(c == 0),
                                         stop=(c == NCH - 1)),
                              reads=[WK2[c], phk(c)],
                              writes=[PB[bank]], tag=f"C{s}w{t5}_{oc}_{c}")
                    vop(lambda oc=oc, sl=sl, bank=bank:
                        VEC.scalar_tensor_tensor(
                            xc(oc)[:, sl], pb[bank][:, :],
                            bout_c[:, oc:oc + 1], xc(oc)[:, sl],
                            OP.add, OP.add),
                        [PB[bank], "bout_c", XK[oc]],
                        [XK[oc], PB[bank]], f"C{s}x{t5}_{oc}")

        def emit_l1_ffn(s):
            emit_ln(f"n3_{s}")
            qint = bigC[:, 0:4096].bitcast(BF16)[:, 0:NCH * 768]
            qint_c = [qint[:, c * 768:(c + 1) * 768] for c in range(NCH)]
            q2v = bigC[:, 0:8192].bitcast(BF16)[:, 0:32 * 512]
            for q in range(NQ):
                tl_lo = 4 * q
                tl_hi = min(NT1, 4 * q + 5)
                tls = list(range(tl_lo, tl_hi))
                ntl = len(tls)
                base = tl_lo * P
                if q == 0:
                    vop(lambda: VEC.memset(
                            qint.rearrange("p (c t) -> p c t",
                                           c=NCH)[:, :, 0:1], 0.0),
                        [], ["bC0", "bC1", "bC2"], f"D{s}qz{q}")
                    vop(lambda: VEC.memset(qp(3)[:, 0:1], 0.0),
                        [], ["qp3"], f"D{s}iz{q}")
                else:
                    vop(lambda: VEC.tensor_copy(
                            qint.rearrange("p (c t) -> p c t",
                                           c=NCH)[:, :, 0:1],
                            qcarry[:, 0:8]),
                        ["qcarry"], ["bC0", "bC1", "bC2"], f"D{s}qc{q}")
                    vop(lambda: VEC.tensor_copy(qp(3)[:, 0:1],
                                                icarry[:, :]),
                        ["icarry"], ["qp3"], f"D{s}ic{q}")
                for g4 in range(0, ntl, 4):
                    sub = tls[g4:g4 + 4]
                    rep_tiles(lambda tau: rstd_tm[:, tau:tau + 1], sub, 7,
                              qp(0)[:, g4 * P:g4 * P + len(sub) * P],
                              ["qp0"], ["rstd_tm"], f"D{s}r{q}_{g4}")
                for g4 in range(0, ntl, 4):
                    nsub = min(ntl, g4 + 4) - g4
                    ones_rep(nm_row[0:1, base + g4 * P:
                                    base + (g4 + nsub) * P], nsub * P, 7,
                             qp(1)[:, g4 * P:(g4 + nsub) * P], ["qp1"],
                             f"D{s}n{q}_{g4}")
                vop(lambda ntl=ntl: VEC.tensor_mul(
                        qp(1)[:, 0:ntl * P], qp(1)[:, 0:ntl * P],
                        qp(0)[:, 0:ntl * P]),
                    ["qp1", "qp0"], ["qp1"], f"D{s}nr{q}")
                for j in range(ntl):
                    tau = tls[j]
                    jk = f"bB{j // 2}"
                    rj = qp(0)[:, j * P:(j + 1) * P]
                    nj = qp(1)[:, j * P:(j + 1) * P]
                    tdst = bigB[:, j * 1024:(j + 1) * 1024].rearrange(
                        "p (c t) -> p c t", c=NCH)
                    vop(lambda tau=tau, rj=rj, tdst=tdst: VEC.tensor_tensor(
                            tdst, xstr(tau),
                            rj.unsqueeze(1).broadcast_to((P, NCH, P)),
                            OP.mult),
                        XK + ["qp0"], [jk], f"D{s}x0{q}_{j}")
                    vop(lambda nj=nj, tdst=tdst: VEC.tensor_tensor(
                            tdst, tdst,
                            nj.unsqueeze(1).broadcast_to((P, NCH, P)),
                            OP.add),
                        [jk, "qp1"], [jk], f"D{s}x1{q}_{j}")
                    tview = bigB[:, j * 1024:(j + 1) * 1024].rearrange(
                        "p (c t) -> p t c", t=P)
                    vop(lambda tview=tview, jk=jk: VEC.reduce_max(
                            scr128[:, 0:P], tview, AX,
                            apply_absolute_value=True),
                        [jk], ["scr128"], f"D{s}am{q}_{j}")
                    for jj in range(4):
                        for kk in range(4):
                            vop(lambda jj=jj, kk=kk: VEC.transpose(
                                    qp(4)[kk * 32:(kk + 1) * 32,
                                          jj * 32:(jj + 1) * 32],
                                    scr128[jj * 32:(jj + 1) * 32,
                                           kk * 32:(kk + 1) * 32]),
                                ["scr128"], ["qp4"], f"D{s}at{q}_{j}_{jj}{kk}")
                    vop(lambda: VEC.reduce_max(scr128[:, 1:2],
                                               qp(4)[:, 0:P], AX),
                        ["qp4"], ["scr128"], f"D{s}ar{q}_{j}")
                    vop(lambda j=j: VEC.tensor_copy(
                            amax_q[:, j:j + 1], scr128[:, 1:2]),
                        ["scr128"], ["amax_q"], f"D{s}ac{q}_{j}")
                vop(lambda ntl=ntl: VEC.tensor_single_scalar(
                        amax_q[:, 0:ntl], amax_q[:, 0:ntl], 1e-5, OP.max),
                    ["amax_q"], ["amax_q"], f"D{s}sc{q}")
                vop(lambda ntl=ntl: VEC.tensor_scalar_mul(
                        den_sb[:, 0:ntl], amax_q[:, 0:ntl], 1.0 / 127.0),
                    ["amax_q"], ["den_sb"], f"D{s}si{q}")
                vop(lambda ntl=ntl: VEC.reciprocal(
                        amax_q[:, 0:ntl], den_sb[:, 0:ntl]),
                    ["den_sb"], ["amax_q"], f"D{s}sr{q}")
                for g4 in range(0, ntl, 4):
                    sub = list(range(g4, min(ntl, g4 + 4)))
                    rep_tiles(lambda j: amax_q[:, j:j + 1], sub, 7,
                              qp(2)[:, 1 + g4 * P:1 + g4 * P + len(sub) * P],
                              ["qp2"], ["amax_q"], f"D{s}sb{q}_{g4}")
                    rep_tiles(lambda j: den_sb[:, j:j + 1], sub, 6,
                              qp(3)[:, 1 + g4 * P:1 + g4 * P + len(sub) * P],
                              ["qp3"], ["den_sb"], f"D{s}ib{q}_{g4}")
                if "amax1" in debug_taps and q == 0:
                    dbg1 = nc.dram_tensor("dbg_amax", [P, 4], F32,
                                          kind="ExternalOutput").ap()
                    dbg2 = nc.dram_tensor("dbg_s", [P, 512], F32,
                                          kind="ExternalOutput").ap()

                    dma("qs", dbg1, amax_q[:, 0:4], ["amax_q"], ["dbgA"],
                        "dbgA")
                    dma("qs", dbg2, qpool[:, 2 * 1024 + 1:2 * 1024 + 513],
                        ["qp2"], ["dbgS"], "dbgS")
                    dma("qs", dbg3, bigB[:, 0:8192], KB, ["dbgX"], "dbgX") if False else None
                for j in range(ntl):
                    jk = f"bB{j // 2}"
                    sj = qp(2)[:, 1 + j * P:1 + (j + 1) * P]
                    t_f = bigB[:, j * 1024:(j + 1) * 1024]
                    tdst = t_f.rearrange("p (c t) -> p c t", c=NCH)
                    t_i = bigB[:, 6144:6144 + NCH * P].bitcast(I32)[
                        :, 0:NCH * P]
                    t_d = bigB[:, 7168:7168 + NCH * P]
                    t_g = bigB[:, 8192:8192 + NCH * P]
                    vop(lambda sj=sj, tdst=tdst: VEC.tensor_tensor(
                            tdst, tdst,
                            sj.unsqueeze(1).broadcast_to((P, NCH, P)),
                            OP.mult),
                        [jk, "qp2"], [jk], f"D{s}c2{q}_{j}")
                    vop(lambda t_i=t_i, t_f=t_f: GPS.tensor_copy(t_i, t_f),
                        [jk], ["bB3"], f"D{s}c3{q}_{j}", eng="gp")
                    vop(lambda t_d=t_d, t_f=t_f, t_i=t_i: GPS.tensor_tensor(
                            t_d, t_f, t_i, OP.subtract),
                        [jk, "bB3"], ["bB3"], f"D{s}c4{q}_{j}", eng="gp")
                    pg.op("sc", lambda t_g=t_g, t_d=t_d: SCA.activation(
                            t_g, t_d, AF.Sign, bias=-0.5, scale=1.0),
                          reads=["bB3", "cst2"], writes=["bB4"],
                          tag=f"D{s}c5{q}_{j}")
                    pg.op("sc", lambda t_g=t_g: SCA.activation(
                            t_g, t_g, AF.Relu),
                          reads=["bB4"], writes=["bB4"],
                          tag=f"D{s}c5b{q}_{j}")
                    pg.op("sc", lambda t_d=t_d: SCA.activation(
                            t_d, t_d, AF.Sign, bias=0.5, scale=1.0),
                          reads=["bB3", "cst2"], writes=["bB3"],
                          tag=f"D{s}c6{q}_{j}")
                    pg.op("sc", lambda t_d=t_d: SCA.activation(
                            t_d, t_d, AF.Relu, bias=0.0, scale=-1.0),
                          reads=["bB3", "negone"], writes=["bB3"],
                          tag=f"D{s}c6b{q}_{j}")
                    vop(lambda t_f=t_f, t_g=t_g, t_i=t_i:
                        VEC.scalar_tensor_tensor(t_f, t_g, 1.0, t_i,
                                                 OP.mult, OP.add),
                        ["bB4", "bB3"], [jk], f"D{s}c7{q}_{j}")
                    qdst = qint.rearrange("p (c t) -> p c t",
                                          c=NCH)[:, :, 1 + j * P:
                                                 1 + (j + 1) * P]
                    vop(lambda qdst=qdst, t_f=t_f, t_d=t_d:
                        VEC.scalar_tensor_tensor(
                            qdst, t_d.rearrange("p (c t) -> p c t", c=NCH),
                            -1.0, t_f.rearrange("p (c t) -> p c t", c=NCH),
                            OP.mult, OP.add),
                        ["bB3", jk], ["bC0", "bC1", "bC2"],
                        f"D{s}c8{q}_{j}")
                if "qint" in debug_taps and q == 0:
                    dbg4 = nc.dram_tensor("dbg_qint", [NCH, P, 256], F32,
                                          kind="ExternalOutput").ap()
                    for c_ in range(NCH):
                        dma("qs", dbg4[c_], bigC[:, c_ * 384:c_ * 384 + 256],
                            ["bC0", "bC1", "bC2", "bC3"], ["dbgQ"],
                            f"dbgQ{c_}")
                off_main = 0
                n_edge = 2 if q < NQ - 1 else 1

                def emit_hc_front(hc, q=q, n_edge=n_edge,
                                  off_main=off_main):
                    pp = hc % 2
                    hkey = f"hg{hc}"
                    uk = f"qp{pp}"
                    uvw = qp(pp)[:, 0:514]
                    dma("qa", t1s[:, pp * D:(pp + 1) * D],
                        t1T_d[:, hc * P:(hc + 1) * P].rearrange(
                            "(c p) n -> p c n", p=P),
                        ["t1T_d"], [f"t1s{pp}"], f"D{s}h{q}_{hc}l")
                    bank = hc % 7
                    for c in range(NCH):
                        pg.op("pe", lambda c=c, pp=pp, bank=bank,
                              off_main=off_main: TEN.matmul(
                                pb[bank][:, :],
                                t1s[:, pp * D + c * P:pp * D + (c + 1) * P],
                                qint_c[c][:, off_main:off_main + 512],
                                start=(c == 0), stop=(c == NCH - 1)),
                              reads=[f"t1s{pp}", "bC0", "bC1", "bC2"],
                              writes=[PB[bank]], tag=f"D{s}h{q}_{hc}_{c}")
                    if n_edge > 0:
                        for c in range(NCH):
                            pg.op("pe", lambda c=c, pp=pp, hc=hc,
                                  n_edge=n_edge: TEN.matmul(
                                    pb[7][:, hc * 2:hc * 2 + n_edge],
                                    t1s[:, pp * D + c * P:
                                        pp * D + (c + 1) * P],
                                    qint_c[c][:, 512:512 + n_edge],
                                    start=(c == 0), stop=(c == NCH - 1),
                                    skip_group_check=True),
                                  reads=[f"t1s{pp}", "bC0", "bC1", "bC2"],
                                  writes=["pb7"], tag=f"D{s}he{q}_{hc}_{c}")
                    hblk = hga[:, hc * 512:(hc + 1) * 512]
                    if q == NQ - 1 and hc < 2:
                        vop(lambda uvw=uvw: VEC.memset(uvw[:, 513:514], 0.0),
                            [], [uk], f"D{s}uz{q}_{hc}")
                    vop(lambda uvw=uvw, bank=bank: VEC.tensor_mul(
                            uvw[:, 0:512], pb[bank][:, :],
                            qp(3)[:, 0:512]),
                        [PB[bank], "qp3"], [uk, PB[bank]],
                        f"D{s}hev{q}_{hc}")
                    if n_edge > 0:
                        vop(lambda uvw=uvw, hc=hc, n_edge=n_edge:
                            VEC.tensor_mul(
                                uvw[:, 512:512 + n_edge],
                                pb[7][:, hc * 2:hc * 2 + n_edge],
                                qp(3)[:, 512:512 + n_edge]),
                            ["pb7", "qp3"], [uk, "pb7"],
                            f"D{s}hee{q}_{hc}")
                    gv = hblk
                    gscr = bC(4)[:, (hc % 2) * 512:(hc % 2) * 512 + 512]
                    w0 = convw_c[:, 3 * hc:3 * hc + 1]
                    w1 = convw_c[:, 3 * hc + 1:3 * hc + 2]
                    w2 = convw_c[:, 3 * hc + 2:3 * hc + 3]
                    pg.op("sc", lambda gv=gv, uvw=uvw, w0=w0:
                          SCA.activation(gv, uvw[:, 0:512], AF.Copy,
                                         scale=w0),
                          reads=[uk, "convw_c"], writes=[hkey],
                          tag=f"D{s}cv0{q}_{hc}")
                    vop(lambda gscr=gscr, uvw=uvw, w2=w2:
                        GPS.tensor_tensor(
                            gscr, uvw[:, 2:514],
                            w2.broadcast_to((P, 512)), OP.mult),
                        [uk, "convw_c"], ["bC4"], f"D{s}cv2a{q}_{hc}",
                        eng="gp")
                    vop(lambda gv=gv, uvw=uvw, w1=w1:
                        VEC.scalar_tensor_tensor(
                            gv, uvw[:, 1:513], w1, gv, OP.mult, OP.add),
                        [uk, "convw_c", hkey], [hkey], f"D{s}cv1{q}_{hc}")
                    vop(lambda gv=gv, gscr=gscr:
                        GPS.tensor_tensor(gv, gv, gscr, OP.add),
                        ["bC4", hkey], [hkey], f"D{s}cv2b{q}_{hc}",
                        eng="gp")
                    pg.op("sc", lambda gv=gv, hc=hc: SCA.activation(
                            gv, gv, AF.Silu, bias=convb_c[:, hc:hc + 1],
                            scale=1.0),
                          reads=[hkey, "convb_c"], writes=[hkey],
                          tag=f"D{s}sl{q}_{hc}")

                def emit_hc_tail(hc, q=q):
                    hkey = f"hg{hc}"
                    gv = hga[:, hc * 512:(hc + 1) * 512]
                    treeB = bC(3)[:, 0:512]
                    ngv = bC(3)[:, 512:1024]
                    vop(lambda gv=gv, ngv=ngv: GPS.tensor_tensor(
                            ngv, gv, negone[:, 0:1].broadcast_to((P, 512)),
                            OP.mult),
                        [hkey, "negone", "bC3n"], ["bC3n"],
                        f"D{s}mn{q}_{hc}", eng="gp")
                    if hc == 0:
                        vop(lambda gv=gv: VEC.tensor_copy(
                                qp(4)[:, 0:512], gv),
                            [hkey], ["qp4"], f"D{s}mf{q}_{hc}")
                        vop(lambda ngv=ngv, treeB=treeB: VEC.tensor_copy(
                                treeB, ngv),
                            ["bC3n"], ["bC3", "bC3n"], f"D{s}mg{q}_{hc}")
                    else:
                        vop(lambda gv=gv: VEC.tensor_tensor(
                                qp(4)[:, 0:512], qp(4)[:, 0:512], gv,
                                OP.max),
                            [hkey, "qp4"], ["qp4"], f"D{s}mf{q}_{hc}")
                        vop(lambda ngv=ngv, treeB=treeB: VEC.tensor_tensor(
                                treeB, treeB, ngv, OP.max),
                            ["bC3n", "bC3"], ["bC3", "bC3n"],
                            f"D{s}mg{q}_{hc}")

                for hc in range(32):
                    emit_hc_front(hc)
                    if hc >= 2:
                        emit_hc_tail(hc - 2)
                emit_hc_tail(30)
                emit_hc_tail(31)
                if q < NQ - 1:
                    vop(lambda: VEC.tensor_copy(
                            qcarry[:, 0:8],
                            qint.rearrange("p (c t) -> p c t",
                                           c=NCH)[:, :, 512:513]),
                        ["bC0", "bC1", "bC2"], ["qcarry"], f"D{s}qs{q}")
                    vop(lambda: VEC.tensor_copy(icarry[:, :],
                                                qp(3)[:, 512:513]),
                        ["qp3"], ["icarry"], f"D{s}is{q}")
                vop(lambda: VEC.tensor_tensor(
                        qp(4)[:, 0:512], qp(4)[:, 0:512], bC(3)[:, 0:512],
                        OP.max),
                    ["qp4", "bC3"], ["qp4"], f"D{s}mj{q}")
                for j4 in range(4):
                    for jj in range(4):
                        for kk in range(4):
                            vop(lambda j4=j4, jj=jj, kk=kk: VEC.transpose(
                                    scr128[kk * 32:(kk + 1) * 32,
                                           jj * 32:(jj + 1) * 32],
                                    qp(4)[jj * 32:(jj + 1) * 32,
                                          j4 * P + kk * 32:
                                          j4 * P + (kk + 1) * 32]),
                                ["qp4"], ["scr128"],
                                f"D{s}a2{q}_{j4}_{jj}{kk}")
                    vop(lambda: VEC.reduce_max(scr128[:, 1:2],
                                               scr128[:, 0:P], AX),
                        ["scr128"], ["scr128"], f"D{s}a2r{q}_{j4}")
                    vop(lambda j4=j4: VEC.tensor_copy(
                            amax_q[:, j4:j4 + 1], scr128[:, 1:2]),
                        ["scr128"], ["amax_q"], f"D{s}a2c{q}_{j4}")
                vop(lambda: VEC.tensor_single_scalar(
                        amax_q[:, 0:4], amax_q[:, 0:4], 1e-5, OP.max),
                    ["amax_q"], ["amax_q"], f"D{s}2c{q}")
                vop(lambda: VEC.tensor_scalar_mul(
                        den_sb[:, 0:4], amax_q[:, 0:4], swr[:, 0:1]),
                    ["amax_q", "swr"], ["den_sb"], f"D{s}2i{q}")
                vop(lambda: VEC.reciprocal(amax_q[:, 4:8], amax_q[:, 0:4]),
                    ["amax_q"], ["amax_q"], f"D{s}2r{q}")
                vop(lambda: VEC.tensor_scalar_mul(
                        amax_q[:, 4:8], amax_q[:, 4:8], 127.0),
                    ["amax_q"], ["amax_q"], f"D{s}2s{q}")
                rep_tiles(lambda j: amax_q[:, 4 + j:5 + j], [0, 1, 2, 3], 7,
                          qp(2)[:, 0:512], ["qp2"], ["amax_q"], f"D{s}2b{q}")
                rep_tiles(lambda j: den_sb[:, j:j + 1], [0, 1, 2, 3], 6,
                          qp(3)[:, 0:512], ["qp3"], ["den_sb"], f"D{s}2v{q}")
                def qs0(b2):
                    pr = b2 % 2
                    tk = f"qp{pr}"
                    gview = hga[:, b2 * 512:(b2 + 1) * 512]
                    gk = f"hg{b2}"
                    ta = qp(pr)[:, 0:512]
                    ti = qp(pr)[:, 512:1024].bitcast(I32)[:, 0:512]
                    vop(lambda gview=gview, ta=ta: VEC.tensor_mul(
                            ta, gview, qp(2)[:, 0:512]),
                        [gk, "qp2"], [tk], f"D{s}z0{q}_{b2}")
                    vop(lambda ta=ta, ti=ti: VEC.tensor_copy(ti, ta),
                        [tk], [tk], f"D{s}z1{q}_{b2}")

                def qs1(b2):
                    pr = b2 % 2
                    tk = f"qp{pr}"
                    gview = hga[:, b2 * 512:(b2 + 1) * 512]
                    gk = f"hg{b2}"
                    ta = qp(pr)[:, 0:512]
                    ti = qp(pr)[:, 512:1024].bitcast(I32)[:, 0:512]
                    vop(lambda ta=ta, ti=ti, gview=gview: GPS.tensor_tensor(
                            gview, ta, ti, OP.subtract),
                        [tk, gk], [gk], f"D{s}z2{q}_{b2}", eng="gp")

                def qs2(b2):
                    pr = b2 % 2
                    td = hga[:, b2 * 512:(b2 + 1) * 512]
                    gk = f"hg{b2}"
                    tgk = "qp2b" if pr == 0 else "qp3b"
                    tg = (qp(2) if pr == 0 else qp(3))[:, 512:1024]
                    pg.op("sc", lambda td=td, tg=tg: SCA.activation(
                            tg, td, AF.Sign, bias=-0.5, scale=1.0),
                          reads=[gk, "cst2"], writes=[tgk],
                          tag=f"D{s}z3{q}_{b2}")
                    pg.op("sc", lambda tg=tg: SCA.activation(
                            tg, tg, AF.Relu),
                          reads=[tgk], writes=[tgk], tag=f"D{s}z3b{q}_{b2}")
                    pg.op("sc", lambda td=td: SCA.activation(
                            td, td, AF.Sign, bias=0.5, scale=1.0),
                          reads=[gk, "cst2"], writes=[gk],
                          tag=f"D{s}z4{q}_{b2}")
                    pg.op("sc", lambda td=td: SCA.activation(
                            td, td, AF.Relu, bias=0.0, scale=-1.0),
                          reads=[gk, "negone"], writes=[gk],
                          tag=f"D{s}z4b{q}_{b2}")

                def qs3(b2):
                    pr = b2 % 2
                    tk = f"qp{pr}"
                    tl = hga[:, b2 * 512:(b2 + 1) * 512]
                    gk = f"hg{b2}"
                    tgk = "qp2b" if pr == 0 else "qp3b"
                    tg = (qp(2) if pr == 0 else qp(3))[:, 512:1024]
                    ta = qp(pr)[:, 0:512]
                    ti = qp(pr)[:, 512:1024].bitcast(I32)[:, 0:512]
                    vop(lambda ta=ta, tg=tg, ti=ti:
                        VEC.scalar_tensor_tensor(
                            ta, tg, 1.0, ti, OP.mult, OP.add),
                        [tgk, tk], [tk], f"D{s}z5{q}_{b2}")
                    vop(lambda ta=ta, tl=tl, b2=b2:
                        VEC.scalar_tensor_tensor(
                            q2v[:, b2 * 512:(b2 + 1) * 512], tl, -1.0, ta,
                            OP.mult, OP.add),
                        [gk, tk], [f"bC{b2 // 4}"], f"D{s}z6{q}_{b2}")

                for i in range(34):
                    if i >= 2:
                        qs3(i - 2)
                    if i < 32:
                        qs0(i)
                    if 1 <= i <= 32:
                        qs1(i - 1)
                        qs2(i - 1)
                for kg in range(32):
                    wp = kg % 2
                    dma("qa", t2s[:, wp * D:(wp + 1) * D],
                        t2T_d[kg * P:(kg + 1) * P, :],
                        ["t2T_d"], [f"t2s{wp}"], f"D{s}w2l{q}_{kg}")
                    for oc in range(NCH):
                        pg.op("pe", lambda oc=oc, kg=kg, wp=wp:
                              TEN.matmul(
                                pb[oc][:, :],
                                t2s[:, wp * D + oc * P:
                                    wp * D + (oc + 1) * P],
                                q2v[:, kg * 512:(kg + 1) * 512],
                                start=(kg == 0), stop=(kg == 31)),
                              reads=[f"t2s{wp}", f"bC{kg // 4}"],
                              writes=[PB[oc]],
                              tag=f"D{s}w2{q}_{oc}_{kg}")
                sl = slice(q * 512, (q + 1) * 512)
                for oc in range(NCH):
                    pr = oc % 2
                    vop(lambda oc=oc, pr=pr: VEC.tensor_mul(
                            qp(pr)[:, 0:512], pb[oc][:, :],
                            qp(3)[:, 0:512]),
                        [PB[oc], "qp3"], [f"qp{pr}", PB[oc]],
                        f"D{s}w2e{q}_{oc}")
                    vop(lambda oc=oc, sl=sl, pr=pr: VEC.tensor_add(
                            xc(oc)[:, sl], xc(oc)[:, sl],
                            qp(pr)[:, 0:512]),
                        [f"qp{pr}", XK[oc]], [XK[oc]], f"D{s}w2x{q}_{oc}")

        for s in range(B):
            emit_load_x(s)
            if n_blocks >= 1:
                emit_l0_attn(s)
            if n_blocks >= 2:
                emit_l0_ffn(s)
            if n_blocks >= 3:
                emit_l1_attn(s)
            if n_blocks >= 4:
                emit_l1_ffn(s)
            emit_store_x(s)

        pg.op("gp", lambda: GPS.nop(),
              reads=[f"y{s}" for s in range(B)], writes=[], tag="final")

        nc.all_engine_barrier()
        r = nc._kernel_sem_range
        clear = range(r.start + 3, r.stop)
        nc.gpsimd.dma_reset(clear)
        nc.gpsimd.sem_clear(clear)
        nc.all_engine_barrier()
        pg.emit(nc, sems)
        nc.all_engine_barrier()
        nc.gpsimd.dma_reset(clear)
        nc.gpsimd.sem_clear(clear)
        nc.all_engine_barrier()

    return nc


def host_constants():
    ident = np.eye(P, dtype=np.float32)
    ones_col = np.ones((1, P), dtype=np.float32)
    onesT = np.ones((P, P), dtype=np.float32)
    ones128 = np.ones((P, 1), dtype=np.float32)
    sel16 = np.zeros((16, D), dtype=np.float32)
    for c in range(8):
        sel16[2 * c, c * 128:c * 128 + 64] = 1.0
        sel16[2 * c + 1, c * 128 + 64:c * 128 + 128] = 1.0
    identb = np.eye(P, dtype=np.float32).astype(ml_dtypes.bfloat16)
    return {"c_ident": ident, "c_ones_col": ones_col, "c_onesT": onesT,
            "c_ones128": ones128, "c_sel16": sel16, "c_identb": identb}


def kernel(**inputs):
    B_total = inputs["L0_attn_x"].shape[0]
    Bc = B_total // N_CORES
    nc = build_program(B=Bc, T=inputs["L0_attn_x"].shape[1], n_blocks=4)
    consts = host_constants()
    wnames = [k for k in inputs if k != "L0_attn_x"]
    in_maps = []
    for core in range(N_CORES):
        m = {"x": np.ascontiguousarray(
                inputs["L0_attn_x"][core * Bc:(core + 1) * Bc])}
        for k in wnames:
            m[k] = np.asarray(inputs[k])
        m.update(consts)
        in_maps.append(m)
    res = run_bass_kernel_spmd(nc, in_maps, core_ids=list(range(N_CORES)))
    return np.concatenate([r["y"] for r in res.results], axis=0)


# version 14
# speedup vs baseline: 1.0275x; 1.0275x over previous
from contextlib import ExitStack

import numpy as np
import ml_dtypes

import concourse.bass as bass
import concourse.mybir as mybir
from concourse.bass_utils import run_bass_kernel_spmd

F32 = mybir.dt.float32
BF16 = mybir.dt.bfloat16
I32 = mybir.dt.int32
AX = mybir.AxisListType.X
OP = mybir.AluOpType
AF = mybir.ActivationFunctionType

P = 128
D = 1024
NCH = 8
N_CORES = 8

DMA_ENGINES = ("qs", "qa", "qg")
ALL_ENGINES = ("pe", "dve", "sc", "gp", "qs", "qa", "qg")
PHYS = {"pe": "tensor", "dve": "vector", "sc": "scalar", "gp": "gpsimd",
        "qs": "sync", "qa": "scalar", "qg": "gpsimd"}


NSUB = 8


class Op:
    __slots__ = ("engine", "fn", "reads", "writes", "idx", "eng_pos",
                 "need_inc", "waits", "tag", "semname", "qidx")

    def __init__(self, engine, fn, reads, writes, idx, tag):
        self.engine = engine
        self.fn = fn
        self.reads = reads
        self.writes = writes
        self.idx = idx
        self.eng_pos = 0
        self.need_inc = False
        self.waits = {}
        self.tag = tag
        self.semname = engine
        self.qidx = 0


class Prog:
    def __init__(self, paranoid=False, paranoid_tags=(),
                 paranoid_engines=("dve", "sc", "gp")):
        self.ops = []
        self.qcount = {q: 0 for q in DMA_ENGINES}
        self.paranoid = paranoid
        self.paranoid_tags = tuple(paranoid_tags)
        self.paranoid_engines = tuple(paranoid_engines)

    def _ptag(self, o):
        return any(o.tag.startswith(p) for p in self.paranoid_tags)

    def op(self, engine, fn, reads=(), writes=(), tag=""):
        assert engine in ALL_ENGINES, engine
        o = Op(engine, fn, tuple(reads), tuple(writes), len(self.ops), tag)
        if engine in DMA_ENGINES:
            o.qidx = self.qcount[engine]
            self.qcount[engine] += 1
            o.semname = f"{engine}{o.qidx % NSUB}"
        self.ops.append(o)
        return o

    def resolve(self):
        last_write = {}
        readers = {}
        deps_of = [set() for _ in self.ops]
        for o in self.ops:
            deps = deps_of[o.idx]
            for k in o.reads:
                w = last_write.get(k)
                if w is not None:
                    deps.add(w.idx)
            for k in o.writes:
                w = last_write.get(k)
                if w is not None:
                    deps.add(w.idx)
                for r in readers.get(k, ()):
                    deps.add(r.idx)
            for k in o.writes:
                last_write[k] = o
                readers[k] = []
            for k in o.reads:
                readers.setdefault(k, []).append(o)
        for o in self.ops:
            if o.engine in DMA_ENGINES:
                o.need_inc = True
            for d in deps_of[o.idx]:
                dop = self.ops[d]
                if (self.paranoid or dop.engine != o.engine
                        or dop.engine in DMA_ENGINES or self._ptag(o)
                        or o.engine in self.paranoid_engines):
                    dop.need_inc = True
        counts = {}
        for o in self.ops:
            if o.need_inc:
                counts[o.semname] = counts.get(o.semname, 0) + (
                    16 if o.engine in DMA_ENGINES else 1)
            o.eng_pos = counts.get(o.semname, 0)
        for o in self.ops:
            if o.engine in DMA_ENGINES and o.qidx >= NSUB:
                o.waits[o.semname] = max(
                    o.waits.get(o.semname, 0), o.eng_pos - 16)
            for d in deps_of[o.idx]:
                dop = self.ops[d]
                if (not self.paranoid and not self._ptag(o)
                        and o.engine not in self.paranoid_engines
                        and dop.engine == o.engine
                        and dop.engine not in DMA_ENGINES):
                    continue
                if not dop.need_inc:
                    continue
                cur = o.waits.get(dop.semname, 0)
                o.waits[dop.semname] = max(cur, dop.eng_pos)
        seen = {}
        for o in self.ops:
            s = seen.setdefault(o.engine, {})
            for e in list(o.waits):
                v = o.waits[e]
                if s.get(e, -1) >= v:
                    del o.waits[e]
                else:
                    s[e] = v

    def emit(self, nc, sems):
        self.resolve()
        by_phys = {}
        for o in self.ops:
            by_phys.setdefault(PHYS[o.engine], []).append(o)

        def run_stream(eng_obj, ops_list):
            for o in ops_list:
                for e, v in sorted(o.waits.items()):
                    eng_obj.wait_ge(sems[e], v)
                last = o.fn()
                if o.need_inc:
                    assert last is not None, f"op {o.tag} returned None"
                    last.then_inc(sems[o.semname],
                                  16 if o.engine in DMA_ENGINES else 1)

        with nc.Block() as block:
            if "sync" in by_phys:
                @block.sync
                def _(eng):
                    run_stream(eng, by_phys["sync"])
            if "tensor" in by_phys:
                @block.tensor
                def _(eng):
                    run_stream(eng, by_phys["tensor"])
            if "vector" in by_phys:
                @block.vector
                def _(eng):
                    run_stream(eng, by_phys["vector"])
            if "scalar" in by_phys:
                @block.scalar
                def _(eng):
                    run_stream(eng, by_phys["scalar"])
            if "gpsimd" in by_phys:
                @block.gpsimd
                def _(eng):
                    run_stream(eng, by_phys["gpsimd"])


def build_program(B=4, T=2048, n_blocks=4, paranoid=False, debug_taps=(),
                  paranoid_tags=()):
    NT1 = T // 128
    NT5 = T // 512
    NQ = T // 512

    nc = bass.Bass("TRN2", target_bir_lowering=False, debug=False,
                   detect_race_conditions=paranoid,
                   dynamic_dma_scratch_size=64)

    x_in = nc.dram_tensor("x", [B, T, D], F32, kind="ExternalInput").ap()
    y_out = nc.dram_tensor("y", [B, T, D], F32, kind="ExternalOutput").ap()

    def win(name, shape):
        return nc.dram_tensor(name, shape, F32, kind="ExternalInput").ap()

    Wq = win("L0_attn_Wq", [D, D]); bq = win("L0_attn_bq", [D])
    Wk = win("L0_attn_Wk", [D, D]); bk = win("L0_attn_bk", [D])
    Wv = win("L0_attn_Wv", [D, D]); bv = win("L0_attn_bv", [D])
    Wo = win("L0_attn_Wo", [D, D]); bo = win("L0_attn_bo", [D])
    Wf = win("L0_ffn_W", [D, D]); bfv = win("L0_ffn_b", [D])
    lora_down_h = win("L0_ffn_lora_down", [4, 100])
    lora_up_h = win("L0_ffn_lora_up", [50, 4])
    down_aux_h = win("L0_ffn_down_aux", [100, D])
    up_aux_h = win("L0_ffn_up_aux", [D, 50])
    Wqkv = win("L1_attn_w_qkv", [3 * D, D]); bqkv = win("L1_attn_b_qkv", [3 * D])
    Wg1 = win("L1_attn_w_g1", [64, D]); bg1v = win("L1_attn_b_g1", [64])
    Wg2 = win("L1_attn_w_g2", [D, 64]); bg2v = win("L1_attn_b_g2", [D])
    Wout = win("L1_attn_w_out", [D, D]); boutv = win("L1_attn_b_out", [D])
    ln_gv = win("L1_attn_ln_g", [D]); ln_bv = win("L1_attn_ln_b", [D])
    W1 = win("L1_ffn_w1", [4 * D, D])
    convw_h = win("L1_ffn_conv_w", [4 * D, 1, 3])
    convb_h = win("L1_ffn_conv_b", [4 * D])
    W2 = win("L1_ffn_w2", [D, 4 * D])

    c_ident = nc.dram_tensor("c_ident", [P, P], F32, kind="ExternalInput").ap()
    c_identb = nc.dram_tensor("c_identb", [P, P], BF16, kind="ExternalInput").ap()
    c_ones_col = nc.dram_tensor("c_ones_col", [1, P], F32, kind="ExternalInput").ap()
    c_onesT = nc.dram_tensor("c_onesT", [P, P], F32, kind="ExternalInput").ap()
    c_ones128 = nc.dram_tensor("c_ones128", [P, 1], F32, kind="ExternalInput").ap()
    c_sel16 = nc.dram_tensor("c_sel16", [16, D], F32, kind="ExternalInput").ap()

    wqT_d = nc.dram_tensor("wqT_d", [D, D], F32, kind="Internal").ap()
    wkT_d = nc.dram_tensor("wkT_d", [D, D], F32, kind="Internal").ap()
    wvT_d = nc.dram_tensor("wvT_d", [D, D], F32, kind="Internal").ap()
    woT_d = nc.dram_tensor("woT_d", [D, D], F32, kind="Internal").ap()
    weffT_d = nc.dram_tensor("weffT_d", [D, D], F32, kind="Internal").ap()
    qkvT_d = nc.dram_tensor("qkvT_d", [D, 3 * D], F32, kind="Internal").ap()
    woutT_d = nc.dram_tensor("woutT_d", [D, D], F32, kind="Internal").ap()
    g1T_d = nc.dram_tensor("g1T_d", [D, 64], F32, kind="Internal").ap()
    g2T_d = nc.dram_tensor("g2T_d", [64, D], F32, kind="Internal").ap()
    t1T_d = nc.dram_tensor("t1T_d", [D, 4 * D], BF16, kind="Internal").ap()
    t2T_d = nc.dram_tensor("t2T_d", [4 * D, D], BF16, kind="Internal").ap()

    pg = Prog(paranoid=paranoid, paranoid_tags=paranoid_tags)

    with ExitStack() as ctx:
        E = ctx.enter_context

        def sb(name, shape, dt=F32):
            return E(nc.sbuf_tensor(name, shape, dt))

        x_all = sb("x_all", [P, NCH * T])
        bigB = sb("bigB", [P, 16384])
        bigC = sb("bigC", [P, 8192])
        qpool = sb("qpool", [P, 4608])
        band = sb("band", [P, 1536])
        nm_row = sb("nm_row", [65, T])
        rows = sb("rows", [P, 4096])
        kv_acc = sb("kv_acc", [P, 1024])
        norm_lhs = sb("norm_lhs", [P, 16 * NCH])
        scr128 = sb("scr128", [P, P])
        kvblk = scr128
        rstd_tm = sb("rstd_tm", [P, NT1])
        var_tm = sb("var_tm", [P, NT1])
        amax_q = sb("amax_q", [P, 8])
        den_sb = sb("den_sb", [P, 8])
        s_w = sb("s_w", [P, 2])
        swr = sb("swr", [P, 1])
        red_acc = sb("red_acc", [P, 1])
        wsum_tmp = sb("wsum_tmp", [P, 24])
        t1s = sb("t1s", [P, 2 * D], BF16)
        t2s = sb("t2s", [P, 2 * D], BF16)
        qcarry = sb("qcarry", [P, 16], BF16)
        icarry = sb("icarry", [P, 1])
        ident = sb("ident", [P, P])
        identb = sb("identb", [P, P], BF16)
        onesT = sb("onesT", [P, P])
        bq_c = sb("bq_c", [P, 8]); bo_c = sb("bo_c", [P, 8])
        bf_c = sb("bf_c", [P, 8]); bout_c = sb("bout_c", [P, 8])
        bq1_c = sb("bq1_c", [P, 8]); bg2_c = sb("bg2_c", [P, 8])
        lng_c = sb("lng_c", [P, 8]); lnb_c = sb("lnb_c", [P, 8])
        bg1_c = sb("bg1_c", [64, 1])
        convw_c = sb("convw_c", [P, 96])
        convb_c = sb("convb_c", [P, 32])
        negone = sb("negone", [P, 1])
        cst2 = sb("cst2", [P, 2])
        nc.const_aps.register(F32, 0.5, cst2[:, 0:1])
        nc.const_aps.register(F32, -0.5, cst2[:, 1:2])
        nc.const_aps.register(F32, -1.0, negone[:, 0:1])

        pb = [E(nc.psum_tensor(f"pb{i}", [P, 512], F32)) for i in range(8)]
        PB = [f"pb{i}" for i in range(8)]

        sems = {}
        for e in ("pe", "dve", "sc", "gp"):
            sems[e] = E(nc.semaphore(f"sem_{e}"))
        for q in DMA_ENGINES:
            for j in range(NSUB):
                sems[f"{q}{j}"] = E(nc.semaphore(f"sem_{q}{j}"))
        TEN, VEC, SCA, GPS, SYN = (nc.tensor, nc.vector, nc.scalar,
                                   nc.gpsimd, nc.sync)

        def xc(c):
            return x_all[:, c * T:(c + 1) * T]

        XK = [f"x{c}" for c in range(NCH)]

        def xstr(tau):
            return x_all[:, 0:NCH * T].rearrange(
                "p (c t) -> p c t", c=NCH)[:, :, tau * P:(tau + 1) * P]

        def bB(i):
            return bigB[:, i * 1024:(i + 1) * 1024]

        KB = [f"bB{i}" for i in range(16)]

        def bC(i):
            return bigC[:, i * 1024:(i + 1) * 1024]

        KC = [f"bC{i}" for i in range(8)]

        def qp(i):
            return qpool[:, i * 1024:min((i + 1) * 1024, 4608)]

        KQ = [f"qp{i}" for i in range(5)]

        bkbc = None
        bvbc = None
        wres = [bB(c) for c in range(8)]
        wres2 = [bB(8 + c) for c in range(8)]
        WK1 = [f"bB{c}" for c in range(8)]
        WK2 = [f"bB{8 + c}" for c in range(8)]
        hga = bigB[:, 0:32 * 512]
        bkbc = bigC[:, 3 * 1024:4 * 1024]
        bvbc = bigC[:, 4 * 1024:5 * 1024]
        norm16t = band[0:16, 1024:1536]
        sel16 = band[0:16, 0:D]
        ghv = band[64:128, 1024:1536]

        ROWS = {}
        for i, nmn in enumerate(["wq", "weff", "q1", "g1", "bk", "bv",
                                 "bk1", "bv1", "wk", "wv", "k1", "v1"]):
            ROWS[nmn] = (32 * (i % 3), (i // 3) * 1024)

        def row(nmn, lo, n):
            p0, f0 = ROWS[nmn]
            return rows[p0:p0 + 1, f0 + lo:f0 + lo + n]

        def rowbase(nmn):
            return ROWS[nmn][0]

        def nmr_at(nmn, lo, n):
            b = ROWS[nmn][0]
            return nm_row[b:b + 1, lo:lo + n]

        def ones_at(nmn):
            b = ROWS[nmn][0]
            return onesT[b:b + 1, 0:P]

        def dma(q, out, in_, reads, writes, tag="", slow=False):
            eng = {"qs": SYN, "qa": SCA, "qg": GPS}[q]

            def _go(out=out, in_=in_, eng=eng, slow=slow, tag=tag):
                try:
                    return eng.dma_start(out=out, in_=in_,
                                         allow_slow_non_contiguous=slow)
                except Exception as e:
                    raise RuntimeError(f"dma op {tag!r} failed: {e}") from e

            pg.op(q, _go, reads=reads, writes=writes, tag=tag)

        def vop(fn, reads, writes, tag="", eng="dve"):
            pg.op(eng, fn, reads=reads, writes=writes, tag=tag)

        dma("qs", ident[:], c_ident, [], ["ident"], "c0")
        dma("qs", identb[:], c_identb, [], ["identb"], "c1")
        dma("qs", onesT[:], c_onesT, [], ["onesT"], "c3")
        dma("qs", sel16, c_sel16, [], ["sel16"], "c5")
        vop(lambda: VEC.memset(negone[:, :], -1.0), [], ["negone"], "c6")
        vop(lambda: VEC.memset(cst2[:, 0:1], 0.5), [], ["cst2"], "c7")
        vop(lambda: VEC.memset(cst2[:, 1:2], -0.5), [], ["cst2"], "c8")

        def load_bias_col(dst, dst_key, src_vec, n, tag):
            tmp = bB(0)
            dma("qs", tmp[0:n, 0:P], src_vec.rearrange("(a b) -> a b", b=P),
                [], ["bB0"], f"{tag}l")
            pg.op("pe", lambda tmp=tmp, n=n: TEN.transpose(
                    pb[0][0:P, 0:n], tmp[0:n, 0:P], ident[0:n, 0:n]),
                  reads=["bB0", "ident"], writes=["pb0"], tag=f"{tag}t")
            vop(lambda dst=dst, n=n: VEC.tensor_copy(dst[:, 0:n],
                                                     pb[0][:, 0:n]),
                ["pb0"], [dst_key, "pb0"], f"{tag}e")

        load_bias_col(bq_c, "bq_c", bq, 8, "bq")
        load_bias_col(bo_c, "bo_c", bo, 8, "bo")
        load_bias_col(bf_c, "bf_c", bfv, 8, "bf")
        load_bias_col(bout_c, "bout_c", boutv, 8, "bout")
        load_bias_col(bq1_c, "bq1_c", bqkv[0:D], 8, "bq1")
        load_bias_col(bg2_c, "bg2_c", bg2v, 8, "bg2")
        load_bias_col(lng_c, "lng_c", ln_gv, 8, "lng")
        load_bias_col(lnb_c, "lnb_c", ln_bv, 8, "lnb")
        dma("qs", bB(0)[0:1, 0:64], bg1v.rearrange("(a b) -> a b", a=1),
            [], ["bB0"], "bg1l")
        pg.op("pe", lambda: TEN.transpose(pb[0][0:64, 0:1], bB(0)[0:1, 0:64],
                                          ident[0:1, 0:1]),
              reads=["bB0", "ident"], writes=["pb0"], tag="bg1t")
        vop(lambda: VEC.tensor_copy(bg1_c[:, :], pb[0][0:64, 0:1]),
            ["pb0"], ["bg1_c", "pb0"], "bg1e")
        for nmn, vec in (("bk", bk), ("bv", bv), ("bk1", bqkv[D:2 * D]),
                         ("bv1", bqkv[2 * D:3 * D])):
            dma("qs", row(nmn, 0, D), vec.rearrange("(a b) -> a b", a=1),
                [], ["rows"], f"r_{nmn}")
        for j in range(32):
            dma("qs", convw_c[:, 3 * j:3 * j + 3],
                convw_h.rearrange("a b c -> (b a) c")[j * P:(j + 1) * P, :],
                [], ["convw_c"], f"cw{j}")
        dma("qs", convb_c[:, :], convb_h.rearrange("(a b) -> b a", b=P),
            [], ["convb_c"], "cb", slow=True)

        lora_s = bigB[0:100, 2 * 1024:2 * 1024 + 4]
        lora_ut = bigB[0:4, 2 * 1024 + 16:2 * 1024 + 66]
        msum = bigB[0:50, 2 * 1024 + 80:2 * 1024 + 81]
        daux = bigB[0:100, 3 * 1024:4 * 1024]
        lora_a = bigB[0:4, 4 * 1024:5 * 1024]
        lora_m = bigB[0:50, 5 * 1024:6 * 1024]
        upaux_t = bigB[0:50, 6 * 1024:7 * 1024]

        dma("qs", bB(1)[0:50, 0:4], lora_up_h, [], ["bB1"], "lu")
        dma("qs", bB(1)[0:4, 100:200], lora_down_h, [], ["bB1"], "ldn")
        dma("qs", daux, down_aux_h, [], ["bB3"], "dax")
        pg.op("pe", lambda: TEN.transpose(pb[0][0:100, 0:4],
                                          bB(1)[0:4, 100:200],
                                          ident[0:4, 0:4]),
              reads=["bB1", "ident"], writes=["pb0"], tag="ldT")
        vop(lambda: VEC.tensor_copy(lora_s, pb[0][0:100, 0:4]),
            ["pb0"], ["bB2", "pb0"], "ldTe")
        pg.op("pe", lambda: TEN.transpose(pb[0][0:4, 0:50], bB(1)[0:50, 0:4],
                                          ident[0:50, 0:50]),
              reads=["bB1", "ident"], writes=["pb0"], tag="luT")
        vop(lambda: VEC.tensor_copy(lora_ut, pb[0][0:4, 0:50]),
            ["pb0"], ["bB2", "pb0"], "luTe")
        for ic in range(NCH):
            dma("qs", bB(1)[0:P, 0:50], up_aux_h[ic * P:(ic + 1) * P, :],
                [], ["bB1"], f"ua{ic}")
            pg.op("pe", lambda: TEN.transpose(pb[1][0:50, 0:P],
                                              bB(1)[0:P, 0:50], ident[:, :]),
                  reads=["bB1", "ident"], writes=["pb1"], tag=f"uaT{ic}")
            vop(lambda ic=ic: VEC.tensor_copy(
                    upaux_t[:, ic * P:(ic + 1) * P], pb[1][0:50, 0:P]),
                ["pb1"], ["bB6", "pb1"], f"uaTe{ic}")
        for h in range(2):
            pg.op("pe", lambda h=h: TEN.matmul(
                    pb[0][0:4, :], lora_s, daux[:, h * 512:(h + 1) * 512],
                    start=True, stop=True),
                  reads=["bB2", "bB3"], writes=["pb0"], tag=f"lA{h}")
            vop(lambda h=h: VEC.tensor_copy(
                    lora_a[:, h * 512:(h + 1) * 512], pb[0][0:4, :]),
                ["pb0"], ["bB4", "pb0"], f"lAe{h}")
        for h in range(2):
            pg.op("pe", lambda h=h: TEN.matmul(
                    pb[0][0:50, :], lora_ut, lora_a[:, h * 512:(h + 1) * 512],
                    start=True, stop=True),
                  reads=["bB2", "bB4"], writes=["pb0"], tag=f"lM{h}")
            vop(lambda h=h: VEC.tensor_copy(
                    lora_m[:, h * 512:(h + 1) * 512], pb[0][0:50, :]),
                ["pb0"], ["bB5", "pb0"], f"lMe{h}")
        vop(lambda: VEC.reduce_sum(msum, lora_m, AX),
            ["bB5"], ["bB2"], "lMs")

        def prep_w(W_hbm, wT_d, key, n_out, wsum_names, lora, tag):
            noc = n_out // P
            for oc in range(noc):
                nb = oc % 2
                wn = bB(nb)
                dma("qa", wn[:, :], W_hbm[oc * P:(oc + 1) * P, :],
                    [], [KB[nb]], f"{tag}l{oc}")
                if wsum_names:
                    vop(lambda wn=wn, oc=oc: VEC.reduce_sum(
                            wsum_tmp[:, oc:oc + 1], wn[:, :], AX),
                        [KB[nb]], ["wsum_tmp"], f"{tag}w{oc}")
                for ic in range(NCH):
                    bank = 2 + (ic % 4)
                    pg.op("pe", lambda wn=wn, ic=ic, bank=bank:
                          TEN.transpose(pb[bank][:, 0:P],
                                        wn[:, ic * P:(ic + 1) * P],
                                        ident[:, :]),
                          reads=[KB[nb], "ident"], writes=[PB[bank]],
                          tag=f"{tag}T{oc}_{ic}")
                    if lora:
                        pg.op("pe", lambda ic=ic, oc=oc, bank=bank:
                              TEN.matmul(pb[bank][:, 0:P],
                                         lora_m[:, ic * P:(ic + 1) * P],
                                         upaux_t[:, oc * P:(oc + 1) * P],
                                         start=False, stop=True,
                                         skip_group_check=True),
                              reads=["bB5", "bB6"], writes=[PB[bank]],
                              tag=f"{tag}L{oc}_{ic}")
                    eng = "sc" if ic % 2 == 0 else "dve"
                    fn = SCA.copy if eng == "sc" else VEC.tensor_copy
                    vop(lambda fn=fn, ic=ic, bank=bank: fn(
                            bigC[:, 4096 + (ic % 4) * P:
                                 4096 + (ic % 4) * P + P],
                            pb[bank][:, 0:P]),
                        [PB[bank]], ["bC4", PB[bank]], f"{tag}e{oc}_{ic}",
                        eng=eng)
                    if ic % 4 == 3:
                        lo = (ic - 3) * P
                        dma("qs", wT_d[lo:lo + 512,
                                       oc * P:(oc + 1) * P].rearrange(
                                "(a p) b -> p a b", p=P),
                            bigC[:, 4096:4096 + 512],
                            ["bC4"], [key], f"{tag}s{oc}_{ic}")
            if wsum_names:
                for gi, wname in enumerate(wsum_names):
                    pg.op("pe", lambda gi=gi: TEN.transpose(
                            pb[6][0:8, 0:P], wsum_tmp[:, gi * 8:gi * 8 + 8],
                            ident[:, :]),
                          reads=["wsum_tmp", "ident"], writes=[PB[6]],
                          tag=f"{tag}wT{gi}")
                    vop(lambda gi=gi: VEC.tensor_copy(
                            scr128[0:8, 0:P], pb[6][0:8, 0:P]),
                        [PB[6]], ["scr128", PB[6]], f"{tag}we{gi}")
                    for rr in range(8):
                        dma("qs", row(wname, rr * P, P),
                            scr128[rr:rr + 1, 0:P],
                            ["scr128"], ["rows"], f"{tag}wf{gi}_{rr}")

        prep_w(Wq, wqT_d, "wqT_d", D, ["wq"], False, "pq")
        prep_w(Wk, wkT_d, "wkT_d", D, ["wk"], False, "pk")
        prep_w(Wv, wvT_d, "wvT_d", D, ["wv"], False, "pv")
        prep_w(Wo, woT_d, "woT_d", D, [], False, "po")
        prep_w(Wf, weffT_d, "weffT_d", D, ["weff"], True, "pf")
        for h in range(2):
            pg.op("pe", lambda h=h: TEN.matmul(
                    pb[7][0:1, :], msum,
                    upaux_t[:, h * 512:(h + 1) * 512], start=True, stop=True),
                  reads=["bB2", "bB6"], writes=[PB[7]], tag=f"pfw{h}")
            vop(lambda h=h: VEC.tensor_add(
                    row("weff", h * 512, 512), row("weff", h * 512, 512),
                    pb[7][0:1, :]),
                [PB[7], "rows"], ["rows", PB[7]], f"pfwa{h}")
        prep_w(Wqkv, qkvT_d, "qkvT_d", 3 * D, ["q1", "k1", "v1"], False, "pj")
        prep_w(Wout, woutT_d, "woutT_d", D, [], False, "pu")
        dma("qs", bB(0)[0:64, 0:D], Wg1, [], ["bB0"], "g1l")
        for ic in range(NCH):
            pg.op("pe", lambda ic=ic: TEN.transpose(
                    pb[2][0:P, 0:64], bB(0)[0:64, ic * P:(ic + 1) * P],
                    ident[0:64, 0:64]),
                  reads=["bB0", "ident"], writes=[PB[2]], tag=f"g1T{ic}")
            vop(lambda: VEC.tensor_copy(bB(1)[:, 0:64], pb[2][:, 0:64]),
                [PB[2]], ["bB1", PB[2]], f"g1e{ic}")
            dma("qs", g1T_d[ic * P:(ic + 1) * P, :], bB(1)[:, 0:64],
                ["bB1"], ["g1T_d"], f"g1s{ic}")
        vop(lambda: VEC.reduce_sum(scr128[0:64, 0:1], bB(0)[0:64, 0:D], AX),
            ["bB0"], ["scr128"], "g1w")
        pg.op("pe", lambda: TEN.transpose(pb[2][0:1, 0:64],
                                          scr128[0:64, 0:1],
                                          ident[0:64, 0:64]),
              reads=["scr128", "ident"], writes=[PB[2]], tag="g1wT")
        vop(lambda: VEC.tensor_copy(row("g1", 0, 64), pb[2][0:1, 0:64]),
            [PB[2]], ["rows", PB[2]], "g1we")
        for ic in range(NCH):
            dma("qs", bB(0)[0:P, 0:64], Wg2[ic * P:(ic + 1) * P, :],
                [], ["bB0"], f"g2l{ic}")
            pg.op("pe", lambda: TEN.transpose(pb[3][0:64, 0:P],
                                              bB(0)[0:P, 0:64], ident[:, :]),
                  reads=["bB0", "ident"], writes=[PB[3]], tag=f"g2T{ic}")
            vop(lambda: VEC.tensor_copy(bB(1)[0:64, 0:P], pb[3][0:64, 0:P]),
                [PB[3]], ["bB1", PB[3]], f"g2e{ic}")
            dma("qs", g2T_d[:, ic * P:(ic + 1) * P], bB(1)[0:64, 0:P],
                ["bB1"], ["g2T_d"], f"g2s{ic}")

        q_t = bB(2); q_d = bB(3); q_g = bB(4)
        q_i = bB(5).bitcast(I32)[:, 0:D]
        q_b = bB(6).bitcast(BF16)[:, 0:D]
        q_o = bB(7).bitcast(BF16)[:, 0:D]

        def prep_tern(W_hbm, tT_d, key, n_out, n_in, sw_col, tag):
            noc = n_out // P
            nig = n_in // D
            first = True
            for oc in range(noc):
                for ig in range(nig):
                    nb = (oc * nig + ig) % 2
                    wn = bB(nb)
                    dma("qa", wn[:, :],
                        W_hbm[oc * P:(oc + 1) * P, ig * D:(ig + 1) * D],
                        [], [KB[nb]], f"{tag}ml{oc}_{ig}")
                    vop(lambda wn=wn: VEC.reduce_sum(
                            scr128[:, 1:2], wn[:, :], AX,
                            apply_absolute_value=True),
                        [KB[nb]], ["scr128"], f"{tag}mr{oc}_{ig}")
                    if first:
                        vop(lambda: VEC.tensor_copy(red_acc[:, :],
                                                    scr128[:, 1:2]),
                            ["scr128"], ["red_acc"], f"{tag}mi")
                        first = False
                    else:
                        vop(lambda: VEC.tensor_add(
                                red_acc[:, :], red_acc[:, :], scr128[:, 1:2]),
                            ["scr128", "red_acc"], ["red_acc"],
                            f"{tag}ma{oc}_{ig}")
            pg.op("pe", lambda: TEN.matmul(pb[4][0:1, 0:1], red_acc[:, :],
                                           onesT[:, 0:1], start=True,
                                           stop=True),
                  reads=["red_acc", "onesT"], writes=[PB[4]],
                  tag=f"{tag}ms")
            vop(lambda n_out=n_out, n_in=n_in: VEC.tensor_scalar(
                    scr128[0:1, 0:1], pb[4][0:1, 0:1],
                    1.0 / (n_out * n_in), 1e-5, OP.mult, OP.max),
                [PB[4]], ["scr128", PB[4]], f"{tag}mc")
            vop(lambda: VEC.reciprocal(scr128[0:1, 1:2], scr128[0:1, 0:1]),
                ["scr128"], ["scr128"], f"{tag}mrc")
            pg.op("pe", lambda sw_col=sw_col: TEN.matmul(
                    pb[4][:, sw_col:sw_col + 1], onesT[0:1, 0:P],
                    scr128[0:1, 1:2], start=True, stop=True),
                  reads=["scr128", "onesT"], writes=[PB[4]],
                  tag=f"{tag}mp")
            vop(lambda sw_col=sw_col: VEC.tensor_copy(
                    s_w[:, sw_col:sw_col + 1], pb[4][:, sw_col:sw_col + 1]),
                [PB[4]], ["s_w", PB[4]], f"{tag}me")
            for oc in range(noc):
                for ig in range(nig):
                    nb = (oc * nig + ig) % 2
                    wn = bB(nb)
                    dma("qa", wn[:, :],
                        W_hbm[oc * P:(oc + 1) * P, ig * D:(ig + 1) * D],
                        [], [KB[nb]], f"{tag}tl{oc}_{ig}")
                    vop(lambda wn=wn, sw_col=sw_col: GPS.tensor_tensor(
                            q_t[:, :], wn[:, :],
                            s_w[:, sw_col:sw_col + 1].broadcast_to((P, D)),
                            OP.mult),
                        [KB[nb], "s_w"], ["bB2"], f"{tag}q1_{oc}_{ig}",
                        eng="gp")
                    vop(lambda: GPS.tensor_copy(q_i, q_t[:, :]),
                        ["bB2"], ["bB5"], f"{tag}q2_{oc}_{ig}", eng="gp")
                    vop(lambda: GPS.tensor_tensor(
                            q_d[:, :], q_t[:, :], q_i, OP.subtract),
                        ["bB2", "bB5"], ["bB3"], f"{tag}q3_{oc}_{ig}",
                        eng="gp")
                    vop(lambda: VEC.tensor_single_scalar(
                            q_g[:, :], q_d[:, :], 0.5, OP.is_gt),
                        ["bB3"], ["bB4"], f"{tag}q4_{oc}_{ig}")
                    vop(lambda: VEC.tensor_single_scalar(
                            q_d[:, :], q_d[:, :], -0.5, OP.is_lt),
                        ["bB3"], ["bB3"], f"{tag}q5_{oc}_{ig}")
                    vop(lambda: VEC.scalar_tensor_tensor(
                            q_t[:, :], q_g[:, :], 1.0, q_i, OP.mult, OP.add),
                        ["bB4", "bB5"], ["bB2"], f"{tag}q6_{oc}_{ig}")
                    vop(lambda: VEC.scalar_tensor_tensor(
                            q_t[:, :], q_d[:, :], -1.0, q_t[:, :],
                            OP.mult, OP.add),
                        ["bB3", "bB2"], ["bB2"], f"{tag}q7_{oc}_{ig}")
                    vop(lambda: VEC.tensor_scalar(
                            q_b, q_t[:, :], -1.0, 1.0, OP.max, OP.min),
                        ["bB2"], ["bB6"], f"{tag}q8_{oc}_{ig}")
                    for ic in range(NCH):
                        bank = 2 + (ic % 2)
                        pg.op("pe", lambda ic=ic, bank=bank: TEN.transpose(
                                pb[bank][:, 0:P].bitcast(BF16)[:, 0:P],
                                q_b[:, ic * P:(ic + 1) * P], identb[:, :]),
                              reads=["bB6", "identb"], writes=[PB[bank]],
                              tag=f"{tag}T{oc}_{ig}_{ic}")
                        vop(lambda ic=ic, bank=bank: VEC.tensor_copy(
                                q_o[:, ic * P:(ic + 1) * P],
                                pb[bank][:, 0:P].bitcast(BF16)[:, 0:P]),
                            [PB[bank]], ["bB7", PB[bank]],
                            f"{tag}te{oc}_{ig}_{ic}")
                    dma("qs", tT_d[ig * D:(ig + 1) * D,
                                   oc * P:(oc + 1) * P].rearrange(
                            "(a p) b -> p a b", p=P),
                        q_o,
                        ["bB7"], [key], f"{tag}ts{oc}_{ig}")

        prep_tern(W1, t1T_d, "t1T_d", 4 * D, D, 0, "t1")
        prep_tern(W2, t2T_d, "t2T_d", D, 4 * D, 1, "t2")
        vop(lambda: VEC.reciprocal(s_w[:, 0:1], s_w[:, 0:1]),
            ["s_w"], ["s_w"], "isw1")
        vop(lambda: VEC.reciprocal(s_w[:, 1:2], s_w[:, 1:2]),
            ["s_w"], ["s_w"], "isw2")
        vop(lambda: VEC.tensor_scalar_mul(swr[:, :], s_w[:, 1:2],
                                          1.0 / 127.0),
            ["s_w"], ["swr"], "iswr")
        for k3 in range(3):
            vop(lambda k3=k3: VEC.tensor_scalar_mul(
                    convw_c[:, k3:96:3], convw_c[:, k3:96:3], s_w[:, 0:1]),
                ["convw_c", "s_w"], ["convw_c"], f"cwf{k3}")

        def load_wres(dst, keys, src_d, src_key, col_lo, ncols, tag):
            for c in range(NCH):
                dma("qa", dst[c][:, 0:ncols],
                    src_d[c * P:(c + 1) * P, col_lo:col_lo + ncols],
                    [src_key], [keys[c]], f"{tag}{c}")

        def rep_tiles(src_col_fn, idxs, bank, dst, dst_keys, reads_extra,
                      tag):
            n = len(idxs)
            for j, tau in enumerate(idxs):
                pg.op("pe", lambda j=j, tau=tau, bank=bank: TEN.matmul(
                        pb[bank][:, j * P:(j + 1) * P],
                        src_col_fn(tau).broadcast_to((P, P)),
                        ident[:, :], start=(j == 0), stop=(j == n - 1),
                        skip_group_check=True),
                      reads=list(reads_extra) + ["ident"],
                      writes=[PB[bank]], tag=f"{tag}m{j}")
            vop(lambda n=n, bank=bank, dst=dst: VEC.tensor_copy(
                    dst[:, 0:n * P], pb[bank][:, 0:n * P]),
                [PB[bank]], list(dst_keys) + [PB[bank]], f"{tag}e")

        def ones_rep(row_ap, n, bank, dst, dst_keys, tag):
            pg.op("pe", lambda row_ap=row_ap, n=n, bank=bank: TEN.matmul(
                    pb[bank][:, 0:n], onesT[0:1, 0:P], row_ap,
                    start=True, stop=True),
                  reads=["onesT", "nm_row", "rows"], writes=[PB[bank]],
                  tag=f"{tag}m")
            vop(lambda n=n, bank=bank, dst=dst: VEC.tensor_copy(
                    dst[:, 0:n], pb[bank][:, 0:n]),
                [PB[bank]], list(dst_keys) + [PB[bank]], f"{tag}e")

        def make_bbc(dst, dst_key, rowname, tag):
            for h in range(2):
                pg.op("pe", lambda h=h, rowname=rowname: TEN.matmul(
                        pb[6 + h][:, :], ones_at(rowname),
                        row(rowname, h * 512, 512), start=True, stop=True),
                      reads=["onesT", "rows"], writes=[PB[6 + h]],
                      tag=f"{tag}{h}")
                vop(lambda h=h, dst=dst: VEC.tensor_copy(
                        dst[:, h * 512:(h + 1) * 512], pb[6 + h][:, :]),
                    [PB[6 + h]], [dst_key, PB[6 + h]], f"{tag}e{h}")

        def emit_ln(tag):
            sq = [bigB[:, 0:2048], bigB[:, 2048:4096]]
            sqk = [["bB0", "bB1"], ["bB2", "bB3"]]
            for c in range(NCH):
                b_ = c % 2
                pg.op("sc", lambda c=c, b_=b_: SCA.activation(
                        sq[b_][:, 0:T], xc(c), AF.Square),
                      reads=[XK[c]], writes=sqk[b_], tag=f"{tag}sq{c}")
                for t5 in range(NT5):
                    pg.op("pe", lambda c=c, t5=t5: TEN.matmul(
                            pb[t5][:, :], onesT[:, :],
                            xc(c)[:, t5 * 512:(t5 + 1) * 512],
                            start=(c == 0), stop=(c == NCH - 1)),
                          reads=[XK[c], "onesT"], writes=[PB[t5]],
                          tag=f"{tag}s{c}_{t5}")
                    pg.op("pe", lambda c=c, t5=t5, b_=b_: TEN.matmul(
                            pb[4 + t5][:, :], onesT[:, :],
                            sq[b_][:, t5 * 512:(t5 + 1) * 512],
                            start=(c == 0), stop=(c == NCH - 1)),
                          reads=sqk[b_] + ["onesT"], writes=[PB[4 + t5]],
                          tag=f"{tag}ss{c}_{t5}")
            for t5 in range(NT5):
                sl = slice(t5 * 512, (t5 + 1) * 512)
                for rb in (0, 32, 64):
                    vop(lambda t5=t5, sl=sl, rb=rb: VEC.tensor_single_scalar(
                            nm_row[rb:rb + 1, sl], pb[t5][rb:rb + 1, :],
                            -1.0 / D, OP.mult),
                        [PB[t5]], ["nm_row"], f"{tag}nm{t5}_{rb}")
                vop(lambda t5=t5: VEC.tensor_scalar_mul(
                        qp(0)[:, 0:512], pb[t5][:, :], 1.0 / D),
                    [PB[t5]], ["qp0", PB[t5]], f"{tag}m{t5}")
                vop(lambda: VEC.tensor_mul(qp(0)[:, 0:512], qp(0)[:, 0:512],
                                           qp(0)[:, 0:512]),
                    ["qp0"], ["qp0"], f"{tag}m2{t5}")
                vop(lambda t5=t5: VEC.scalar_tensor_tensor(
                        qp(0)[:, 0:512], pb[4 + t5][:, :], 1.0 / D,
                        qp(0)[:, 0:512], OP.mult, OP.subtract),
                    [PB[4 + t5], "qp0"], ["qp0", PB[4 + t5]], f"{tag}v{t5}")
                for j in range(4):
                    for jj in range(4):
                        vop(lambda j=j, jj=jj: VEC.transpose(
                                scr128[jj * 32:(jj + 1) * 32, 0:32],
                                qp(0)[0:32, j * P + jj * 32:
                                      j * P + (jj + 1) * 32]),
                            ["qp0"], ["scr128"], f"{tag}vt{t5}_{j}_{jj}")
                    vop(lambda t5=t5, j=j: VEC.tensor_copy(
                            var_tm[:, t5 * 4 + j:t5 * 4 + j + 1],
                            scr128[:, 0:1]),
                        ["scr128"], ["var_tm"], f"{tag}vc{t5}_{j}")
            vop(lambda: VEC.tensor_scalar_add(var_tm[:, :], var_tm[:, :],
                                              1e-5),
                ["var_tm"], ["var_tm"], f"{tag}ve")
            vop(lambda: VEC.reciprocal(var_tm[:, :], var_tm[:, :]),
                ["var_tm"], ["var_tm"], f"{tag}vr")
            pg.op("sc", lambda: SCA.activation(rstd_tm[:, :], var_tm[:, :],
                                               AF.Sqrt),
                  reads=["var_tm"], writes=["rstd_tm"], tag=f"{tag}vs")

        def emit_passA(s, is_l1):
            ktile = bC(0)[:, 0:1024]
            vt = bC(1)[:, 0:1024]
            vkeys = ["bC1"]
            wk_key, wv_key = ("k1", "v1") if is_l1 else ("wk", "wv")
            vop(lambda: VEC.memset(kv_acc[:, :], 0.0), [], ["kv_acc"],
                f"A{s}z")
            for t1 in range(NT1):
                tsl = slice(t1 * P, (t1 + 1) * P)
                for half in range(2):
                    hsl = slice(half * 512, (half + 1) * 512)
                    for c in range(NCH):
                        pg.op("pe", lambda c=c, tsl=tsl, hsl=hsl, half=half:
                              TEN.matmul(pb[half][:, :], xc(c)[:, tsl],
                                         wres[c][:, hsl],
                                         start=(c == 0), stop=False),
                              reads=[XK[c], WK1[c]], writes=[PB[half]],
                              tag=f"A{s}k{t1}_{half}_{c}")
                    pg.op("pe", lambda tsl=tsl, half=half, wk_key=wk_key:
                          TEN.matmul(
                            pb[half][:, :],
                            nm_row[rowbase(wk_key):rowbase(wk_key) + 1, tsl],
                            row(wk_key, half * 512, 512),
                            start=False, stop=True),
                          reads=["nm_row", "rows"], writes=[PB[half]],
                          tag=f"A{s}km{t1}_{half}")
                    for c in range(NCH):
                        pg.op("pe", lambda c=c, tsl=tsl, hsl=hsl, half=half:
                              TEN.matmul(pb[2 + half][:, :], xc(c)[:, tsl],
                                         wres2[c][:, hsl],
                                         start=(c == 0), stop=False),
                              reads=[XK[c], WK2[c]], writes=[PB[2 + half]],
                              tag=f"A{s}v{t1}_{half}_{c}")
                    pg.op("pe", lambda tsl=tsl, half=half, wv_key=wv_key:
                          TEN.matmul(
                            pb[2 + half][:, :],
                            nm_row[rowbase(wv_key):rowbase(wv_key) + 1, tsl],
                            row(wv_key, half * 512, 512),
                            start=False, stop=True),
                          reads=["nm_row", "rows"], writes=[PB[2 + half]],
                          tag=f"A{s}vm{t1}_{half}")
                for half in range(2):
                    hsl = slice(half * 512, (half + 1) * 512)
                    if is_l1:
                        vop(lambda t1=t1, half=half: VEC.scalar_tensor_tensor(
                                qp(1)[:, 0:512], pb[half][:, :],
                                rstd_tm[:, t1:t1 + 1],
                                bkbc[:, half * 512:(half + 1) * 512],
                                OP.mult, OP.add),
                            [PB[half], "rstd_tm", "bC3"],
                            ["qp1", PB[half]], f"A{s}ke{t1}_{half}")
                        pg.op("sc", lambda half=half, hsl=hsl, ktile=ktile:
                              SCA.activation(ktile[:, hsl], qp(1)[:, 0:512],
                                             AF.Exp),
                              reads=["qp1"], writes=["bC0"],
                              tag=f"A{s}kx{t1}_{half}")
                    else:
                        vop(lambda t1=t1, half=half, hsl=hsl, ktile=ktile:
                            VEC.scalar_tensor_tensor(
                                ktile[:, hsl], pb[half][:, :],
                                rstd_tm[:, t1:t1 + 1],
                                bkbc[:, half * 512:(half + 1) * 512],
                                OP.mult, OP.add),
                            [PB[half], "rstd_tm", "bC3"],
                            ["bC0", PB[half]], f"A{s}ke{t1}_{half}")
                        vop(lambda hsl=hsl, ktile=ktile:
                            VEC.tensor_scalar_min(
                                qp(1)[:, 0:512], ktile[:, hsl], 0.0),
                            ["bC0"], ["qp1"], f"A{s}kmn{t1}_{half}")
                        pg.op("sc", lambda: SCA.activation(
                                qp(1)[:, 0:512], qp(1)[:, 0:512], AF.Exp),
                              reads=["qp1"], writes=["qp1"],
                              tag=f"A{s}kx{t1}_{half}")
                        vop(lambda hsl=hsl, ktile=ktile:
                            VEC.scalar_tensor_tensor(
                                ktile[:, hsl], ktile[:, hsl], 0.0,
                                qp(1)[:, 0:512], OP.max, OP.add),
                            ["bC0", "qp1"], ["bC0"], f"A{s}kp{t1}_{half}")
                    vop(lambda t1=t1, half=half, hsl=hsl, vt=vt:
                        VEC.scalar_tensor_tensor(
                            vt[:, hsl], pb[2 + half][:, :],
                            rstd_tm[:, t1:t1 + 1],
                            bvbc[:, half * 512:(half + 1) * 512],
                            OP.mult, OP.add),
                        [PB[2 + half], "rstd_tm", "bC4"],
                        vkeys + [PB[2 + half]], f"A{s}ve{t1}_{half}")
                for h in range(16):
                    p_ = h // 2
                    bank = 4 + p_ // 4
                    out_ap = pb[bank][64 * (h % 2):64 * (h % 2) + 64,
                                      (p_ % 4) * 64:(p_ % 4) * 64 + 64]
                    rhs = vt[:, h * 64:(h + 1) * 64]
                    pg.op("pe", lambda h=h, out_ap=out_ap, rhs=rhs,
                          ktile=ktile: TEN.matmul(
                            out_ap, ktile[:, h * 64:(h + 1) * 64],
                            rhs, start=(h % 8 <= 1), stop=(h % 8 >= 6),
                            skip_group_check=True),
                          reads=["bC0"] + vkeys, writes=[PB[4 + p_ // 4]],
                          tag=f"A{s}kv{t1}_{h}")
                for p_ in range(8):
                    pg.op("pe", lambda p_=p_, t1=t1, ktile=ktile:
                          TEN.matmul(
                            pb[6][:, p_:p_ + 1],
                            ktile[:, p_ * P:(p_ + 1) * P],
                            onesT[:, 0:1],
                            start=(t1 == 0 and p_ == 0),
                            stop=(t1 == NT1 - 1 and p_ == 7),
                            skip_group_check=True),
                          reads=["bC0", "onesT"], writes=[PB[6]],
                          tag=f"A{s}dn{t1}_{p_}")
                for b_ in range(2):
                    vop(lambda b_=b_: VEC.tensor_add(
                            kv_acc[:, b_ * 256:(b_ + 1) * 256],
                            kv_acc[:, b_ * 256:(b_ + 1) * 256],
                            pb[4 + b_][:, 0:256]),
                        [PB[4 + b_], "kv_acc"], ["kv_acc", PB[4 + b_]],
                        f"A{s}a{t1}_{b_}")
            vop(lambda: VEC.tensor_copy(den_sb[:, :], pb[6][:, 0:8]),
                [PB[6]], ["den_sb", PB[6]], f"A{s}de")
            if is_l1:
                vop(lambda: VEC.reciprocal(den_sb[:, :], den_sb[:, :]),
                    ["den_sb"], ["den_sb"], f"A{s}dr")
                for p_ in range(8):
                    vop(lambda p_=p_: VEC.tensor_scalar_mul(
                            kv_acc[:, p_ * 64:(p_ + 1) * 64],
                            kv_acc[:, p_ * 64:(p_ + 1) * 64],
                            den_sb[:, p_:p_ + 1]),
                        ["kv_acc", "den_sb"], ["kv_acc"], f"A{s}sc{p_}")

        def emit_load_x(s):
            for t1 in range(NT1):
                nb = t1 % 2
                land = qp(2 + nb)
                dma("qs", land[:, 0:D], x_in[s, t1 * P:(t1 + 1) * P, :],
                    [], [KQ[2 + nb]], f"lx{s}_{t1}")
                for c in range(NCH):
                    bank = c % 4
                    pg.op("pe", lambda land=land, c=c, bank=bank:
                          TEN.transpose(pb[bank][:, 0:P],
                                        land[:, c * P:(c + 1) * P],
                                        ident[:, :]),
                          reads=[KQ[2 + nb], "ident"], writes=[PB[bank]],
                          tag=f"lxT{s}_{t1}_{c}")
                    eng = "sc" if c % 2 == 0 else "dve"
                    fn = SCA.copy if eng == "sc" else VEC.tensor_copy
                    vop(lambda fn=fn, c=c, t1=t1, bank=bank: fn(
                            xc(c)[:, t1 * P:(t1 + 1) * P], pb[bank][:, 0:P]),
                        [PB[bank]], [XK[c], PB[bank]], f"lxE{s}_{t1}_{c}",
                        eng=eng)

        def emit_store_x(s):
            for t1 in range(NT1):
                nb = t1 % 2
                land = qp(2 + nb)
                for c in range(NCH):
                    bank = 4 + c % 4
                    pg.op("pe", lambda c=c, t1=t1, bank=bank: TEN.transpose(
                            pb[bank][:, 0:P], xc(c)[:, t1 * P:(t1 + 1) * P],
                            ident[:, :]),
                          reads=[XK[c], "ident"], writes=[PB[bank]],
                          tag=f"sx{s}_{t1}_{c}")
                    eng = "sc" if c % 2 == 0 else "dve"
                    fn = SCA.copy if eng == "sc" else VEC.tensor_copy
                    vop(lambda fn=fn, land=land, c=c, bank=bank: fn(
                            land[:, c * P:(c + 1) * P], pb[bank][:, 0:P]),
                        [PB[bank]], [KQ[2 + nb], PB[bank]],
                        f"sxE{s}_{t1}_{c}", eng=eng)
                dma("qs", y_out[s, t1 * P:(t1 + 1) * P, :], land[:, 0:D],
                    [KQ[2 + nb]], [f"y{s}"], f"sy{s}_{t1}")

        def rtile(t5, dst, dst_keys, tag):
            rep_tiles(lambda tau: rstd_tm[:, tau:tau + 1],
                      [4 * t5 + j for j in range(4)], 7, dst, dst_keys,
                      ["rstd_tm"], tag)

        def phv(c):
            return bC(4 + c // 2)[:, (c % 2) * 512:(c % 2) * 512 + 512]

        def phk(c):
            return KC[4 + c // 2]

        def emit_l0_attn(s):
            emit_ln(f"n0_{s}")
            make_bbc(bkbc, "bC3", "bk", f"bb0{s}")
            make_bbc(bvbc, "bC4", "bv", f"bv0{s}")
            load_wres(wres, WK1, wkT_d, "wkT_d", 0, D, f"wk{s}_")
            load_wres(wres2, WK2, wvT_d, "wvT_d", 0, D, f"wv{s}_")
            emit_passA(s, is_l1=False)
            vop(lambda: VEC.memset(norm_lhs[:, :], 0.0), [], ["norm_lhs"],
                f"B{s}nz")
            for c in range(NCH):
                vop(lambda c=c: VEC.tensor_copy(
                        norm_lhs[0:64, c * 16 + 2 * c:c * 16 + 2 * c + 1],
                        den_sb[0:64, c:c + 1]),
                    ["den_sb"], ["norm_lhs"], f"B{s}n0{c}")
                vop(lambda c=c: VEC.tensor_copy(
                        norm_lhs[64:128,
                                 c * 16 + 2 * c + 1:c * 16 + 2 * c + 2],
                        den_sb[64:128, c:c + 1]),
                    ["den_sb"], ["norm_lhs"], f"B{s}n1{c}")
            load_wres(wres, WK1, wqT_d, "wqT_d", 0, D, f"wq{s}_")
            load_wres(wres2, WK2, woT_d, "woT_d", 0, D, f"wo{s}_")
            for t5 in range(NT5):
                sl = slice(t5 * 512, (t5 + 1) * 512)
                rtile(t5, qp(0)[:, 0:512], ["qp0"], f"B{s}r{t5}")
                for oc in range(NCH):
                    bank = oc
                    for c in range(NCH):
                        pg.op("pe", lambda oc=oc, c=c, sl=sl, bank=bank:
                              TEN.matmul(pb[bank][:, :],
                                         wres[c][:, oc * P:(oc + 1) * P],
                                         xc(c)[:, sl],
                                         start=(c == 0), stop=False),
                              reads=[WK1[c], XK[c]], writes=[PB[bank]],
                              tag=f"B{s}q{t5}_{oc}_{c}")
                    pg.op("pe", lambda oc=oc, sl=sl, bank=bank: TEN.matmul(
                            pb[bank][:, :], row("wq", oc * P, P),
                            nm_row[rowbase("wq"):rowbase("wq") + 1,
                                   sl],
                            start=False, stop=True),
                          reads=["rows", "nm_row"], writes=[PB[bank]],
                          tag=f"B{s}qm{t5}_{oc}")
                for oc in range(NCH):
                    bank = oc
                    vop(lambda bank=bank: VEC.tensor_mul(
                            qp(1)[:, 0:512], pb[bank][:, :],
                            qp(0)[:, 0:512]),
                        [PB[bank], "qp0"], ["qp1", PB[bank]],
                        f"B{s}e0{t5}_{oc}")
                    vop(lambda oc=oc: VEC.tensor_scalar(
                            qp(2)[:, 0:512], qp(1)[:, 0:512],
                            bq_c[:, oc:oc + 1], 0.0, OP.add, OP.min),
                        ["qp1", "bq_c"], ["qp2"], f"B{s}e1{t5}_{oc}")
                    pg.op("sc", lambda: SCA.activation(
                            qp(2)[:, 0:512], qp(2)[:, 0:512], AF.Exp),
                          reads=["qp2"], writes=["qp2"],
                          tag=f"B{s}e2{t5}_{oc}")
                    pg.op("sc", lambda oc=oc: SCA.activation(
                            qp(1)[:, 0:512], qp(1)[:, 0:512], AF.Identity,
                            bias=bq_c[:, oc:oc + 1], scale=1.0),
                          reads=["qp1", "bq_c"], writes=["qp1"],
                          tag=f"B{s}e3{t5}_{oc}")
                    vop(lambda oc=oc: VEC.scalar_tensor_tensor(
                            phv(oc), qp(1)[:, 0:512], 0.0, qp(2)[:, 0:512],
                            OP.max, OP.add),
                        ["qp1", "qp2"], [phk(oc)], f"B{s}e4{t5}_{oc}")
                for c in range(NCH):
                    pg.op("pe", lambda c=c: TEN.matmul(
                            pb[0][0:16, :], norm_lhs[:, c * 16:(c + 1) * 16],
                            phv(c), start=(c == 0), stop=(c == NCH - 1)),
                          reads=["norm_lhs", phk(c)],
                          writes=["pb0"], tag=f"B{s}nm{t5}_{c}")
                vop(lambda: VEC.tensor_single_scalar(
                        norm16t, pb[0][0:16, :], 1e-6, OP.max),
                    ["pb0"], ["norm16t", "pb0"], f"B{s}nx{t5}")
                vop(lambda: VEC.reciprocal(norm16t, norm16t),
                    ["norm16t"], ["norm16t"], f"B{s}nr{t5}")
                for c in range(NCH):
                    vop(lambda: VEC.memset(kvblk[:, :], 0.0), [], ["kvblk"],
                        f"B{s}bz{t5}_{c}")
                    vop(lambda c=c: VEC.tensor_copy(
                            kvblk[0:64, 0:64],
                            kv_acc[0:64, c * 64:c * 64 + 64]),
                        ["kv_acc"], ["kvblk"], f"B{s}b0{t5}_{c}")
                    vop(lambda c=c: VEC.tensor_copy(
                            kvblk[64:128, 64:128],
                            kv_acc[64:128, c * 64:c * 64 + 64]),
                        ["kv_acc"], ["kvblk"], f"B{s}b1{t5}_{c}")
                    pg.op("pe", lambda c=c: TEN.matmul(
                            pb[1][:, :], kvblk[:, :], phv(c),
                            start=True, stop=True),
                          reads=["kvblk", phk(c)], writes=["pb1"],
                          tag=f"B{s}o{t5}_{c}")
                    pg.op("pe", lambda c=c: TEN.matmul(
                            pb[2][:, :], sel16[:, c * P:(c + 1) * P],
                            norm16t, start=True, stop=True),
                          reads=["sel16", "norm16t"], writes=["pb2"],
                          tag=f"B{s}p{t5}_{c}")
                    pg.op("sc", lambda: SCA.copy(qp(1)[:, 0:512],
                                                 pb[2][:, :]),
                          reads=["pb2"], writes=["qp1", "pb2"],
                          tag=f"B{s}pe{t5}_{c}")
                    vop(lambda c=c: VEC.tensor_mul(
                            phv(c), pb[1][:, :], qp(1)[:, 0:512]),
                        ["pb1", "qp1"], [phk(c), "pb1"],
                        f"B{s}d{t5}_{c}")
                for oc in range(NCH):
                    bank = 3 + oc % 5
                    for c in range(NCH):
                        pg.op("pe", lambda oc=oc, c=c, bank=bank:
                              TEN.matmul(pb[bank][:, :],
                                         wres2[c][:, oc * P:(oc + 1) * P],
                                         phv(c), start=(c == 0),
                                         stop=(c == NCH - 1)),
                              reads=[WK2[c], phk(c)],
                              writes=[PB[bank]], tag=f"B{s}w{t5}_{oc}_{c}")
                    vop(lambda oc=oc, sl=sl, bank=bank:
                        VEC.scalar_tensor_tensor(
                            xc(oc)[:, sl], pb[bank][:, :],
                            bo_c[:, oc:oc + 1], xc(oc)[:, sl],
                            OP.add, OP.add),
                        [PB[bank], "bo_c", XK[oc]],
                        [XK[oc], PB[bank]], f"B{s}x{t5}_{oc}")

        def emit_l0_ffn(s):
            emit_ln(f"n1_{s}")
            load_wres(wres, WK1, weffT_d, "weffT_d", 0, D, f"wf{s}_")
            for t5 in range(NT5):
                sl = slice(t5 * 512, (t5 + 1) * 512)
                rtile(t5, qp(0)[:, 0:512], ["qp0"], f"F{s}r{t5}")
                for oc in range(NCH):
                    bank = oc % 6
                    for c in range(NCH):
                        pg.op("pe", lambda oc=oc, c=c, sl=sl, bank=bank:
                              TEN.matmul(pb[bank][:, :],
                                         wres[c][:, oc * P:(oc + 1) * P],
                                         xc(c)[:, sl],
                                         start=(c == 0), stop=False),
                              reads=[WK1[c], XK[c]], writes=[PB[bank]],
                              tag=f"F{s}m{t5}_{oc}_{c}")
                    pg.op("pe", lambda oc=oc, sl=sl, bank=bank: TEN.matmul(
                            pb[bank][:, :], row("weff", oc * P, P),
                            nm_row[rowbase("weff"):rowbase("weff") + 1,
                                   sl],
                            start=False, stop=True),
                          reads=["rows", "nm_row"], writes=[PB[bank]],
                          tag=f"F{s}t{t5}_{oc}")
                    vop(lambda oc=oc, bank=bank: VEC.tensor_mul(
                            phv(oc), pb[bank][:, :], qp(0)[:, 0:512]),
                        [PB[bank], "qp0"], [phk(oc), PB[bank]],
                        f"F{s}e{t5}_{oc}")
                for oc in range(NCH):
                    vop(lambda oc=oc, sl=sl: VEC.scalar_tensor_tensor(
                            xc(oc)[:, sl], phv(oc),
                            bf_c[:, oc:oc + 1], xc(oc)[:, sl],
                            OP.add, OP.add),
                        [phk(oc), "bf_c", XK[oc]], [XK[oc]],
                        f"F{s}x{t5}_{oc}")

        def emit_l1_attn(s):
            emit_ln(f"n2_{s}")
            make_bbc(bkbc, "bC3", "bk1", f"bb1{s}")
            make_bbc(bvbc, "bC4", "bv1", f"bv1{s}")
            load_wres(wres, WK1, qkvT_d, "qkvT_d", D, D, f"jk{s}_")
            load_wres(wres2, WK2, qkvT_d, "qkvT_d", 2 * D, D, f"jv{s}_")
            emit_passA(s, is_l1=True)
            l1tap = "l1" in debug_taps and s == 0
            if l1tap:
                dbg_kv = nc.dram_tensor("dbg_kv", [P, 1024], F32,
                                        kind="ExternalOutput").ap()
                dbg_den = nc.dram_tensor("dbg_den", [P, 8], F32,
                                         kind="ExternalOutput").ap()
                dbg_q = nc.dram_tensor("dbg_q", [P, 4096], F32,
                                       kind="ExternalOutput").ap()
                dbg_o = nc.dram_tensor("dbg_o", [P, 4096], F32,
                                       kind="ExternalOutput").ap()
                dbg_n = nc.dram_tensor("dbg_n", [P, 4096], F32,
                                       kind="ExternalOutput").ap()
                dbg_gh = nc.dram_tensor("dbg_gh", [64, 512], F32,
                                        kind="ExternalOutput").ap()
                dbg_gate = nc.dram_tensor("dbg_gate", [P, 512], F32,
                                          kind="ExternalOutput").ap()
                dma("qs", dbg_kv, kv_acc[:, :], ["kv_acc"], ["dKV"], "dKV")
                dma("qs", dbg_den, den_sb[:, :], ["den_sb"], ["dDE"], "dDE")
            load_wres(wres, WK1, qkvT_d, "qkvT_d", 0, D, f"jq{s}_")
            load_wres(wres2, WK2, woutT_d, "woutT_d", 0, D, f"ju{s}_")
            for c in range(NCH):
                dma("qa", bC(0)[:, c * 64:(c + 1) * 64],
                    g1T_d[c * P:(c + 1) * P, :], ["g1T_d"], ["bC0"],
                    f"G1{s}_{c}")
            dma("qa", bC(1)[64:128, 0:D], g2T_d, ["g2T_d"], ["bC1"],
                f"G2{s}")
            for t5 in range(NT5):
                sl = slice(t5 * 512, (t5 + 1) * 512)
                rtile(t5, qp(0)[:, 0:512], ["qp0"], f"C{s}r{t5}")
                for oc in range(NCH):
                    bank = oc
                    for c in range(NCH):
                        pg.op("pe", lambda oc=oc, c=c, sl=sl, bank=bank:
                              TEN.matmul(pb[bank][:, :],
                                         wres[c][:, oc * P:(oc + 1) * P],
                                         xc(c)[:, sl],
                                         start=(c == 0), stop=False),
                              reads=[WK1[c], XK[c]], writes=[PB[bank]],
                              tag=f"C{s}q{t5}_{oc}_{c}")
                    pg.op("pe", lambda oc=oc, sl=sl, bank=bank: TEN.matmul(
                            pb[bank][:, :], row("q1", oc * P, P),
                            nm_row[rowbase("q1"):rowbase("q1") + 1,
                                   sl],
                            start=False, stop=True),
                          reads=["rows", "nm_row"], writes=[PB[bank]],
                          tag=f"C{s}qm{t5}_{oc}")
                for oc in range(NCH):
                    bank = oc
                    vop(lambda bank=bank: VEC.tensor_mul(
                            qp(1)[:, 0:512], pb[bank][:, :],
                            qp(0)[:, 0:512]),
                        [PB[bank], "qp0"], ["qp1", PB[bank]],
                        f"C{s}e0{t5}_{oc}")
                    pg.op("sc", lambda oc=oc: SCA.activation(
                            phv(oc), qp(1)[:, 0:512], AF.Silu,
                            bias=bq1_c[:, oc:oc + 1], scale=1.0),
                          reads=["qp1", "bq1_c"], writes=[phk(oc)],
                          tag=f"C{s}e1{t5}_{oc}")
                if l1tap and t5 == 0:
                    dma("qs", dbg_q, bigC[:, 4096:8192],
                        [phk(c_) for c_ in range(NCH)], ["dQ"], "dQ")
                for c in range(NCH):
                    vop(lambda: VEC.memset(kvblk[:, :], 0.0), [], ["kvblk"],
                        f"C{s}bz{t5}_{c}")
                    vop(lambda c=c: VEC.tensor_copy(
                            kvblk[0:64, 0:64],
                            kv_acc[0:64, c * 64:c * 64 + 64]),
                        ["kv_acc"], ["kvblk"], f"C{s}b0{t5}_{c}")
                    vop(lambda c=c: VEC.tensor_copy(
                            kvblk[64:128, 64:128],
                            kv_acc[64:128, c * 64:c * 64 + 64]),
                        ["kv_acc"], ["kvblk"], f"C{s}b1{t5}_{c}")
                    pg.op("pe", lambda c=c: TEN.matmul(
                            pb[0][:, :], kvblk[:, :], phv(c),
                            start=True, stop=True),
                          reads=["kvblk", phk(c)], writes=["pb0"],
                          tag=f"C{s}o{t5}_{c}")
                    vop(lambda c=c: VEC.tensor_copy(phv(c), pb[0][:, :]),
                        ["pb0"], [phk(c), "pb0"], f"C{s}oe{t5}_{c}")
                if l1tap and t5 == 0:
                    dma("qs", dbg_o, bigC[:, 4096:8192],
                        [phk(c_) for c_ in range(NCH)], ["dO"], "dO")
                for c in range(NCH):
                    pg.op("sc", lambda c=c: SCA.activation(
                            qp(1)[:, 0:512], phv(c), AF.Square),
                          reads=[phk(c)], writes=["qp1"],
                          tag=f"C{s}s{t5}_{c}")
                    pg.op("pe", lambda c=c: TEN.matmul(
                            pb[2][:, :], onesT[:, :], phv(c),
                            start=(c == 0), stop=(c == NCH - 1)),
                          reads=[phk(c), "onesT"], writes=["pb2"],
                          tag=f"C{s}sm{t5}_{c}")
                    pg.op("pe", lambda c=c: TEN.matmul(
                            pb[3][:, :], onesT[:, :], qp(1)[:, 0:512],
                            start=(c == 0), stop=(c == NCH - 1)),
                          reads=["qp1", "onesT"], writes=["pb3"],
                          tag=f"C{s}sq{t5}_{c}")
                vop(lambda: VEC.tensor_scalar_mul(qp(2)[:, 0:512],
                                                  pb[2][:, :], 1.0 / D),
                    ["pb2"], ["qp2", "pb2"], f"C{s}lm{t5}")
                vop(lambda: VEC.tensor_mul(qp(1)[:, 0:512], qp(2)[:, 0:512],
                                           qp(2)[:, 0:512]),
                    ["qp2"], ["qp1"], f"C{s}l2{t5}")
                vop(lambda: VEC.scalar_tensor_tensor(
                        qp(1)[:, 0:512], pb[3][:, :], 1.0 / D,
                        qp(1)[:, 0:512], OP.mult, OP.subtract),
                    ["pb3", "qp1"], ["qp1", "pb3"], f"C{s}lv{t5}")
                vop(lambda: VEC.tensor_scalar_add(qp(1)[:, 0:512],
                                                  qp(1)[:, 0:512], 1e-5),
                    ["qp1"], ["qp1"], f"C{s}le{t5}")
                vop(lambda: VEC.reciprocal(qp(1)[:, 0:512], qp(1)[:, 0:512]),
                    ["qp1"], ["qp1"], f"C{s}lr{t5}")
                pg.op("sc", lambda: SCA.activation(
                        qp(1)[:, 0:512], qp(1)[:, 0:512], AF.Sqrt),
                      reads=["qp1"], writes=["qp1"], tag=f"C{s}ls{t5}")
                vop(lambda: VEC.scalar_tensor_tensor(
                        qp(2)[:, 0:512], qp(2)[:, 0:512], -1.0,
                        qp(1)[:, 0:512], OP.mult, OP.mult),
                    ["qp2", "qp1"], ["qp2"], f"C{s}ln{t5}")
                for c in range(NCH):
                    vop(lambda c=c: GPS.tensor_mul(
                            qp(3)[:, 0:512], phv(c), qp(1)[:, 0:512]),
                        [phk(c), "qp1"], ["qp3"], f"C{s}a0{t5}_{c}",
                        eng="gp")
                    vop(lambda: GPS.tensor_add(
                            qp(3)[:, 0:512], qp(3)[:, 0:512],
                            qp(2)[:, 0:512]),
                        ["qp3", "qp2"], ["qp3"], f"C{s}a1{t5}_{c}",
                        eng="gp")
                    vop(lambda c=c: VEC.tensor_scalar(
                            phv(c), qp(3)[:, 0:512], lng_c[:, c:c + 1],
                            lnb_c[:, c:c + 1], OP.mult, OP.add),
                        ["qp3", "lng_c", "lnb_c"], [phk(c)],
                        f"C{s}a2{t5}_{c}")
                if l1tap and t5 == 0:
                    dma("qs", dbg_n, bigC[:, 4096:8192],
                        [phk(c_) for c_ in range(NCH)], ["dN"], "dN")
                for c in range(NCH):
                    pg.op("pe", lambda c=c, sl=sl: TEN.matmul(
                            pb[4][0:64, :], bC(0)[:, c * 64:(c + 1) * 64],
                            xc(c)[:, sl], start=(c == 0), stop=False),
                          reads=["bC0", XK[c]], writes=["pb4"],
                          tag=f"C{s}g{t5}_{c}")
                pg.op("pe", lambda sl=sl: TEN.matmul(
                        pb[4][0:64, :], row("g1", 0, 64),
                        nm_row[rowbase("g1"):rowbase("g1") + 1, sl],
                        start=False, stop=True),
                      reads=["rows", "nm_row"], writes=["pb4"],
                      tag=f"C{s}gm{t5}")
                vop(lambda: VEC.tensor_mul(ghv, pb[4][0:64, :],
                                           qp(0)[0:64, 0:512]),
                    ["pb4", "qp0"], ["gh", "pb4"], f"C{s}ge{t5}")
                vop(lambda: VEC.tensor_scalar_add(ghv, ghv, bg1_c[:, 0:1]),
                    ["gh", "bg1_c"], ["gh"], f"C{s}gb{t5}")
                if l1tap and t5 == 0:
                    dma("qs", dbg_gh, ghv, ["gh"], ["dGH"], "dGH")
                for oc in range(NCH):
                    pg.op("pe", lambda oc=oc: TEN.matmul(
                            pb[5][:, :], bC(1)[64:128, oc * P:(oc + 1) * P],
                            ghv, start=True, stop=True),
                          reads=["bC1", "gh"], writes=["pb5"],
                          tag=f"C{s}g2{t5}_{oc}")
                    pg.op("sc", lambda oc=oc: SCA.activation(
                            qp(3)[:, 0:512], pb[5][:, :], AF.Sigmoid,
                            bias=bg2_c[:, oc:oc + 1], scale=1.0),
                          reads=["pb5", "bg2_c"], writes=["qp3", "pb5"],
                          tag=f"C{s}g3{t5}_{oc}")
                    if l1tap and t5 == 0 and oc == 0:
                        dma("qs", dbg_gate, qp(3)[:, 0:512], ["qp3"],
                            ["dGA"], "dGA")
                    vop(lambda oc=oc: VEC.tensor_mul(
                            phv(oc), phv(oc), qp(3)[:, 0:512]),
                        [phk(oc), "qp3"], [phk(oc)], f"C{s}g4{t5}_{oc}")
                for oc in range(NCH):
                    bank = oc % 6
                    for c in range(NCH):
                        pg.op("pe", lambda oc=oc, c=c, bank=bank:
                              TEN.matmul(pb[bank][:, :],
                                         wres2[c][:, oc * P:(oc + 1) * P],
                                         phv(c), start=(c == 0),
                                         stop=(c == NCH - 1)),
                              reads=[WK2[c], phk(c)],
                              writes=[PB[bank]], tag=f"C{s}w{t5}_{oc}_{c}")
                    vop(lambda oc=oc, sl=sl, bank=bank:
                        VEC.scalar_tensor_tensor(
                            xc(oc)[:, sl], pb[bank][:, :],
                            bout_c[:, oc:oc + 1], xc(oc)[:, sl],
                            OP.add, OP.add),
                        [PB[bank], "bout_c", XK[oc]],
                        [XK[oc], PB[bank]], f"C{s}x{t5}_{oc}")

        def emit_l1_ffn(s):
            emit_ln(f"n3_{s}")
            qint = bigC[:, 0:4096].bitcast(BF16)[:, 0:NCH * 768]
            qint_c = [qint[:, c * 768:(c + 1) * 768] for c in range(NCH)]
            q2v = bigC[:, 0:8192].bitcast(BF16)[:, 0:32 * 512]
            for q in range(NQ):
                tl_lo = 4 * q
                tl_hi = min(NT1, 4 * q + 5)
                tls = list(range(tl_lo, tl_hi))
                ntl = len(tls)
                base = tl_lo * P
                if q == 0:
                    vop(lambda: VEC.memset(
                            qint.rearrange("p (c t) -> p c t",
                                           c=NCH)[:, :, 0:1], 0.0),
                        [], ["bC0", "bC1", "bC2"], f"D{s}qz{q}")
                    vop(lambda: VEC.memset(qp(3)[:, 0:1], 0.0),
                        [], ["qp3"], f"D{s}iz{q}")
                else:
                    vop(lambda: VEC.tensor_copy(
                            qint.rearrange("p (c t) -> p c t",
                                           c=NCH)[:, :, 0:1],
                            qcarry[:, 0:8]),
                        ["qcarry"], ["bC0", "bC1", "bC2"], f"D{s}qc{q}")
                    vop(lambda: VEC.tensor_copy(qp(3)[:, 0:1],
                                                icarry[:, :]),
                        ["icarry"], ["qp3"], f"D{s}ic{q}")
                for g4 in range(0, ntl, 4):
                    sub = tls[g4:g4 + 4]
                    rep_tiles(lambda tau: rstd_tm[:, tau:tau + 1], sub, 7,
                              qp(0)[:, g4 * P:g4 * P + len(sub) * P],
                              ["qp0"], ["rstd_tm"], f"D{s}r{q}_{g4}")
                for g4 in range(0, ntl, 4):
                    nsub = min(ntl, g4 + 4) - g4
                    ones_rep(nm_row[0:1, base + g4 * P:
                                    base + (g4 + nsub) * P], nsub * P, 7,
                             qp(1)[:, g4 * P:(g4 + nsub) * P], ["qp1"],
                             f"D{s}n{q}_{g4}")
                vop(lambda ntl=ntl: VEC.tensor_mul(
                        qp(1)[:, 0:ntl * P], qp(1)[:, 0:ntl * P],
                        qp(0)[:, 0:ntl * P]),
                    ["qp1", "qp0"], ["qp1"], f"D{s}nr{q}")
                for j in range(ntl):
                    tau = tls[j]
                    jk = f"bB{j // 2}"
                    rj = qp(0)[:, j * P:(j + 1) * P]
                    nj = qp(1)[:, j * P:(j + 1) * P]
                    tdst = bigB[:, j * 1024:(j + 1) * 1024].rearrange(
                        "p (c t) -> p c t", c=NCH)
                    vop(lambda tau=tau, rj=rj, tdst=tdst: VEC.tensor_tensor(
                            tdst, xstr(tau),
                            rj.unsqueeze(1).broadcast_to((P, NCH, P)),
                            OP.mult),
                        XK + ["qp0"], [jk], f"D{s}x0{q}_{j}")
                    vop(lambda nj=nj, tdst=tdst: VEC.tensor_tensor(
                            tdst, tdst,
                            nj.unsqueeze(1).broadcast_to((P, NCH, P)),
                            OP.add),
                        [jk, "qp1"], [jk], f"D{s}x1{q}_{j}")
                    tview = bigB[:, j * 1024:(j + 1) * 1024].rearrange(
                        "p (c t) -> p t c", t=P)
                    vop(lambda tview=tview, jk=jk: VEC.reduce_max(
                            scr128[:, 0:P], tview, AX,
                            apply_absolute_value=True),
                        [jk], ["scr128"], f"D{s}am{q}_{j}")
                    for jj in range(4):
                        for kk in range(4):
                            vop(lambda jj=jj, kk=kk: VEC.transpose(
                                    qp(4)[kk * 32:(kk + 1) * 32,
                                          jj * 32:(jj + 1) * 32],
                                    scr128[jj * 32:(jj + 1) * 32,
                                           kk * 32:(kk + 1) * 32]),
                                ["scr128"], ["qp4"], f"D{s}at{q}_{j}_{jj}{kk}")
                    vop(lambda: VEC.reduce_max(scr128[:, 1:2],
                                               qp(4)[:, 0:P], AX),
                        ["qp4"], ["scr128"], f"D{s}ar{q}_{j}")
                    vop(lambda j=j: VEC.tensor_copy(
                            amax_q[:, j:j + 1], scr128[:, 1:2]),
                        ["scr128"], ["amax_q"], f"D{s}ac{q}_{j}")
                vop(lambda ntl=ntl: VEC.tensor_single_scalar(
                        amax_q[:, 0:ntl], amax_q[:, 0:ntl], 1e-5, OP.max),
                    ["amax_q"], ["amax_q"], f"D{s}sc{q}")
                vop(lambda ntl=ntl: VEC.tensor_scalar_mul(
                        den_sb[:, 0:ntl], amax_q[:, 0:ntl], 1.0 / 127.0),
                    ["amax_q"], ["den_sb"], f"D{s}si{q}")
                vop(lambda ntl=ntl: VEC.reciprocal(
                        amax_q[:, 0:ntl], den_sb[:, 0:ntl]),
                    ["den_sb"], ["amax_q"], f"D{s}sr{q}")
                for g4 in range(0, ntl, 4):
                    sub = list(range(g4, min(ntl, g4 + 4)))
                    rep_tiles(lambda j: amax_q[:, j:j + 1], sub, 7,
                              qp(2)[:, 1 + g4 * P:1 + g4 * P + len(sub) * P],
                              ["qp2"], ["amax_q"], f"D{s}sb{q}_{g4}")
                    rep_tiles(lambda j: den_sb[:, j:j + 1], sub, 6,
                              qp(3)[:, 1 + g4 * P:1 + g4 * P + len(sub) * P],
                              ["qp3"], ["den_sb"], f"D{s}ib{q}_{g4}")
                if "amax1" in debug_taps and q == 0:
                    dbg1 = nc.dram_tensor("dbg_amax", [P, 4], F32,
                                          kind="ExternalOutput").ap()
                    dbg2 = nc.dram_tensor("dbg_s", [P, 512], F32,
                                          kind="ExternalOutput").ap()

                    dma("qs", dbg1, amax_q[:, 0:4], ["amax_q"], ["dbgA"],
                        "dbgA")
                    dma("qs", dbg2, qpool[:, 2 * 1024 + 1:2 * 1024 + 513],
                        ["qp2"], ["dbgS"], "dbgS")
                    dma("qs", dbg3, bigB[:, 0:8192], KB, ["dbgX"], "dbgX") if False else None
                for j in range(ntl):
                    jk = f"bB{j // 2}"
                    sj = qp(2)[:, 1 + j * P:1 + (j + 1) * P]
                    t_f = bigB[:, j * 1024:(j + 1) * 1024]
                    tdst = t_f.rearrange("p (c t) -> p c t", c=NCH)
                    t_i = bigB[:, 6144:6144 + NCH * P].bitcast(I32)[
                        :, 0:NCH * P]
                    t_d = bigB[:, 7168:7168 + NCH * P]
                    t_g = bigB[:, 8192:8192 + NCH * P]
                    vop(lambda sj=sj, tdst=tdst: VEC.tensor_tensor(
                            tdst, tdst,
                            sj.unsqueeze(1).broadcast_to((P, NCH, P)),
                            OP.mult),
                        [jk, "qp2"], [jk], f"D{s}c2{q}_{j}")
                    vop(lambda t_i=t_i, t_f=t_f: GPS.tensor_copy(t_i, t_f),
                        [jk], ["bB3"], f"D{s}c3{q}_{j}", eng="gp")
                    vop(lambda t_d=t_d, t_f=t_f, t_i=t_i: GPS.tensor_tensor(
                            t_d, t_f, t_i, OP.subtract),
                        [jk, "bB3"], ["bB3"], f"D{s}c4{q}_{j}", eng="gp")
                    vop(lambda t_g=t_g, t_d=t_d: VEC.tensor_single_scalar(
                            t_g, t_d, 0.5, OP.is_gt),
                        ["bB3"], ["bB4"], f"D{s}c5{q}_{j}")
                    vop(lambda t_d=t_d: VEC.tensor_single_scalar(
                            t_d, t_d, -0.5, OP.is_lt),
                        ["bB3"], ["bB3"], f"D{s}c6{q}_{j}")
                    vop(lambda t_f=t_f, t_g=t_g, t_i=t_i:
                        VEC.scalar_tensor_tensor(t_f, t_g, 1.0, t_i,
                                                 OP.mult, OP.add),
                        ["bB4", "bB3"], [jk], f"D{s}c7{q}_{j}")
                    qdst = qint.rearrange("p (c t) -> p c t",
                                          c=NCH)[:, :, 1 + j * P:
                                                 1 + (j + 1) * P]
                    vop(lambda qdst=qdst, t_f=t_f, t_d=t_d:
                        VEC.scalar_tensor_tensor(
                            qdst, t_d.rearrange("p (c t) -> p c t", c=NCH),
                            -1.0, t_f.rearrange("p (c t) -> p c t", c=NCH),
                            OP.mult, OP.add),
                        ["bB3", jk], ["bC0", "bC1", "bC2"],
                        f"D{s}c8{q}_{j}")
                if "qint" in debug_taps and q == 0:
                    dbg4 = nc.dram_tensor("dbg_qint", [NCH, P, 256], F32,
                                          kind="ExternalOutput").ap()
                    for c_ in range(NCH):
                        dma("qs", dbg4[c_], bigC[:, c_ * 384:c_ * 384 + 256],
                            ["bC0", "bC1", "bC2", "bC3"], ["dbgQ"],
                            f"dbgQ{c_}")
                off_main = 0
                n_edge = 2 if q < NQ - 1 else 1

                def emit_hc_front(hc, q=q, n_edge=n_edge,
                                  off_main=off_main):
                    pp = hc % 2
                    hkey = f"hg{hc}"
                    uk = f"qp{pp}"
                    uvw = qp(pp)[:, 0:514]
                    dma("qa", t1s[:, pp * D:(pp + 1) * D],
                        t1T_d[:, hc * P:(hc + 1) * P].rearrange(
                            "(c p) n -> p c n", p=P),
                        ["t1T_d"], [f"t1s{pp}"], f"D{s}h{q}_{hc}l")
                    bank = hc % 7
                    for c in range(NCH):
                        pg.op("pe", lambda c=c, pp=pp, bank=bank,
                              off_main=off_main: TEN.matmul(
                                pb[bank][:, :],
                                t1s[:, pp * D + c * P:pp * D + (c + 1) * P],
                                qint_c[c][:, off_main:off_main + 512],
                                start=(c == 0), stop=(c == NCH - 1)),
                              reads=[f"t1s{pp}", "bC0", "bC1", "bC2"],
                              writes=[PB[bank]], tag=f"D{s}h{q}_{hc}_{c}")
                    if n_edge > 0:
                        for c in range(NCH):
                            pg.op("pe", lambda c=c, pp=pp, hc=hc,
                                  n_edge=n_edge: TEN.matmul(
                                    pb[7][:, hc * 2:hc * 2 + n_edge],
                                    t1s[:, pp * D + c * P:
                                        pp * D + (c + 1) * P],
                                    qint_c[c][:, 512:512 + n_edge],
                                    start=(c == 0), stop=(c == NCH - 1),
                                    skip_group_check=True),
                                  reads=[f"t1s{pp}", "bC0", "bC1", "bC2"],
                                  writes=["pb7"], tag=f"D{s}he{q}_{hc}_{c}")
                    hblk = hga[:, hc * 512:(hc + 1) * 512]
                    if q == NQ - 1 and hc < 2:
                        vop(lambda uvw=uvw: VEC.memset(uvw[:, 513:514], 0.0),
                            [], [uk], f"D{s}uz{q}_{hc}")
                    vop(lambda uvw=uvw, bank=bank: VEC.tensor_mul(
                            uvw[:, 0:512], pb[bank][:, :],
                            qp(3)[:, 0:512]),
                        [PB[bank], "qp3"], [uk, PB[bank]],
                        f"D{s}hev{q}_{hc}")
                    if n_edge > 0:
                        vop(lambda uvw=uvw, hc=hc, n_edge=n_edge:
                            VEC.tensor_mul(
                                uvw[:, 512:512 + n_edge],
                                pb[7][:, hc * 2:hc * 2 + n_edge],
                                qp(3)[:, 512:512 + n_edge]),
                            ["pb7", "qp3"], [uk, "pb7"],
                            f"D{s}hee{q}_{hc}")
                    gv = hblk
                    gscr = bC(4)[:, (hc % 2) * 512:(hc % 2) * 512 + 512]
                    w0 = convw_c[:, 3 * hc:3 * hc + 1]
                    w1 = convw_c[:, 3 * hc + 1:3 * hc + 2]
                    w2 = convw_c[:, 3 * hc + 2:3 * hc + 3]
                    pg.op("sc", lambda gv=gv, uvw=uvw, w0=w0:
                          SCA.activation(gv, uvw[:, 0:512], AF.Copy,
                                         scale=w0),
                          reads=[uk, "convw_c"], writes=[hkey],
                          tag=f"D{s}cv0{q}_{hc}")
                    vop(lambda gscr=gscr, uvw=uvw, w2=w2:
                        GPS.tensor_tensor(
                            gscr, uvw[:, 2:514],
                            w2.broadcast_to((P, 512)), OP.mult),
                        [uk, "convw_c"], ["bC4"], f"D{s}cv2a{q}_{hc}",
                        eng="gp")
                    vop(lambda gv=gv, uvw=uvw, w1=w1:
                        VEC.scalar_tensor_tensor(
                            gv, uvw[:, 1:513], w1, gv, OP.mult, OP.add),
                        [uk, "convw_c", hkey], [hkey], f"D{s}cv1{q}_{hc}")
                    vop(lambda gv=gv, gscr=gscr:
                        GPS.tensor_tensor(gv, gv, gscr, OP.add),
                        ["bC4", hkey], [hkey], f"D{s}cv2b{q}_{hc}",
                        eng="gp")
                    pg.op("sc", lambda gv=gv, hc=hc: SCA.activation(
                            gv, gv, AF.Silu, bias=convb_c[:, hc:hc + 1],
                            scale=1.0),
                          reads=[hkey, "convb_c"], writes=[hkey],
                          tag=f"D{s}sl{q}_{hc}")

                def emit_hc_tail(hc, q=q):
                    hkey = f"hg{hc}"
                    gv = hga[:, hc * 512:(hc + 1) * 512]
                    treeB = bC(3)[:, 0:512]
                    ngv = bC(3)[:, 512:1024]
                    vop(lambda gv=gv, ngv=ngv: GPS.tensor_tensor(
                            ngv, gv, negone[:, 0:1].broadcast_to((P, 512)),
                            OP.mult),
                        [hkey, "negone", "bC3n"], ["bC3n"],
                        f"D{s}mn{q}_{hc}", eng="gp")
                    if hc == 0:
                        vop(lambda gv=gv: VEC.tensor_copy(
                                qp(4)[:, 0:512], gv),
                            [hkey], ["qp4"], f"D{s}mf{q}_{hc}")
                        vop(lambda ngv=ngv, treeB=treeB: VEC.tensor_copy(
                                treeB, ngv),
                            ["bC3n"], ["bC3", "bC3n"], f"D{s}mg{q}_{hc}")
                    else:
                        vop(lambda gv=gv: VEC.tensor_tensor(
                                qp(4)[:, 0:512], qp(4)[:, 0:512], gv,
                                OP.max),
                            [hkey, "qp4"], ["qp4"], f"D{s}mf{q}_{hc}")
                        vop(lambda ngv=ngv, treeB=treeB: VEC.tensor_tensor(
                                treeB, treeB, ngv, OP.max),
                            ["bC3n", "bC3"], ["bC3", "bC3n"],
                            f"D{s}mg{q}_{hc}")

                for hc in range(32):
                    emit_hc_front(hc)
                    if hc >= 2:
                        emit_hc_tail(hc - 2)
                emit_hc_tail(30)
                emit_hc_tail(31)
                if q < NQ - 1:
                    vop(lambda: VEC.tensor_copy(
                            qcarry[:, 0:8],
                            qint.rearrange("p (c t) -> p c t",
                                           c=NCH)[:, :, 512:513]),
                        ["bC0", "bC1", "bC2"], ["qcarry"], f"D{s}qs{q}")
                    vop(lambda: VEC.tensor_copy(icarry[:, :],
                                                qp(3)[:, 512:513]),
                        ["qp3"], ["icarry"], f"D{s}is{q}")
                vop(lambda: VEC.tensor_tensor(
                        qp(4)[:, 0:512], qp(4)[:, 0:512], bC(3)[:, 0:512],
                        OP.max),
                    ["qp4", "bC3"], ["qp4"], f"D{s}mj{q}")
                for j4 in range(4):
                    for jj in range(4):
                        for kk in range(4):
                            vop(lambda j4=j4, jj=jj, kk=kk: VEC.transpose(
                                    scr128[kk * 32:(kk + 1) * 32,
                                           jj * 32:(jj + 1) * 32],
                                    qp(4)[jj * 32:(jj + 1) * 32,
                                          j4 * P + kk * 32:
                                          j4 * P + (kk + 1) * 32]),
                                ["qp4"], ["scr128"],
                                f"D{s}a2{q}_{j4}_{jj}{kk}")
                    vop(lambda: VEC.reduce_max(scr128[:, 1:2],
                                               scr128[:, 0:P], AX),
                        ["scr128"], ["scr128"], f"D{s}a2r{q}_{j4}")
                    vop(lambda j4=j4: VEC.tensor_copy(
                            amax_q[:, j4:j4 + 1], scr128[:, 1:2]),
                        ["scr128"], ["amax_q"], f"D{s}a2c{q}_{j4}")
                vop(lambda: VEC.tensor_single_scalar(
                        amax_q[:, 0:4], amax_q[:, 0:4], 1e-5, OP.max),
                    ["amax_q"], ["amax_q"], f"D{s}2c{q}")
                vop(lambda: VEC.tensor_scalar_mul(
                        den_sb[:, 0:4], amax_q[:, 0:4], swr[:, 0:1]),
                    ["amax_q", "swr"], ["den_sb"], f"D{s}2i{q}")
                vop(lambda: VEC.reciprocal(amax_q[:, 4:8], amax_q[:, 0:4]),
                    ["amax_q"], ["amax_q"], f"D{s}2r{q}")
                vop(lambda: VEC.tensor_scalar_mul(
                        amax_q[:, 4:8], amax_q[:, 4:8], 127.0),
                    ["amax_q"], ["amax_q"], f"D{s}2s{q}")
                rep_tiles(lambda j: amax_q[:, 4 + j:5 + j], [0, 1, 2, 3], 7,
                          qp(2)[:, 0:512], ["qp2"], ["amax_q"], f"D{s}2b{q}")
                rep_tiles(lambda j: den_sb[:, j:j + 1], [0, 1, 2, 3], 6,
                          qp(3)[:, 0:512], ["qp3"], ["den_sb"], f"D{s}2v{q}")
                def qs0(b2):
                    pr = b2 % 2
                    tk = f"qp{pr}"
                    gview = hga[:, b2 * 512:(b2 + 1) * 512]
                    gk = f"hg{b2}"
                    ta = qp(pr)[:, 0:512]
                    ti = qp(pr)[:, 512:1024].bitcast(I32)[:, 0:512]
                    vop(lambda gview=gview, ta=ta: VEC.tensor_mul(
                            ta, gview, qp(2)[:, 0:512]),
                        [gk, "qp2"], [tk], f"D{s}z0{q}_{b2}")
                    vop(lambda ta=ta, ti=ti: VEC.tensor_copy(ti, ta),
                        [tk], [tk], f"D{s}z1{q}_{b2}")

                def qs1(b2):
                    pr = b2 % 2
                    tk = f"qp{pr}"
                    gview = hga[:, b2 * 512:(b2 + 1) * 512]
                    gk = f"hg{b2}"
                    ta = qp(pr)[:, 0:512]
                    ti = qp(pr)[:, 512:1024].bitcast(I32)[:, 0:512]
                    vop(lambda ta=ta, ti=ti, gview=gview: GPS.tensor_tensor(
                            gview, ta, ti, OP.subtract),
                        [tk, gk], [gk], f"D{s}z2{q}_{b2}", eng="gp")

                def qs2(b2):
                    pr = b2 % 2
                    td = hga[:, b2 * 512:(b2 + 1) * 512]
                    gk = f"hg{b2}"
                    tgk = "qp2b" if pr == 0 else "qp3b"
                    tg = (qp(2) if pr == 0 else qp(3))[:, 512:1024]
                    pg.op("sc", lambda td=td, tg=tg: SCA.activation(
                            tg, td, AF.Sign, bias=-0.5, scale=1.0),
                          reads=[gk, "cst2"], writes=[tgk],
                          tag=f"D{s}z3{q}_{b2}")
                    pg.op("sc", lambda tg=tg: SCA.activation(
                            tg, tg, AF.Relu),
                          reads=[tgk], writes=[tgk], tag=f"D{s}z3b{q}_{b2}")
                    pg.op("sc", lambda td=td: SCA.activation(
                            td, td, AF.Sign, bias=0.5, scale=1.0),
                          reads=[gk, "cst2"], writes=[gk],
                          tag=f"D{s}z4{q}_{b2}")
                    pg.op("sc", lambda td=td: SCA.activation(
                            td, td, AF.Relu, bias=0.0, scale=-1.0),
                          reads=[gk, "negone"], writes=[gk],
                          tag=f"D{s}z4b{q}_{b2}")

                def qs3(b2):
                    pr = b2 % 2
                    tk = f"qp{pr}"
                    tl = hga[:, b2 * 512:(b2 + 1) * 512]
                    gk = f"hg{b2}"
                    tgk = "qp2b" if pr == 0 else "qp3b"
                    tg = (qp(2) if pr == 0 else qp(3))[:, 512:1024]
                    ta = qp(pr)[:, 0:512]
                    ti = qp(pr)[:, 512:1024].bitcast(I32)[:, 0:512]
                    vop(lambda ta=ta, tg=tg, ti=ti:
                        VEC.scalar_tensor_tensor(
                            ta, tg, 1.0, ti, OP.mult, OP.add),
                        [tgk, tk], [tk], f"D{s}z5{q}_{b2}")
                    vop(lambda ta=ta, tl=tl, b2=b2:
                        VEC.scalar_tensor_tensor(
                            q2v[:, b2 * 512:(b2 + 1) * 512], tl, -1.0, ta,
                            OP.mult, OP.add),
                        [gk, tk], [f"bC{b2 // 4}"], f"D{s}z6{q}_{b2}")

                for i in range(34):
                    if i >= 2:
                        qs3(i - 2)
                    if i < 32:
                        qs0(i)
                    if 1 <= i <= 32:
                        qs1(i - 1)
                        qs2(i - 1)
                for kg in range(32):
                    wp = kg % 2
                    dma("qa", t2s[:, wp * D:(wp + 1) * D],
                        t2T_d[kg * P:(kg + 1) * P, :],
                        ["t2T_d"], [f"t2s{wp}"], f"D{s}w2l{q}_{kg}")
                    for oc in range(NCH):
                        pg.op("pe", lambda oc=oc, kg=kg, wp=wp:
                              TEN.matmul(
                                pb[oc][:, :],
                                t2s[:, wp * D + oc * P:
                                    wp * D + (oc + 1) * P],
                                q2v[:, kg * 512:(kg + 1) * 512],
                                start=(kg == 0), stop=(kg == 31)),
                              reads=[f"t2s{wp}", f"bC{kg // 4}"],
                              writes=[PB[oc]],
                              tag=f"D{s}w2{q}_{oc}_{kg}")
                sl = slice(q * 512, (q + 1) * 512)
                for oc in range(NCH):
                    pr = oc % 2
                    vop(lambda oc=oc, pr=pr: VEC.tensor_mul(
                            qp(pr)[:, 0:512], pb[oc][:, :],
                            qp(3)[:, 0:512]),
                        [PB[oc], "qp3"], [f"qp{pr}", PB[oc]],
                        f"D{s}w2e{q}_{oc}")
                    vop(lambda oc=oc, sl=sl, pr=pr: VEC.tensor_add(
                            xc(oc)[:, sl], xc(oc)[:, sl],
                            qp(pr)[:, 0:512]),
                        [f"qp{pr}", XK[oc]], [XK[oc]], f"D{s}w2x{q}_{oc}")

        for s in range(B):
            emit_load_x(s)
            if n_blocks >= 1:
                emit_l0_attn(s)
            if n_blocks >= 2:
                emit_l0_ffn(s)
            if n_blocks >= 3:
                emit_l1_attn(s)
            if n_blocks >= 4:
                emit_l1_ffn(s)
            emit_store_x(s)

        pg.op("gp", lambda: GPS.nop(),
              reads=[f"y{s}" for s in range(B)], writes=[], tag="final")

        nc.all_engine_barrier()
        r = nc._kernel_sem_range
        clear = range(r.start + 3, r.stop)
        nc.gpsimd.dma_reset(clear)
        nc.gpsimd.sem_clear(clear)
        nc.all_engine_barrier()
        pg.emit(nc, sems)
        nc.all_engine_barrier()
        nc.gpsimd.dma_reset(clear)
        nc.gpsimd.sem_clear(clear)
        nc.all_engine_barrier()

    return nc


def host_constants():
    ident = np.eye(P, dtype=np.float32)
    ones_col = np.ones((1, P), dtype=np.float32)
    onesT = np.ones((P, P), dtype=np.float32)
    ones128 = np.ones((P, 1), dtype=np.float32)
    sel16 = np.zeros((16, D), dtype=np.float32)
    for c in range(8):
        sel16[2 * c, c * 128:c * 128 + 64] = 1.0
        sel16[2 * c + 1, c * 128 + 64:c * 128 + 128] = 1.0
    identb = np.eye(P, dtype=np.float32).astype(ml_dtypes.bfloat16)
    return {"c_ident": ident, "c_ones_col": ones_col, "c_onesT": onesT,
            "c_ones128": ones128, "c_sel16": sel16, "c_identb": identb}


def kernel(**inputs):
    B_total = inputs["L0_attn_x"].shape[0]
    Bc = B_total // N_CORES
    nc = build_program(B=Bc, T=inputs["L0_attn_x"].shape[1], n_blocks=4)
    consts = host_constants()
    wnames = [k for k in inputs if k != "L0_attn_x"]
    in_maps = []
    for core in range(N_CORES):
        m = {"x": np.ascontiguousarray(
                inputs["L0_attn_x"][core * Bc:(core + 1) * Bc])}
        for k in wnames:
            m[k] = np.asarray(inputs[k])
        m.update(consts)
        in_maps.append(m)
    res = run_bass_kernel_spmd(nc, in_maps, core_ids=list(range(N_CORES)))
    return np.concatenate([r["y"] for r in res.results], axis=0)


# version 16
# speedup vs baseline: 1.2359x; 1.2028x over previous
from contextlib import ExitStack

import numpy as np
import ml_dtypes

import concourse.bass as bass
import concourse.mybir as mybir
from concourse.bass_utils import run_bass_kernel_spmd

F32 = mybir.dt.float32
BF16 = mybir.dt.bfloat16
I32 = mybir.dt.int32
AX = mybir.AxisListType.X
OP = mybir.AluOpType
AF = mybir.ActivationFunctionType

P = 128
D = 1024
NCH = 8
N_CORES = 8

DMA_ENGINES = ("qs", "qa", "qg")
ALL_ENGINES = ("pe", "dve", "sc", "gp", "qs", "qa", "qg")
PHYS = {"pe": "tensor", "dve": "vector", "sc": "scalar", "gp": "gpsimd",
        "qs": "sync", "qa": "scalar", "qg": "gpsimd"}


NSUB = 8


class Op:
    __slots__ = ("engine", "fn", "reads", "writes", "idx", "eng_pos",
                 "need_inc", "waits", "tag", "semname", "qidx")

    def __init__(self, engine, fn, reads, writes, idx, tag):
        self.engine = engine
        self.fn = fn
        self.reads = reads
        self.writes = writes
        self.idx = idx
        self.eng_pos = 0
        self.need_inc = False
        self.waits = {}
        self.tag = tag
        self.semname = engine
        self.qidx = 0


class Prog:
    def __init__(self, paranoid=False, paranoid_tags=(),
                 paranoid_engines=("dve", "sc", "gp")):
        self.ops = []
        self.qcount = {q: 0 for q in DMA_ENGINES}
        self.paranoid = paranoid
        self.paranoid_tags = tuple(paranoid_tags)
        self.paranoid_engines = tuple(paranoid_engines)

    def _ptag(self, o):
        return any(o.tag.startswith(p) for p in self.paranoid_tags)

    def op(self, engine, fn, reads=(), writes=(), tag=""):
        assert engine in ALL_ENGINES, engine
        o = Op(engine, fn, tuple(reads), tuple(writes), len(self.ops), tag)
        if engine in DMA_ENGINES:
            o.qidx = self.qcount[engine]
            self.qcount[engine] += 1
            o.semname = f"{engine}{o.qidx % NSUB}"
        self.ops.append(o)
        return o

    def resolve(self):
        last_write = {}
        readers = {}
        deps_of = [set() for _ in self.ops]
        for o in self.ops:
            deps = deps_of[o.idx]
            for k in o.reads:
                w = last_write.get(k)
                if w is not None:
                    deps.add(w.idx)
            for k in o.writes:
                w = last_write.get(k)
                if w is not None:
                    deps.add(w.idx)
                for r in readers.get(k, ()):
                    deps.add(r.idx)
            for k in o.writes:
                last_write[k] = o
                readers[k] = []
            for k in o.reads:
                readers.setdefault(k, []).append(o)
        for o in self.ops:
            if o.engine in DMA_ENGINES:
                o.need_inc = True
            for d in deps_of[o.idx]:
                dop = self.ops[d]
                if (self.paranoid or dop.engine != o.engine
                        or dop.engine in DMA_ENGINES or self._ptag(o)
                        or o.engine in self.paranoid_engines):
                    dop.need_inc = True
        counts = {}
        for o in self.ops:
            if o.need_inc:
                counts[o.semname] = counts.get(o.semname, 0) + (
                    16 if o.engine in DMA_ENGINES else 1)
            o.eng_pos = counts.get(o.semname, 0)
        for o in self.ops:
            if o.engine in DMA_ENGINES and o.qidx >= NSUB:
                o.waits[o.semname] = max(
                    o.waits.get(o.semname, 0), o.eng_pos - 16)
            for d in deps_of[o.idx]:
                dop = self.ops[d]
                if (not self.paranoid and not self._ptag(o)
                        and o.engine not in self.paranoid_engines
                        and dop.engine == o.engine
                        and dop.engine not in DMA_ENGINES):
                    continue
                if not dop.need_inc:
                    continue
                cur = o.waits.get(dop.semname, 0)
                o.waits[dop.semname] = max(cur, dop.eng_pos)
        seen = {}
        for o in self.ops:
            s = seen.setdefault(o.engine, {})
            for e in list(o.waits):
                v = o.waits[e]
                if s.get(e, -1) >= v:
                    del o.waits[e]
                else:
                    s[e] = v

    def emit(self, nc, sems):
        self.resolve()
        by_phys = {}
        for o in self.ops:
            by_phys.setdefault(PHYS[o.engine], []).append(o)

        def run_stream(eng_obj, ops_list):
            for o in ops_list:
                for e, v in sorted(o.waits.items()):
                    eng_obj.wait_ge(sems[e], v)
                last = o.fn()
                if o.need_inc:
                    assert last is not None, f"op {o.tag} returned None"
                    last.then_inc(sems[o.semname],
                                  16 if o.engine in DMA_ENGINES else 1)

        with nc.Block() as block:
            if "sync" in by_phys:
                @block.sync
                def _(eng):
                    run_stream(eng, by_phys["sync"])
            if "tensor" in by_phys:
                @block.tensor
                def _(eng):
                    run_stream(eng, by_phys["tensor"])
            if "vector" in by_phys:
                @block.vector
                def _(eng):
                    run_stream(eng, by_phys["vector"])
            if "scalar" in by_phys:
                @block.scalar
                def _(eng):
                    run_stream(eng, by_phys["scalar"])
            if "gpsimd" in by_phys:
                @block.gpsimd
                def _(eng):
                    run_stream(eng, by_phys["gpsimd"])


def build_program(B=4, T=2048, n_blocks=4, paranoid=False, debug_taps=(),
                  paranoid_tags=()):
    NT1 = T // 128
    NT5 = T // 512
    NQ = T // 512

    nc = bass.Bass("TRN2", target_bir_lowering=False, debug=False,
                   detect_race_conditions=paranoid,
                   dynamic_dma_scratch_size=64)

    x_in = nc.dram_tensor("x", [B, T, D], F32, kind="ExternalInput").ap()
    y_out = nc.dram_tensor("y", [B, T, D], F32, kind="ExternalOutput").ap()

    def win(name, shape):
        return nc.dram_tensor(name, shape, F32, kind="ExternalInput").ap()

    Wq = win("L0_attn_Wq", [D, D]); bq = win("L0_attn_bq", [D])
    Wk = win("L0_attn_Wk", [D, D]); bk = win("L0_attn_bk", [D])
    Wv = win("L0_attn_Wv", [D, D]); bv = win("L0_attn_bv", [D])
    Wo = win("L0_attn_Wo", [D, D]); bo = win("L0_attn_bo", [D])
    Wf = win("L0_ffn_W", [D, D]); bfv = win("L0_ffn_b", [D])
    lora_down_h = win("L0_ffn_lora_down", [4, 100])
    lora_up_h = win("L0_ffn_lora_up", [50, 4])
    down_aux_h = win("L0_ffn_down_aux", [100, D])
    up_aux_h = win("L0_ffn_up_aux", [D, 50])
    Wqkv = win("L1_attn_w_qkv", [3 * D, D]); bqkv = win("L1_attn_b_qkv", [3 * D])
    Wg1 = win("L1_attn_w_g1", [64, D]); bg1v = win("L1_attn_b_g1", [64])
    Wg2 = win("L1_attn_w_g2", [D, 64]); bg2v = win("L1_attn_b_g2", [D])
    Wout = win("L1_attn_w_out", [D, D]); boutv = win("L1_attn_b_out", [D])
    ln_gv = win("L1_attn_ln_g", [D]); ln_bv = win("L1_attn_ln_b", [D])
    W1 = win("L1_ffn_w1", [4 * D, D])
    convw_h = win("L1_ffn_conv_w", [4 * D, 1, 3])
    convb_h = win("L1_ffn_conv_b", [4 * D])
    W2 = win("L1_ffn_w2", [D, 4 * D])

    c_ident = nc.dram_tensor("c_ident", [P, P], F32, kind="ExternalInput").ap()
    c_identb = nc.dram_tensor("c_identb", [P, P], BF16, kind="ExternalInput").ap()
    c_ones_col = nc.dram_tensor("c_ones_col", [1, P], F32, kind="ExternalInput").ap()
    c_onesT = nc.dram_tensor("c_onesT", [P, P], F32, kind="ExternalInput").ap()
    c_ones128 = nc.dram_tensor("c_ones128", [P, 1], F32, kind="ExternalInput").ap()
    c_sel16 = nc.dram_tensor("c_sel16", [16, D], F32, kind="ExternalInput").ap()

    wqT_d = nc.dram_tensor("wqT_d", [D, D], F32, kind="Internal").ap()
    wkT_d = nc.dram_tensor("wkT_d", [D, D], F32, kind="Internal").ap()
    wvT_d = nc.dram_tensor("wvT_d", [D, D], F32, kind="Internal").ap()
    woT_d = nc.dram_tensor("woT_d", [D, D], F32, kind="Internal").ap()
    weffT_d = nc.dram_tensor("weffT_d", [D, D], F32, kind="Internal").ap()
    qkvT_d = nc.dram_tensor("qkvT_d", [D, 3 * D], F32, kind="Internal").ap()
    woutT_d = nc.dram_tensor("woutT_d", [D, D], F32, kind="Internal").ap()
    g1T_d = nc.dram_tensor("g1T_d", [D, 64], F32, kind="Internal").ap()
    g2T_d = nc.dram_tensor("g2T_d", [64, D], F32, kind="Internal").ap()
    t1T_d = nc.dram_tensor("t1T_d", [D, 4 * D], BF16, kind="Internal").ap()
    t2T_d = nc.dram_tensor("t2T_d", [4 * D, D], BF16, kind="Internal").ap()

    pg = Prog(paranoid=paranoid, paranoid_tags=paranoid_tags)

    with ExitStack() as ctx:
        E = ctx.enter_context

        def sb(name, shape, dt=F32):
            return E(nc.sbuf_tensor(name, shape, dt))

        x_all = sb("x_all", [P, NCH * T])
        bigB = sb("bigB", [P, 16384])
        bigC = sb("bigC", [P, 8192])
        qpool = sb("qpool", [P, 4608])
        band = sb("band", [P, 1536])
        nm_row = sb("nm_row", [65, T])
        rows = sb("rows", [P, 4096])
        kv_acc = sb("kv_acc", [P, 1024])
        norm_lhs = sb("norm_lhs", [P, 16 * NCH])
        scr128 = sb("scr128", [P, P])
        kvblk = scr128
        rstd_tm = sb("rstd_tm", [P, NT1])
        var_tm = sb("var_tm", [P, NT1])
        amax_q = sb("amax_q", [P, 8])
        den_sb = sb("den_sb", [P, 8])
        s_w = sb("s_w", [P, 2])
        swr = sb("swr", [P, 1])
        red_acc = sb("red_acc", [P, 1])
        wsum_tmp = sb("wsum_tmp", [P, 24])
        t1s = sb("t1s", [P, 2 * D], BF16)
        t2s = sb("t2s", [P, 2 * D], BF16)
        qcarry = sb("qcarry", [P, 16], BF16)
        icarry = sb("icarry", [P, 1])
        ident = sb("ident", [P, P])
        identb = sb("identb", [P, P], BF16)
        onesT = sb("onesT", [P, P])
        bq_c = sb("bq_c", [P, 8]); bo_c = sb("bo_c", [P, 8])
        bf_c = sb("bf_c", [P, 8]); bout_c = sb("bout_c", [P, 8])
        bq1_c = sb("bq1_c", [P, 8]); bg2_c = sb("bg2_c", [P, 8])
        lng_c = sb("lng_c", [P, 8]); lnb_c = sb("lnb_c", [P, 8])
        bg1_c = sb("bg1_c", [64, 1])
        convw_c = sb("convw_c", [P, 96])
        convb_c = sb("convb_c", [P, 32])
        negone = sb("negone", [P, 1])
        cst2 = sb("cst2", [P, 2])
        nc.const_aps.register(F32, 0.5, cst2[:, 0:1])
        nc.const_aps.register(F32, -0.5, cst2[:, 1:2])
        nc.const_aps.register(F32, -1.0, negone[:, 0:1])

        pb = [E(nc.psum_tensor(f"pb{i}", [P, 512], F32)) for i in range(8)]
        PB = [f"pb{i}" for i in range(8)]

        sems = {}
        for e in ("pe", "dve", "sc", "gp"):
            sems[e] = E(nc.semaphore(f"sem_{e}"))
        for q in DMA_ENGINES:
            for j in range(NSUB):
                sems[f"{q}{j}"] = E(nc.semaphore(f"sem_{q}{j}"))
        TEN, VEC, SCA, GPS, SYN = (nc.tensor, nc.vector, nc.scalar,
                                   nc.gpsimd, nc.sync)

        def xc(c):
            return x_all[:, c * T:(c + 1) * T]

        XK = [f"x{c}" for c in range(NCH)]

        def xstr(tau):
            return x_all[:, 0:NCH * T].rearrange(
                "p (c t) -> p c t", c=NCH)[:, :, tau * P:(tau + 1) * P]

        def bB(i):
            return bigB[:, i * 1024:(i + 1) * 1024]

        KB = [f"bB{i}" for i in range(16)]

        def bC(i):
            return bigC[:, i * 1024:(i + 1) * 1024]

        KC = [f"bC{i}" for i in range(8)]

        def qp(i):
            return qpool[:, i * 1024:min((i + 1) * 1024, 4608)]

        KQ = [f"qp{i}" for i in range(5)]

        bkbc = None
        bvbc = None
        wres = [bB(c) for c in range(8)]
        wres2 = [bB(8 + c) for c in range(8)]
        WK1 = [f"bB{c}" for c in range(8)]
        WK2 = [f"bB{8 + c}" for c in range(8)]
        hga = bigB[:, 0:32 * 512]
        bkbc = bigC[:, 3 * 1024:4 * 1024]
        bvbc = bigC[:, 4 * 1024:5 * 1024]
        norm16t = band[0:16, 1024:1536]
        sel16 = band[0:16, 0:D]
        ghv = band[64:128, 1024:1536]

        ROWS = {}
        for i, nmn in enumerate(["wq", "weff", "q1", "g1", "bk", "bv",
                                 "bk1", "bv1", "wk", "wv", "k1", "v1"]):
            ROWS[nmn] = (32 * (i % 3), (i // 3) * 1024)

        def row(nmn, lo, n):
            p0, f0 = ROWS[nmn]
            return rows[p0:p0 + 1, f0 + lo:f0 + lo + n]

        def rowbase(nmn):
            return ROWS[nmn][0]

        def nmr_at(nmn, lo, n):
            b = ROWS[nmn][0]
            return nm_row[b:b + 1, lo:lo + n]

        def ones_at(nmn):
            b = ROWS[nmn][0]
            return onesT[b:b + 1, 0:P]

        def dma(q, out, in_, reads, writes, tag="", slow=False):
            eng = {"qs": SYN, "qa": SCA, "qg": GPS}[q]

            def _go(out=out, in_=in_, eng=eng, slow=slow, tag=tag):
                try:
                    return eng.dma_start(out=out, in_=in_,
                                         allow_slow_non_contiguous=slow)
                except Exception as e:
                    raise RuntimeError(f"dma op {tag!r} failed: {e}") from e

            pg.op(q, _go, reads=reads, writes=writes, tag=tag)

        def vop(fn, reads, writes, tag="", eng="dve"):
            pg.op(eng, fn, reads=reads, writes=writes, tag=tag)

        dma("qs", ident[:], c_ident, [], ["ident"], "c0")
        dma("qs", identb[:], c_identb, [], ["identb"], "c1")
        dma("qs", onesT[:], c_onesT, [], ["onesT"], "c3")
        dma("qs", sel16, c_sel16, [], ["sel16"], "c5")
        vop(lambda: VEC.memset(negone[:, :], -1.0), [], ["negone"], "c6")
        vop(lambda: VEC.memset(cst2[:, 0:1], 0.5), [], ["cst2"], "c7")
        vop(lambda: VEC.memset(cst2[:, 1:2], -0.5), [], ["cst2"], "c8")

        def load_bias_col(dst, dst_key, src_vec, n, tag):
            tmp = bB(0)
            dma("qs", tmp[0:n, 0:P], src_vec.rearrange("(a b) -> a b", b=P),
                [], ["bB0"], f"{tag}l")
            pg.op("pe", lambda tmp=tmp, n=n: TEN.transpose(
                    pb[0][0:P, 0:n], tmp[0:n, 0:P], ident[0:n, 0:n]),
                  reads=["bB0", "ident"], writes=["pb0"], tag=f"{tag}t")
            vop(lambda dst=dst, n=n: VEC.tensor_copy(dst[:, 0:n],
                                                     pb[0][:, 0:n]),
                ["pb0"], [dst_key, "pb0"], f"{tag}e")

        load_bias_col(bq_c, "bq_c", bq, 8, "bq")
        load_bias_col(bo_c, "bo_c", bo, 8, "bo")
        load_bias_col(bf_c, "bf_c", bfv, 8, "bf")
        load_bias_col(bout_c, "bout_c", boutv, 8, "bout")
        load_bias_col(bq1_c, "bq1_c", bqkv[0:D], 8, "bq1")
        load_bias_col(bg2_c, "bg2_c", bg2v, 8, "bg2")
        load_bias_col(lng_c, "lng_c", ln_gv, 8, "lng")
        load_bias_col(lnb_c, "lnb_c", ln_bv, 8, "lnb")
        dma("qs", bB(0)[0:1, 0:64], bg1v.rearrange("(a b) -> a b", a=1),
            [], ["bB0"], "bg1l")
        pg.op("pe", lambda: TEN.transpose(pb[0][0:64, 0:1], bB(0)[0:1, 0:64],
                                          ident[0:1, 0:1]),
              reads=["bB0", "ident"], writes=["pb0"], tag="bg1t")
        vop(lambda: VEC.tensor_copy(bg1_c[:, :], pb[0][0:64, 0:1]),
            ["pb0"], ["bg1_c", "pb0"], "bg1e")
        for nmn, vec in (("bk", bk), ("bv", bv), ("bk1", bqkv[D:2 * D]),
                         ("bv1", bqkv[2 * D:3 * D])):
            dma("qs", row(nmn, 0, D), vec.rearrange("(a b) -> a b", a=1),
                [], ["rows"], f"r_{nmn}")
        for j in range(32):
            dma("qs", convw_c[:, 3 * j:3 * j + 3],
                convw_h.rearrange("a b c -> (b a) c")[j * P:(j + 1) * P, :],
                [], ["convw_c"], f"cw{j}")
        dma("qs", convb_c[:, :], convb_h.rearrange("(a b) -> b a", b=P),
            [], ["convb_c"], "cb", slow=True)

        lora_s = bigB[0:100, 2 * 1024:2 * 1024 + 4]
        lora_ut = bigB[0:4, 2 * 1024 + 16:2 * 1024 + 66]
        msum = bigB[0:50, 2 * 1024 + 80:2 * 1024 + 81]
        daux = bigB[0:100, 3 * 1024:4 * 1024]
        lora_a = bigB[0:4, 4 * 1024:5 * 1024]
        lora_m = bigB[0:50, 5 * 1024:6 * 1024]
        upaux_t = bigB[0:50, 6 * 1024:7 * 1024]

        dma("qs", bB(1)[0:50, 0:4], lora_up_h, [], ["bB1"], "lu")
        dma("qs", bB(1)[0:4, 100:200], lora_down_h, [], ["bB1"], "ldn")
        dma("qs", daux, down_aux_h, [], ["bB3"], "dax")
        pg.op("pe", lambda: TEN.transpose(pb[0][0:100, 0:4],
                                          bB(1)[0:4, 100:200],
                                          ident[0:4, 0:4]),
              reads=["bB1", "ident"], writes=["pb0"], tag="ldT")
        vop(lambda: VEC.tensor_copy(lora_s, pb[0][0:100, 0:4]),
            ["pb0"], ["bB2", "pb0"], "ldTe")
        pg.op("pe", lambda: TEN.transpose(pb[0][0:4, 0:50], bB(1)[0:50, 0:4],
                                          ident[0:50, 0:50]),
              reads=["bB1", "ident"], writes=["pb0"], tag="luT")
        vop(lambda: VEC.tensor_copy(lora_ut, pb[0][0:4, 0:50]),
            ["pb0"], ["bB2", "pb0"], "luTe")
        for ic in range(NCH):
            dma("qs", bB(1)[0:P, 0:50], up_aux_h[ic * P:(ic + 1) * P, :],
                [], ["bB1"], f"ua{ic}")
            pg.op("pe", lambda: TEN.transpose(pb[1][0:50, 0:P],
                                              bB(1)[0:P, 0:50], ident[:, :]),
                  reads=["bB1", "ident"], writes=["pb1"], tag=f"uaT{ic}")
            vop(lambda ic=ic: VEC.tensor_copy(
                    upaux_t[:, ic * P:(ic + 1) * P], pb[1][0:50, 0:P]),
                ["pb1"], ["bB6", "pb1"], f"uaTe{ic}")
        for h in range(2):
            pg.op("pe", lambda h=h: TEN.matmul(
                    pb[0][0:4, :], lora_s, daux[:, h * 512:(h + 1) * 512],
                    start=True, stop=True),
                  reads=["bB2", "bB3"], writes=["pb0"], tag=f"lA{h}")
            vop(lambda h=h: VEC.tensor_copy(
                    lora_a[:, h * 512:(h + 1) * 512], pb[0][0:4, :]),
                ["pb0"], ["bB4", "pb0"], f"lAe{h}")
        for h in range(2):
            pg.op("pe", lambda h=h: TEN.matmul(
                    pb[0][0:50, :], lora_ut, lora_a[:, h * 512:(h + 1) * 512],
                    start=True, stop=True),
                  reads=["bB2", "bB4"], writes=["pb0"], tag=f"lM{h}")
            vop(lambda h=h: VEC.tensor_copy(
                    lora_m[:, h * 512:(h + 1) * 512], pb[0][0:50, :]),
                ["pb0"], ["bB5", "pb0"], f"lMe{h}")
        vop(lambda: VEC.reduce_sum(msum, lora_m, AX),
            ["bB5"], ["bB2"], "lMs")

        def prep_w(W_hbm, wT_d, key, n_out, wsum_names, lora, tag):
            noc = n_out // P
            for oc in range(noc):
                nb = oc % 2
                wn = bB(nb)
                dma("qa", wn[:, :], W_hbm[oc * P:(oc + 1) * P, :],
                    [], [KB[nb]], f"{tag}l{oc}")
                if wsum_names:
                    vop(lambda wn=wn, oc=oc: VEC.reduce_sum(
                            wsum_tmp[:, oc:oc + 1], wn[:, :], AX),
                        [KB[nb]], ["wsum_tmp"], f"{tag}w{oc}")
                for ic in range(NCH):
                    bank = 2 + (ic % 4)
                    pg.op("pe", lambda wn=wn, ic=ic, bank=bank:
                          TEN.transpose(pb[bank][:, 0:P],
                                        wn[:, ic * P:(ic + 1) * P],
                                        ident[:, :]),
                          reads=[KB[nb], "ident"], writes=[PB[bank]],
                          tag=f"{tag}T{oc}_{ic}")
                    if lora:
                        pg.op("pe", lambda ic=ic, oc=oc, bank=bank:
                              TEN.matmul(pb[bank][:, 0:P],
                                         lora_m[:, ic * P:(ic + 1) * P],
                                         upaux_t[:, oc * P:(oc + 1) * P],
                                         start=False, stop=True,
                                         skip_group_check=True),
                              reads=["bB5", "bB6"], writes=[PB[bank]],
                              tag=f"{tag}L{oc}_{ic}")
                    eng = "sc" if ic % 2 == 0 else "dve"
                    fn = SCA.copy if eng == "sc" else VEC.tensor_copy
                    vop(lambda fn=fn, ic=ic, bank=bank: fn(
                            bigC[:, 4096 + (ic % 4) * P:
                                 4096 + (ic % 4) * P + P],
                            pb[bank][:, 0:P]),
                        [PB[bank]], ["bC4", PB[bank]], f"{tag}e{oc}_{ic}",
                        eng=eng)
                    if ic % 4 == 3:
                        lo = (ic - 3) * P
                        dma("qs", wT_d[lo:lo + 512,
                                       oc * P:(oc + 1) * P].rearrange(
                                "(a p) b -> p a b", p=P),
                            bigC[:, 4096:4096 + 512],
                            ["bC4"], [key], f"{tag}s{oc}_{ic}")
            if wsum_names:
                for gi, wname in enumerate(wsum_names):
                    pg.op("pe", lambda gi=gi: TEN.transpose(
                            pb[6][0:8, 0:P], wsum_tmp[:, gi * 8:gi * 8 + 8],
                            ident[:, :]),
                          reads=["wsum_tmp", "ident"], writes=[PB[6]],
                          tag=f"{tag}wT{gi}")
                    vop(lambda gi=gi: VEC.tensor_copy(
                            scr128[0:8, 0:P], pb[6][0:8, 0:P]),
                        [PB[6]], ["scr128", PB[6]], f"{tag}we{gi}")
                    for rr in range(8):
                        dma("qs", row(wname, rr * P, P),
                            scr128[rr:rr + 1, 0:P],
                            ["scr128"], ["rows"], f"{tag}wf{gi}_{rr}")

        prep_w(Wq, wqT_d, "wqT_d", D, ["wq"], False, "pq")
        prep_w(Wk, wkT_d, "wkT_d", D, ["wk"], False, "pk")
        prep_w(Wv, wvT_d, "wvT_d", D, ["wv"], False, "pv")
        prep_w(Wo, woT_d, "woT_d", D, [], False, "po")
        prep_w(Wf, weffT_d, "weffT_d", D, ["weff"], True, "pf")
        for h in range(2):
            pg.op("pe", lambda h=h: TEN.matmul(
                    pb[7][0:1, :], msum,
                    upaux_t[:, h * 512:(h + 1) * 512], start=True, stop=True),
                  reads=["bB2", "bB6"], writes=[PB[7]], tag=f"pfw{h}")
            vop(lambda h=h: VEC.tensor_add(
                    row("weff", h * 512, 512), row("weff", h * 512, 512),
                    pb[7][0:1, :]),
                [PB[7], "rows"], ["rows", PB[7]], f"pfwa{h}")
        prep_w(Wqkv, qkvT_d, "qkvT_d", 3 * D, ["q1", "k1", "v1"], False, "pj")
        prep_w(Wout, woutT_d, "woutT_d", D, [], False, "pu")
        dma("qs", bB(0)[0:64, 0:D], Wg1, [], ["bB0"], "g1l")
        for ic in range(NCH):
            pg.op("pe", lambda ic=ic: TEN.transpose(
                    pb[2][0:P, 0:64], bB(0)[0:64, ic * P:(ic + 1) * P],
                    ident[0:64, 0:64]),
                  reads=["bB0", "ident"], writes=[PB[2]], tag=f"g1T{ic}")
            vop(lambda: VEC.tensor_copy(bB(1)[:, 0:64], pb[2][:, 0:64]),
                [PB[2]], ["bB1", PB[2]], f"g1e{ic}")
            dma("qs", g1T_d[ic * P:(ic + 1) * P, :], bB(1)[:, 0:64],
                ["bB1"], ["g1T_d"], f"g1s{ic}")
        vop(lambda: VEC.reduce_sum(scr128[0:64, 0:1], bB(0)[0:64, 0:D], AX),
            ["bB0"], ["scr128"], "g1w")
        pg.op("pe", lambda: TEN.transpose(pb[2][0:1, 0:64],
                                          scr128[0:64, 0:1],
                                          ident[0:64, 0:64]),
              reads=["scr128", "ident"], writes=[PB[2]], tag="g1wT")
        vop(lambda: VEC.tensor_copy(row("g1", 0, 64), pb[2][0:1, 0:64]),
            [PB[2]], ["rows", PB[2]], "g1we")
        for ic in range(NCH):
            dma("qs", bB(0)[0:P, 0:64], Wg2[ic * P:(ic + 1) * P, :],
                [], ["bB0"], f"g2l{ic}")
            pg.op("pe", lambda: TEN.transpose(pb[3][0:64, 0:P],
                                              bB(0)[0:P, 0:64], ident[:, :]),
                  reads=["bB0", "ident"], writes=[PB[3]], tag=f"g2T{ic}")
            vop(lambda: VEC.tensor_copy(bB(1)[0:64, 0:P], pb[3][0:64, 0:P]),
                [PB[3]], ["bB1", PB[3]], f"g2e{ic}")
            dma("qs", g2T_d[:, ic * P:(ic + 1) * P], bB(1)[0:64, 0:P],
                ["bB1"], ["g2T_d"], f"g2s{ic}")

        q_t = bB(2); q_d = bB(3); q_g = bB(4)
        q_i = bB(5).bitcast(I32)[:, 0:D]
        q_b = bB(6).bitcast(BF16)[:, 0:D]
        q_o = bB(7).bitcast(BF16)[:, 0:D]

        def prep_tern(W_hbm, tT_d, key, n_out, n_in, sw_col, tag):
            noc = n_out // P
            nig = n_in // D
            first = True
            for oc in range(noc):
                for ig in range(nig):
                    nb = (oc * nig + ig) % 2
                    wn = bB(nb)
                    dma("qa", wn[:, :],
                        W_hbm[oc * P:(oc + 1) * P, ig * D:(ig + 1) * D],
                        [], [KB[nb]], f"{tag}ml{oc}_{ig}")
                    vop(lambda wn=wn: VEC.reduce_sum(
                            scr128[:, 1:2], wn[:, :], AX,
                            apply_absolute_value=True),
                        [KB[nb]], ["scr128"], f"{tag}mr{oc}_{ig}")
                    if first:
                        vop(lambda: VEC.tensor_copy(red_acc[:, :],
                                                    scr128[:, 1:2]),
                            ["scr128"], ["red_acc"], f"{tag}mi")
                        first = False
                    else:
                        vop(lambda: VEC.tensor_add(
                                red_acc[:, :], red_acc[:, :], scr128[:, 1:2]),
                            ["scr128", "red_acc"], ["red_acc"],
                            f"{tag}ma{oc}_{ig}")
            pg.op("pe", lambda: TEN.matmul(pb[4][0:1, 0:1], red_acc[:, :],
                                           onesT[:, 0:1], start=True,
                                           stop=True),
                  reads=["red_acc", "onesT"], writes=[PB[4]],
                  tag=f"{tag}ms")
            vop(lambda n_out=n_out, n_in=n_in: VEC.tensor_scalar(
                    scr128[0:1, 0:1], pb[4][0:1, 0:1],
                    1.0 / (n_out * n_in), 1e-5, OP.mult, OP.max),
                [PB[4]], ["scr128", PB[4]], f"{tag}mc")
            vop(lambda: VEC.reciprocal(scr128[0:1, 1:2], scr128[0:1, 0:1]),
                ["scr128"], ["scr128"], f"{tag}mrc")
            pg.op("pe", lambda sw_col=sw_col: TEN.matmul(
                    pb[4][:, sw_col:sw_col + 1], onesT[0:1, 0:P],
                    scr128[0:1, 1:2], start=True, stop=True),
                  reads=["scr128", "onesT"], writes=[PB[4]],
                  tag=f"{tag}mp")
            vop(lambda sw_col=sw_col: VEC.tensor_copy(
                    s_w[:, sw_col:sw_col + 1], pb[4][:, sw_col:sw_col + 1]),
                [PB[4]], ["s_w", PB[4]], f"{tag}me")
            for oc in range(noc):
                for ig in range(nig):
                    nb = (oc * nig + ig) % 2
                    wn = bB(nb)
                    dma("qa", wn[:, :],
                        W_hbm[oc * P:(oc + 1) * P, ig * D:(ig + 1) * D],
                        [], [KB[nb]], f"{tag}tl{oc}_{ig}")
                    vop(lambda wn=wn, sw_col=sw_col: GPS.tensor_tensor(
                            q_t[:, :], wn[:, :],
                            s_w[:, sw_col:sw_col + 1].broadcast_to((P, D)),
                            OP.mult),
                        [KB[nb], "s_w"], ["bB2"], f"{tag}q1_{oc}_{ig}",
                        eng="gp")
                    vop(lambda: VEC.tensor_single_scalar(
                            q_g[:, :], q_t[:, :], 0.5, OP.is_gt),
                        ["bB2"], ["bB4"], f"{tag}q4_{oc}_{ig}")
                    vop(lambda: VEC.tensor_single_scalar(
                            q_d[:, :], q_t[:, :], -0.5, OP.is_lt),
                        ["bB2"], ["bB3"], f"{tag}q5_{oc}_{ig}")
                    vop(lambda: VEC.scalar_tensor_tensor(
                            q_b, q_d[:, :], -1.0, q_g[:, :],
                            OP.mult, OP.add),
                        ["bB3", "bB4"], ["bB6"], f"{tag}q8_{oc}_{ig}")
                    for ic in range(NCH):
                        bank = 2 + (ic % 2)
                        pg.op("pe", lambda ic=ic, bank=bank: TEN.transpose(
                                pb[bank][:, 0:P].bitcast(BF16)[:, 0:P],
                                q_b[:, ic * P:(ic + 1) * P], identb[:, :]),
                              reads=["bB6", "identb"], writes=[PB[bank]],
                              tag=f"{tag}T{oc}_{ig}_{ic}")
                        vop(lambda ic=ic, bank=bank: VEC.tensor_copy(
                                q_o[:, ic * P:(ic + 1) * P],
                                pb[bank][:, 0:P].bitcast(BF16)[:, 0:P]),
                            [PB[bank]], ["bB7", PB[bank]],
                            f"{tag}te{oc}_{ig}_{ic}")
                    dma("qs", tT_d[ig * D:(ig + 1) * D,
                                   oc * P:(oc + 1) * P].rearrange(
                            "(a p) b -> p a b", p=P),
                        q_o,
                        ["bB7"], [key], f"{tag}ts{oc}_{ig}")

        prep_tern(W1, t1T_d, "t1T_d", 4 * D, D, 0, "t1")
        prep_tern(W2, t2T_d, "t2T_d", D, 4 * D, 1, "t2")
        vop(lambda: VEC.reciprocal(s_w[:, 0:1], s_w[:, 0:1]),
            ["s_w"], ["s_w"], "isw1")
        vop(lambda: VEC.reciprocal(s_w[:, 1:2], s_w[:, 1:2]),
            ["s_w"], ["s_w"], "isw2")
        vop(lambda: VEC.tensor_scalar_mul(swr[:, :], s_w[:, 1:2],
                                          1.0 / 127.0),
            ["s_w"], ["swr"], "iswr")
        for k3 in range(3):
            vop(lambda k3=k3: VEC.tensor_scalar_mul(
                    convw_c[:, k3:96:3], convw_c[:, k3:96:3], s_w[:, 0:1]),
                ["convw_c", "s_w"], ["convw_c"], f"cwf{k3}")

        def load_wres(dst, keys, src_d, src_key, col_lo, ncols, tag):
            for c in range(NCH):
                dma("qs", dst[c][:, 0:ncols],
                    src_d[c * P:(c + 1) * P, col_lo:col_lo + ncols],
                    [src_key], [keys[c]], f"{tag}{c}")

        def rep_tiles(src_col_fn, idxs, bank, dst, dst_keys, reads_extra,
                      tag):
            n = len(idxs)
            for j, tau in enumerate(idxs):
                pg.op("pe", lambda j=j, tau=tau, bank=bank: TEN.matmul(
                        pb[bank][:, j * P:(j + 1) * P],
                        src_col_fn(tau).broadcast_to((P, P)),
                        ident[:, :], start=(j == 0), stop=(j == n - 1),
                        skip_group_check=True),
                      reads=list(reads_extra) + ["ident"],
                      writes=[PB[bank]], tag=f"{tag}m{j}")
            vop(lambda n=n, bank=bank, dst=dst: VEC.tensor_copy(
                    dst[:, 0:n * P], pb[bank][:, 0:n * P]),
                [PB[bank]], list(dst_keys) + [PB[bank]], f"{tag}e")

        def ones_rep(row_ap, n, bank, dst, dst_keys, tag):
            pg.op("pe", lambda row_ap=row_ap, n=n, bank=bank: TEN.matmul(
                    pb[bank][:, 0:n], onesT[0:1, 0:P], row_ap,
                    start=True, stop=True),
                  reads=["onesT", "nm_row", "rows"], writes=[PB[bank]],
                  tag=f"{tag}m")
            vop(lambda n=n, bank=bank, dst=dst: VEC.tensor_copy(
                    dst[:, 0:n], pb[bank][:, 0:n]),
                [PB[bank]], list(dst_keys) + [PB[bank]], f"{tag}e")

        def make_bbc(dst, dst_key, rowname, tag):
            for h in range(2):
                pg.op("pe", lambda h=h, rowname=rowname: TEN.matmul(
                        pb[6 + h][:, :], ones_at(rowname),
                        row(rowname, h * 512, 512), start=True, stop=True),
                      reads=["onesT", "rows"], writes=[PB[6 + h]],
                      tag=f"{tag}{h}")
                vop(lambda h=h, dst=dst: VEC.tensor_copy(
                        dst[:, h * 512:(h + 1) * 512], pb[6 + h][:, :]),
                    [PB[6 + h]], [dst_key, PB[6 + h]], f"{tag}e{h}")

        def emit_ln(tag):
            sq = [bigB[:, 0:2048], bigB[:, 2048:4096]]
            sqk = [["bB0", "bB1"], ["bB2", "bB3"]]
            for c in range(NCH):
                b_ = c % 2
                pg.op("sc", lambda c=c, b_=b_: SCA.activation(
                        sq[b_][:, 0:T], xc(c), AF.Square),
                      reads=[XK[c]], writes=sqk[b_], tag=f"{tag}sq{c}")
                for t5 in range(NT5):
                    pg.op("pe", lambda c=c, t5=t5: TEN.matmul(
                            pb[t5][:, :], onesT[:, :],
                            xc(c)[:, t5 * 512:(t5 + 1) * 512],
                            start=(c == 0), stop=(c == NCH - 1)),
                          reads=[XK[c], "onesT"], writes=[PB[t5]],
                          tag=f"{tag}s{c}_{t5}")
                    pg.op("pe", lambda c=c, t5=t5, b_=b_: TEN.matmul(
                            pb[4 + t5][:, :], onesT[:, :],
                            sq[b_][:, t5 * 512:(t5 + 1) * 512],
                            start=(c == 0), stop=(c == NCH - 1)),
                          reads=sqk[b_] + ["onesT"], writes=[PB[4 + t5]],
                          tag=f"{tag}ss{c}_{t5}")
            for t5 in range(NT5):
                sl = slice(t5 * 512, (t5 + 1) * 512)
                for rb in (0, 32, 64):
                    vop(lambda t5=t5, sl=sl, rb=rb: VEC.tensor_single_scalar(
                            nm_row[rb:rb + 1, sl], pb[t5][rb:rb + 1, :],
                            -1.0 / D, OP.mult),
                        [PB[t5]], ["nm_row"], f"{tag}nm{t5}_{rb}")
                vop(lambda t5=t5: VEC.tensor_scalar_mul(
                        qp(0)[:, 0:512], pb[t5][:, :], 1.0 / D),
                    [PB[t5]], ["qp0", PB[t5]], f"{tag}m{t5}")
                vop(lambda: VEC.tensor_mul(qp(0)[:, 0:512], qp(0)[:, 0:512],
                                           qp(0)[:, 0:512]),
                    ["qp0"], ["qp0"], f"{tag}m2{t5}")
                vop(lambda t5=t5: VEC.scalar_tensor_tensor(
                        qp(0)[:, 0:512], pb[4 + t5][:, :], 1.0 / D,
                        qp(0)[:, 0:512], OP.mult, OP.subtract),
                    [PB[4 + t5], "qp0"], ["qp0", PB[4 + t5]], f"{tag}v{t5}")
                for j in range(4):
                    for jj in range(4):
                        vop(lambda j=j, jj=jj: VEC.transpose(
                                scr128[jj * 32:(jj + 1) * 32, 0:32],
                                qp(0)[0:32, j * P + jj * 32:
                                      j * P + (jj + 1) * 32]),
                            ["qp0"], ["scr128"], f"{tag}vt{t5}_{j}_{jj}")
                    vop(lambda t5=t5, j=j: VEC.tensor_copy(
                            var_tm[:, t5 * 4 + j:t5 * 4 + j + 1],
                            scr128[:, 0:1]),
                        ["scr128"], ["var_tm"], f"{tag}vc{t5}_{j}")
            vop(lambda: VEC.tensor_scalar_add(var_tm[:, :], var_tm[:, :],
                                              1e-5),
                ["var_tm"], ["var_tm"], f"{tag}ve")
            vop(lambda: VEC.reciprocal(var_tm[:, :], var_tm[:, :]),
                ["var_tm"], ["var_tm"], f"{tag}vr")
            pg.op("sc", lambda: SCA.activation(rstd_tm[:, :], var_tm[:, :],
                                               AF.Sqrt),
                  reads=["var_tm"], writes=["rstd_tm"], tag=f"{tag}vs")

        def emit_passA(s, is_l1):
            ktile = bC(0)[:, 0:1024]
            vt = bC(1)[:, 0:1024]
            vkeys = ["bC1"]
            wk_key, wv_key = ("k1", "v1") if is_l1 else ("wk", "wv")
            vop(lambda: VEC.memset(kv_acc[:, :], 0.0), [], ["kv_acc"],
                f"A{s}z")
            for t1 in range(NT1):
                tsl = slice(t1 * P, (t1 + 1) * P)
                for half in range(2):
                    hsl = slice(half * 512, (half + 1) * 512)
                    for c in range(NCH):
                        pg.op("pe", lambda c=c, tsl=tsl, hsl=hsl, half=half:
                              TEN.matmul(pb[half][:, :], xc(c)[:, tsl],
                                         wres[c][:, hsl],
                                         start=(c == 0), stop=False),
                              reads=[XK[c], WK1[c]], writes=[PB[half]],
                              tag=f"A{s}k{t1}_{half}_{c}")
                    pg.op("pe", lambda tsl=tsl, half=half, wk_key=wk_key:
                          TEN.matmul(
                            pb[half][:, :],
                            nm_row[rowbase(wk_key):rowbase(wk_key) + 1, tsl],
                            row(wk_key, half * 512, 512),
                            start=False, stop=True),
                          reads=["nm_row", "rows"], writes=[PB[half]],
                          tag=f"A{s}km{t1}_{half}")
                    for c in range(NCH):
                        pg.op("pe", lambda c=c, tsl=tsl, hsl=hsl, half=half:
                              TEN.matmul(pb[2 + half][:, :], xc(c)[:, tsl],
                                         wres2[c][:, hsl],
                                         start=(c == 0), stop=False),
                              reads=[XK[c], WK2[c]], writes=[PB[2 + half]],
                              tag=f"A{s}v{t1}_{half}_{c}")
                    pg.op("pe", lambda tsl=tsl, half=half, wv_key=wv_key:
                          TEN.matmul(
                            pb[2 + half][:, :],
                            nm_row[rowbase(wv_key):rowbase(wv_key) + 1, tsl],
                            row(wv_key, half * 512, 512),
                            start=False, stop=True),
                          reads=["nm_row", "rows"], writes=[PB[2 + half]],
                          tag=f"A{s}vm{t1}_{half}")
                for half in range(2):
                    hsl = slice(half * 512, (half + 1) * 512)
                    if is_l1:
                        vop(lambda t1=t1, half=half: VEC.scalar_tensor_tensor(
                                qp(1)[:, 0:512], pb[half][:, :],
                                rstd_tm[:, t1:t1 + 1],
                                bkbc[:, half * 512:(half + 1) * 512],
                                OP.mult, OP.add),
                            [PB[half], "rstd_tm", "bC3"],
                            ["qp1", PB[half]], f"A{s}ke{t1}_{half}")
                        pg.op("sc", lambda half=half, hsl=hsl, ktile=ktile:
                              SCA.activation(ktile[:, hsl], qp(1)[:, 0:512],
                                             AF.Exp),
                              reads=["qp1"], writes=["bC0"],
                              tag=f"A{s}kx{t1}_{half}")
                    else:
                        vop(lambda t1=t1, half=half, hsl=hsl, ktile=ktile:
                            VEC.scalar_tensor_tensor(
                                ktile[:, hsl], pb[half][:, :],
                                rstd_tm[:, t1:t1 + 1],
                                bkbc[:, half * 512:(half + 1) * 512],
                                OP.mult, OP.add),
                            [PB[half], "rstd_tm", "bC3"],
                            ["bC0", PB[half]], f"A{s}ke{t1}_{half}")
                        vop(lambda hsl=hsl, ktile=ktile:
                            VEC.tensor_scalar_min(
                                qp(1)[:, 0:512], ktile[:, hsl], 0.0),
                            ["bC0"], ["qp1"], f"A{s}kmn{t1}_{half}")
                        pg.op("sc", lambda: SCA.activation(
                                qp(1)[:, 0:512], qp(1)[:, 0:512], AF.Exp),
                              reads=["qp1"], writes=["qp1"],
                              tag=f"A{s}kx{t1}_{half}")
                        vop(lambda hsl=hsl, ktile=ktile:
                            VEC.scalar_tensor_tensor(
                                ktile[:, hsl], ktile[:, hsl], 0.0,
                                qp(1)[:, 0:512], OP.max, OP.add),
                            ["bC0", "qp1"], ["bC0"], f"A{s}kp{t1}_{half}")
                    vop(lambda t1=t1, half=half, hsl=hsl, vt=vt:
                        VEC.scalar_tensor_tensor(
                            vt[:, hsl], pb[2 + half][:, :],
                            rstd_tm[:, t1:t1 + 1],
                            bvbc[:, half * 512:(half + 1) * 512],
                            OP.mult, OP.add),
                        [PB[2 + half], "rstd_tm", "bC4"],
                        vkeys + [PB[2 + half]], f"A{s}ve{t1}_{half}")
                for h in range(16):
                    p_ = h // 2
                    bank = 4 + p_ // 4
                    out_ap = pb[bank][64 * (h % 2):64 * (h % 2) + 64,
                                      (p_ % 4) * 64:(p_ % 4) * 64 + 64]
                    rhs = vt[:, h * 64:(h + 1) * 64]
                    pg.op("pe", lambda h=h, out_ap=out_ap, rhs=rhs,
                          ktile=ktile: TEN.matmul(
                            out_ap, ktile[:, h * 64:(h + 1) * 64],
                            rhs, start=(h % 8 <= 1), stop=(h % 8 >= 6),
                            skip_group_check=True),
                          reads=["bC0"] + vkeys, writes=[PB[4 + p_ // 4]],
                          tag=f"A{s}kv{t1}_{h}")
                for p_ in range(8):
                    pg.op("pe", lambda p_=p_, t1=t1, ktile=ktile:
                          TEN.matmul(
                            pb[6][:, p_:p_ + 1],
                            ktile[:, p_ * P:(p_ + 1) * P],
                            onesT[:, 0:1],
                            start=(t1 == 0 and p_ == 0),
                            stop=(t1 == NT1 - 1 and p_ == 7),
                            skip_group_check=True),
                          reads=["bC0", "onesT"], writes=[PB[6]],
                          tag=f"A{s}dn{t1}_{p_}")
                for b_ in range(2):
                    vop(lambda b_=b_: VEC.tensor_add(
                            kv_acc[:, b_ * 256:(b_ + 1) * 256],
                            kv_acc[:, b_ * 256:(b_ + 1) * 256],
                            pb[4 + b_][:, 0:256]),
                        [PB[4 + b_], "kv_acc"], ["kv_acc", PB[4 + b_]],
                        f"A{s}a{t1}_{b_}")
            vop(lambda: VEC.tensor_copy(den_sb[:, :], pb[6][:, 0:8]),
                [PB[6]], ["den_sb", PB[6]], f"A{s}de")
            if is_l1:
                vop(lambda: VEC.reciprocal(den_sb[:, :], den_sb[:, :]),
                    ["den_sb"], ["den_sb"], f"A{s}dr")
                for p_ in range(8):
                    vop(lambda p_=p_: VEC.tensor_scalar_mul(
                            kv_acc[:, p_ * 64:(p_ + 1) * 64],
                            kv_acc[:, p_ * 64:(p_ + 1) * 64],
                            den_sb[:, p_:p_ + 1]),
                        ["kv_acc", "den_sb"], ["kv_acc"], f"A{s}sc{p_}")

        def emit_load_x(s):
            for t1 in range(NT1):
                nb = t1 % 2
                land = qp(2 + nb)
                dma("qs", land[:, 0:D], x_in[s, t1 * P:(t1 + 1) * P, :],
                    [], [KQ[2 + nb]], f"lx{s}_{t1}")
                for c in range(NCH):
                    bank = c % 4
                    pg.op("pe", lambda land=land, c=c, bank=bank:
                          TEN.transpose(pb[bank][:, 0:P],
                                        land[:, c * P:(c + 1) * P],
                                        ident[:, :]),
                          reads=[KQ[2 + nb], "ident"], writes=[PB[bank]],
                          tag=f"lxT{s}_{t1}_{c}")
                    eng = "sc" if c % 2 == 0 else "dve"
                    fn = SCA.copy if eng == "sc" else VEC.tensor_copy
                    vop(lambda fn=fn, c=c, t1=t1, bank=bank: fn(
                            xc(c)[:, t1 * P:(t1 + 1) * P], pb[bank][:, 0:P]),
                        [PB[bank]], [XK[c], PB[bank]], f"lxE{s}_{t1}_{c}",
                        eng=eng)

        def emit_store_x(s):
            for t1 in range(NT1):
                nb = t1 % 2
                land = qp(2 + nb)
                for c in range(NCH):
                    bank = 4 + c % 4
                    pg.op("pe", lambda c=c, t1=t1, bank=bank: TEN.transpose(
                            pb[bank][:, 0:P], xc(c)[:, t1 * P:(t1 + 1) * P],
                            ident[:, :]),
                          reads=[XK[c], "ident"], writes=[PB[bank]],
                          tag=f"sx{s}_{t1}_{c}")
                    eng = "sc" if c % 2 == 0 else "dve"
                    fn = SCA.copy if eng == "sc" else VEC.tensor_copy
                    vop(lambda fn=fn, land=land, c=c, bank=bank: fn(
                            land[:, c * P:(c + 1) * P], pb[bank][:, 0:P]),
                        [PB[bank]], [KQ[2 + nb], PB[bank]],
                        f"sxE{s}_{t1}_{c}", eng=eng)
                dma("qs", y_out[s, t1 * P:(t1 + 1) * P, :], land[:, 0:D],
                    [KQ[2 + nb]], [f"y{s}"], f"sy{s}_{t1}")

        def rtile(t5, dst, dst_keys, tag):
            rep_tiles(lambda tau: rstd_tm[:, tau:tau + 1],
                      [4 * t5 + j for j in range(4)], 7, dst, dst_keys,
                      ["rstd_tm"], tag)

        def phv(c):
            return bC(4 + c // 2)[:, (c % 2) * 512:(c % 2) * 512 + 512]

        def phk(c):
            return KC[4 + c // 2]

        def emit_l0_attn(s):
            emit_ln(f"n0_{s}")
            make_bbc(bkbc, "bC3", "bk", f"bb0{s}")
            make_bbc(bvbc, "bC4", "bv", f"bv0{s}")
            load_wres(wres, WK1, wkT_d, "wkT_d", 0, D, f"wk{s}_")
            load_wres(wres2, WK2, wvT_d, "wvT_d", 0, D, f"wv{s}_")
            emit_passA(s, is_l1=False)
            vop(lambda: VEC.memset(norm_lhs[:, :], 0.0), [], ["norm_lhs"],
                f"B{s}nz")
            for c in range(NCH):
                vop(lambda c=c: VEC.tensor_copy(
                        norm_lhs[0:64, c * 16 + 2 * c:c * 16 + 2 * c + 1],
                        den_sb[0:64, c:c + 1]),
                    ["den_sb"], ["norm_lhs"], f"B{s}n0{c}")
                vop(lambda c=c: VEC.tensor_copy(
                        norm_lhs[64:128,
                                 c * 16 + 2 * c + 1:c * 16 + 2 * c + 2],
                        den_sb[64:128, c:c + 1]),
                    ["den_sb"], ["norm_lhs"], f"B{s}n1{c}")
            load_wres(wres, WK1, wqT_d, "wqT_d", 0, D, f"wq{s}_")
            load_wres(wres2, WK2, woT_d, "woT_d", 0, D, f"wo{s}_")
            for t5 in range(NT5):
                sl = slice(t5 * 512, (t5 + 1) * 512)
                rtile(t5, qp(0)[:, 0:512], ["qp0"], f"B{s}r{t5}")
                for oc in range(NCH):
                    bank = oc
                    for c in range(NCH):
                        pg.op("pe", lambda oc=oc, c=c, sl=sl, bank=bank:
                              TEN.matmul(pb[bank][:, :],
                                         wres[c][:, oc * P:(oc + 1) * P],
                                         xc(c)[:, sl],
                                         start=(c == 0), stop=False),
                              reads=[WK1[c], XK[c]], writes=[PB[bank]],
                              tag=f"B{s}q{t5}_{oc}_{c}")
                    pg.op("pe", lambda oc=oc, sl=sl, bank=bank: TEN.matmul(
                            pb[bank][:, :], row("wq", oc * P, P),
                            nm_row[rowbase("wq"):rowbase("wq") + 1,
                                   sl],
                            start=False, stop=True),
                          reads=["rows", "nm_row"], writes=[PB[bank]],
                          tag=f"B{s}qm{t5}_{oc}")
                for oc in range(NCH):
                    bank = oc
                    vop(lambda bank=bank: VEC.tensor_mul(
                            qp(1)[:, 0:512], pb[bank][:, :],
                            qp(0)[:, 0:512]),
                        [PB[bank], "qp0"], ["qp1", PB[bank]],
                        f"B{s}e0{t5}_{oc}")
                    vop(lambda oc=oc: VEC.tensor_scalar(
                            qp(2)[:, 0:512], qp(1)[:, 0:512],
                            bq_c[:, oc:oc + 1], 0.0, OP.add, OP.min),
                        ["qp1", "bq_c"], ["qp2"], f"B{s}e1{t5}_{oc}")
                    pg.op("sc", lambda: SCA.activation(
                            qp(2)[:, 0:512], qp(2)[:, 0:512], AF.Exp),
                          reads=["qp2"], writes=["qp2"],
                          tag=f"B{s}e2{t5}_{oc}")
                    pg.op("sc", lambda oc=oc: SCA.activation(
                            qp(1)[:, 0:512], qp(1)[:, 0:512], AF.Identity,
                            bias=bq_c[:, oc:oc + 1], scale=1.0),
                          reads=["qp1", "bq_c"], writes=["qp1"],
                          tag=f"B{s}e3{t5}_{oc}")
                    vop(lambda oc=oc: VEC.scalar_tensor_tensor(
                            phv(oc), qp(1)[:, 0:512], 0.0, qp(2)[:, 0:512],
                            OP.max, OP.add),
                        ["qp1", "qp2"], [phk(oc)], f"B{s}e4{t5}_{oc}")
                for c in range(NCH):
                    pg.op("pe", lambda c=c: TEN.matmul(
                            pb[0][0:16, :], norm_lhs[:, c * 16:(c + 1) * 16],
                            phv(c), start=(c == 0), stop=(c == NCH - 1)),
                          reads=["norm_lhs", phk(c)],
                          writes=["pb0"], tag=f"B{s}nm{t5}_{c}")
                vop(lambda: VEC.tensor_single_scalar(
                        norm16t, pb[0][0:16, :], 1e-6, OP.max),
                    ["pb0"], ["norm16t", "pb0"], f"B{s}nx{t5}")
                vop(lambda: VEC.reciprocal(norm16t, norm16t),
                    ["norm16t"], ["norm16t"], f"B{s}nr{t5}")
                for c in range(NCH):
                    vop(lambda: VEC.memset(kvblk[:, :], 0.0), [], ["kvblk"],
                        f"B{s}bz{t5}_{c}")
                    vop(lambda c=c: VEC.tensor_copy(
                            kvblk[0:64, 0:64],
                            kv_acc[0:64, c * 64:c * 64 + 64]),
                        ["kv_acc"], ["kvblk"], f"B{s}b0{t5}_{c}")
                    vop(lambda c=c: VEC.tensor_copy(
                            kvblk[64:128, 64:128],
                            kv_acc[64:128, c * 64:c * 64 + 64]),
                        ["kv_acc"], ["kvblk"], f"B{s}b1{t5}_{c}")
                    pg.op("pe", lambda c=c: TEN.matmul(
                            pb[1][:, :], kvblk[:, :], phv(c),
                            start=True, stop=True),
                          reads=["kvblk", phk(c)], writes=["pb1"],
                          tag=f"B{s}o{t5}_{c}")
                    pg.op("pe", lambda c=c: TEN.matmul(
                            pb[2][:, :], sel16[:, c * P:(c + 1) * P],
                            norm16t, start=True, stop=True),
                          reads=["sel16", "norm16t"], writes=["pb2"],
                          tag=f"B{s}p{t5}_{c}")
                    pg.op("sc", lambda: SCA.copy(qp(1)[:, 0:512],
                                                 pb[2][:, :]),
                          reads=["pb2"], writes=["qp1", "pb2"],
                          tag=f"B{s}pe{t5}_{c}")
                    vop(lambda c=c: VEC.tensor_mul(
                            phv(c), pb[1][:, :], qp(1)[:, 0:512]),
                        ["pb1", "qp1"], [phk(c), "pb1"],
                        f"B{s}d{t5}_{c}")
                for oc in range(NCH):
                    bank = 3 + oc % 5
                    for c in range(NCH):
                        pg.op("pe", lambda oc=oc, c=c, bank=bank:
                              TEN.matmul(pb[bank][:, :],
                                         wres2[c][:, oc * P:(oc + 1) * P],
                                         phv(c), start=(c == 0),
                                         stop=(c == NCH - 1)),
                              reads=[WK2[c], phk(c)],
                              writes=[PB[bank]], tag=f"B{s}w{t5}_{oc}_{c}")
                    vop(lambda oc=oc, sl=sl, bank=bank:
                        VEC.scalar_tensor_tensor(
                            xc(oc)[:, sl], pb[bank][:, :],
                            bo_c[:, oc:oc + 1], xc(oc)[:, sl],
                            OP.add, OP.add),
                        [PB[bank], "bo_c", XK[oc]],
                        [XK[oc], PB[bank]], f"B{s}x{t5}_{oc}")

        def emit_l0_ffn(s):
            emit_ln(f"n1_{s}")
            load_wres(wres, WK1, weffT_d, "weffT_d", 0, D, f"wf{s}_")
            for t5 in range(NT5):
                sl = slice(t5 * 512, (t5 + 1) * 512)
                rtile(t5, qp(0)[:, 0:512], ["qp0"], f"F{s}r{t5}")
                for oc in range(NCH):
                    bank = oc % 6
                    for c in range(NCH):
                        pg.op("pe", lambda oc=oc, c=c, sl=sl, bank=bank:
                              TEN.matmul(pb[bank][:, :],
                                         wres[c][:, oc * P:(oc + 1) * P],
                                         xc(c)[:, sl],
                                         start=(c == 0), stop=False),
                              reads=[WK1[c], XK[c]], writes=[PB[bank]],
                              tag=f"F{s}m{t5}_{oc}_{c}")
                    pg.op("pe", lambda oc=oc, sl=sl, bank=bank: TEN.matmul(
                            pb[bank][:, :], row("weff", oc * P, P),
                            nm_row[rowbase("weff"):rowbase("weff") + 1,
                                   sl],
                            start=False, stop=True),
                          reads=["rows", "nm_row"], writes=[PB[bank]],
                          tag=f"F{s}t{t5}_{oc}")
                    vop(lambda oc=oc, bank=bank: VEC.tensor_mul(
                            phv(oc), pb[bank][:, :], qp(0)[:, 0:512]),
                        [PB[bank], "qp0"], [phk(oc), PB[bank]],
                        f"F{s}e{t5}_{oc}")
                for oc in range(NCH):
                    vop(lambda oc=oc, sl=sl: VEC.scalar_tensor_tensor(
                            xc(oc)[:, sl], phv(oc),
                            bf_c[:, oc:oc + 1], xc(oc)[:, sl],
                            OP.add, OP.add),
                        [phk(oc), "bf_c", XK[oc]], [XK[oc]],
                        f"F{s}x{t5}_{oc}")

        def emit_l1_attn(s):
            emit_ln(f"n2_{s}")
            make_bbc(bkbc, "bC3", "bk1", f"bb1{s}")
            make_bbc(bvbc, "bC4", "bv1", f"bv1{s}")
            load_wres(wres, WK1, qkvT_d, "qkvT_d", D, D, f"jk{s}_")
            load_wres(wres2, WK2, qkvT_d, "qkvT_d", 2 * D, D, f"jv{s}_")
            emit_passA(s, is_l1=True)
            l1tap = "l1" in debug_taps and s == 0
            if l1tap:
                dbg_kv = nc.dram_tensor("dbg_kv", [P, 1024], F32,
                                        kind="ExternalOutput").ap()
                dbg_den = nc.dram_tensor("dbg_den", [P, 8], F32,
                                         kind="ExternalOutput").ap()
                dbg_q = nc.dram_tensor("dbg_q", [P, 4096], F32,
                                       kind="ExternalOutput").ap()
                dbg_o = nc.dram_tensor("dbg_o", [P, 4096], F32,
                                       kind="ExternalOutput").ap()
                dbg_n = nc.dram_tensor("dbg_n", [P, 4096], F32,
                                       kind="ExternalOutput").ap()
                dbg_gh = nc.dram_tensor("dbg_gh", [64, 512], F32,
                                        kind="ExternalOutput").ap()
                dbg_gate = nc.dram_tensor("dbg_gate", [P, 512], F32,
                                          kind="ExternalOutput").ap()
                dma("qs", dbg_kv, kv_acc[:, :], ["kv_acc"], ["dKV"], "dKV")
                dma("qs", dbg_den, den_sb[:, :], ["den_sb"], ["dDE"], "dDE")
            load_wres(wres, WK1, qkvT_d, "qkvT_d", 0, D, f"jq{s}_")
            load_wres(wres2, WK2, woutT_d, "woutT_d", 0, D, f"ju{s}_")
            for c in range(NCH):
                dma("qs", bC(0)[:, c * 64:(c + 1) * 64],
                    g1T_d[c * P:(c + 1) * P, :], ["g1T_d"], ["bC0"],
                    f"G1{s}_{c}")
            dma("qs", bC(1)[64:128, 0:D], g2T_d, ["g2T_d"], ["bC1"],
                f"G2{s}")
            for t5 in range(NT5):
                sl = slice(t5 * 512, (t5 + 1) * 512)
                rtile(t5, qp(0)[:, 0:512], ["qp0"], f"C{s}r{t5}")
                for oc in range(NCH):
                    bank = oc
                    for c in range(NCH):
                        pg.op("pe", lambda oc=oc, c=c, sl=sl, bank=bank:
                              TEN.matmul(pb[bank][:, :],
                                         wres[c][:, oc * P:(oc + 1) * P],
                                         xc(c)[:, sl],
                                         start=(c == 0), stop=False),
                              reads=[WK1[c], XK[c]], writes=[PB[bank]],
                              tag=f"C{s}q{t5}_{oc}_{c}")
                    pg.op("pe", lambda oc=oc, sl=sl, bank=bank: TEN.matmul(
                            pb[bank][:, :], row("q1", oc * P, P),
                            nm_row[rowbase("q1"):rowbase("q1") + 1,
                                   sl],
                            start=False, stop=True),
                          reads=["rows", "nm_row"], writes=[PB[bank]],
                          tag=f"C{s}qm{t5}_{oc}")
                for oc in range(NCH):
                    bank = oc
                    vop(lambda bank=bank: VEC.tensor_mul(
                            qp(1)[:, 0:512], pb[bank][:, :],
                            qp(0)[:, 0:512]),
                        [PB[bank], "qp0"], ["qp1", PB[bank]],
                        f"C{s}e0{t5}_{oc}")
                    pg.op("sc", lambda oc=oc: SCA.activation(
                            phv(oc), qp(1)[:, 0:512], AF.Silu,
                            bias=bq1_c[:, oc:oc + 1], scale=1.0),
                          reads=["qp1", "bq1_c"], writes=[phk(oc)],
                          tag=f"C{s}e1{t5}_{oc}")
                if l1tap and t5 == 0:
                    dma("qs", dbg_q, bigC[:, 4096:8192],
                        [phk(c_) for c_ in range(NCH)], ["dQ"], "dQ")
                for c in range(NCH):
                    vop(lambda: VEC.memset(kvblk[:, :], 0.0), [], ["kvblk"],
                        f"C{s}bz{t5}_{c}")
                    vop(lambda c=c: VEC.tensor_copy(
                            kvblk[0:64, 0:64],
                            kv_acc[0:64, c * 64:c * 64 + 64]),
                        ["kv_acc"], ["kvblk"], f"C{s}b0{t5}_{c}")
                    vop(lambda c=c: VEC.tensor_copy(
                            kvblk[64:128, 64:128],
                            kv_acc[64:128, c * 64:c * 64 + 64]),
                        ["kv_acc"], ["kvblk"], f"C{s}b1{t5}_{c}")
                    pg.op("pe", lambda c=c: TEN.matmul(
                            pb[0][:, :], kvblk[:, :], phv(c),
                            start=True, stop=True),
                          reads=["kvblk", phk(c)], writes=["pb0"],
                          tag=f"C{s}o{t5}_{c}")
                    vop(lambda c=c: VEC.tensor_copy(phv(c), pb[0][:, :]),
                        ["pb0"], [phk(c), "pb0"], f"C{s}oe{t5}_{c}")
                if l1tap and t5 == 0:
                    dma("qs", dbg_o, bigC[:, 4096:8192],
                        [phk(c_) for c_ in range(NCH)], ["dO"], "dO")
                for c in range(NCH):
                    pg.op("sc", lambda c=c: SCA.activation(
                            qp(1)[:, 0:512], phv(c), AF.Square),
                          reads=[phk(c)], writes=["qp1"],
                          tag=f"C{s}s{t5}_{c}")
                    pg.op("pe", lambda c=c: TEN.matmul(
                            pb[2][:, :], onesT[:, :], phv(c),
                            start=(c == 0), stop=(c == NCH - 1)),
                          reads=[phk(c), "onesT"], writes=["pb2"],
                          tag=f"C{s}sm{t5}_{c}")
                    pg.op("pe", lambda c=c: TEN.matmul(
                            pb[3][:, :], onesT[:, :], qp(1)[:, 0:512],
                            start=(c == 0), stop=(c == NCH - 1)),
                          reads=["qp1", "onesT"], writes=["pb3"],
                          tag=f"C{s}sq{t5}_{c}")
                vop(lambda: VEC.tensor_scalar_mul(qp(2)[:, 0:512],
                                                  pb[2][:, :], 1.0 / D),
                    ["pb2"], ["qp2", "pb2"], f"C{s}lm{t5}")
                vop(lambda: VEC.tensor_mul(qp(1)[:, 0:512], qp(2)[:, 0:512],
                                           qp(2)[:, 0:512]),
                    ["qp2"], ["qp1"], f"C{s}l2{t5}")
                vop(lambda: VEC.scalar_tensor_tensor(
                        qp(1)[:, 0:512], pb[3][:, :], 1.0 / D,
                        qp(1)[:, 0:512], OP.mult, OP.subtract),
                    ["pb3", "qp1"], ["qp1", "pb3"], f"C{s}lv{t5}")
                vop(lambda: VEC.tensor_scalar_add(qp(1)[:, 0:512],
                                                  qp(1)[:, 0:512], 1e-5),
                    ["qp1"], ["qp1"], f"C{s}le{t5}")
                vop(lambda: VEC.reciprocal(qp(1)[:, 0:512], qp(1)[:, 0:512]),
                    ["qp1"], ["qp1"], f"C{s}lr{t5}")
                pg.op("sc", lambda: SCA.activation(
                        qp(1)[:, 0:512], qp(1)[:, 0:512], AF.Sqrt),
                      reads=["qp1"], writes=["qp1"], tag=f"C{s}ls{t5}")
                vop(lambda: VEC.scalar_tensor_tensor(
                        qp(2)[:, 0:512], qp(2)[:, 0:512], -1.0,
                        qp(1)[:, 0:512], OP.mult, OP.mult),
                    ["qp2", "qp1"], ["qp2"], f"C{s}ln{t5}")
                for c in range(NCH):
                    vop(lambda c=c: GPS.tensor_mul(
                            qp(3)[:, 0:512], phv(c), qp(1)[:, 0:512]),
                        [phk(c), "qp1"], ["qp3"], f"C{s}a0{t5}_{c}",
                        eng="gp")
                    vop(lambda: GPS.tensor_add(
                            qp(3)[:, 0:512], qp(3)[:, 0:512],
                            qp(2)[:, 0:512]),
                        ["qp3", "qp2"], ["qp3"], f"C{s}a1{t5}_{c}",
                        eng="gp")
                    vop(lambda c=c: VEC.tensor_scalar(
                            phv(c), qp(3)[:, 0:512], lng_c[:, c:c + 1],
                            lnb_c[:, c:c + 1], OP.mult, OP.add),
                        ["qp3", "lng_c", "lnb_c"], [phk(c)],
                        f"C{s}a2{t5}_{c}")
                if l1tap and t5 == 0:
                    dma("qs", dbg_n, bigC[:, 4096:8192],
                        [phk(c_) for c_ in range(NCH)], ["dN"], "dN")
                for c in range(NCH):
                    pg.op("pe", lambda c=c, sl=sl: TEN.matmul(
                            pb[4][0:64, :], bC(0)[:, c * 64:(c + 1) * 64],
                            xc(c)[:, sl], start=(c == 0), stop=False),
                          reads=["bC0", XK[c]], writes=["pb4"],
                          tag=f"C{s}g{t5}_{c}")
                pg.op("pe", lambda sl=sl: TEN.matmul(
                        pb[4][0:64, :], row("g1", 0, 64),
                        nm_row[rowbase("g1"):rowbase("g1") + 1, sl],
                        start=False, stop=True),
                      reads=["rows", "nm_row"], writes=["pb4"],
                      tag=f"C{s}gm{t5}")
                vop(lambda: VEC.tensor_mul(ghv, pb[4][0:64, :],
                                           qp(0)[0:64, 0:512]),
                    ["pb4", "qp0"], ["gh", "pb4"], f"C{s}ge{t5}")
                vop(lambda: VEC.tensor_scalar_add(ghv, ghv, bg1_c[:, 0:1]),
                    ["gh", "bg1_c"], ["gh"], f"C{s}gb{t5}")
                if l1tap and t5 == 0:
                    dma("qs", dbg_gh, ghv, ["gh"], ["dGH"], "dGH")
                for oc in range(NCH):
                    pg.op("pe", lambda oc=oc: TEN.matmul(
                            pb[5][:, :], bC(1)[64:128, oc * P:(oc + 1) * P],
                            ghv, start=True, stop=True),
                          reads=["bC1", "gh"], writes=["pb5"],
                          tag=f"C{s}g2{t5}_{oc}")
                    pg.op("sc", lambda oc=oc: SCA.activation(
                            qp(3)[:, 0:512], pb[5][:, :], AF.Sigmoid,
                            bias=bg2_c[:, oc:oc + 1], scale=1.0),
                          reads=["pb5", "bg2_c"], writes=["qp3", "pb5"],
                          tag=f"C{s}g3{t5}_{oc}")
                    if l1tap and t5 == 0 and oc == 0:
                        dma("qs", dbg_gate, qp(3)[:, 0:512], ["qp3"],
                            ["dGA"], "dGA")
                    vop(lambda oc=oc: VEC.tensor_mul(
                            phv(oc), phv(oc), qp(3)[:, 0:512]),
                        [phk(oc), "qp3"], [phk(oc)], f"C{s}g4{t5}_{oc}")
                for oc in range(NCH):
                    bank = oc % 6
                    for c in range(NCH):
                        pg.op("pe", lambda oc=oc, c=c, bank=bank:
                              TEN.matmul(pb[bank][:, :],
                                         wres2[c][:, oc * P:(oc + 1) * P],
                                         phv(c), start=(c == 0),
                                         stop=(c == NCH - 1)),
                              reads=[WK2[c], phk(c)],
                              writes=[PB[bank]], tag=f"C{s}w{t5}_{oc}_{c}")
                    vop(lambda oc=oc, sl=sl, bank=bank:
                        VEC.scalar_tensor_tensor(
                            xc(oc)[:, sl], pb[bank][:, :],
                            bout_c[:, oc:oc + 1], xc(oc)[:, sl],
                            OP.add, OP.add),
                        [PB[bank], "bout_c", XK[oc]],
                        [XK[oc], PB[bank]], f"C{s}x{t5}_{oc}")

        def emit_l1_ffn(s):
            emit_ln(f"n3_{s}")
            qint = bigC[:, 0:4096].bitcast(BF16)[:, 0:NCH * 768]
            qint_c = [qint[:, c * 768:(c + 1) * 768] for c in range(NCH)]
            q2v = bigC[:, 0:8192].bitcast(BF16)[:, 0:32 * 512]
            for q in range(NQ):
                tl_lo = 4 * q
                tl_hi = min(NT1, 4 * q + 5)
                tls = list(range(tl_lo, tl_hi))
                ntl = len(tls)
                base = tl_lo * P
                if q == 0:
                    vop(lambda: VEC.memset(
                            qint.rearrange("p (c t) -> p c t",
                                           c=NCH)[:, :, 0:1], 0.0),
                        [], ["bC0", "bC1", "bC2"], f"D{s}qz{q}")
                    vop(lambda: VEC.memset(qp(3)[:, 0:1], 0.0),
                        [], ["qp3"], f"D{s}iz{q}")
                else:
                    vop(lambda: VEC.tensor_copy(
                            qint.rearrange("p (c t) -> p c t",
                                           c=NCH)[:, :, 0:1],
                            qcarry[:, 0:8]),
                        ["qcarry"], ["bC0", "bC1", "bC2"], f"D{s}qc{q}")
                    vop(lambda: VEC.tensor_copy(qp(3)[:, 0:1],
                                                icarry[:, :]),
                        ["icarry"], ["qp3"], f"D{s}ic{q}")
                for g4 in range(0, ntl, 4):
                    sub = tls[g4:g4 + 4]
                    rep_tiles(lambda tau: rstd_tm[:, tau:tau + 1], sub, 7,
                              qp(0)[:, g4 * P:g4 * P + len(sub) * P],
                              ["qp0"], ["rstd_tm"], f"D{s}r{q}_{g4}")
                for g4 in range(0, ntl, 4):
                    nsub = min(ntl, g4 + 4) - g4
                    ones_rep(nm_row[0:1, base + g4 * P:
                                    base + (g4 + nsub) * P], nsub * P, 7,
                             qp(1)[:, g4 * P:(g4 + nsub) * P], ["qp1"],
                             f"D{s}n{q}_{g4}")
                vop(lambda ntl=ntl: VEC.tensor_mul(
                        qp(1)[:, 0:ntl * P], qp(1)[:, 0:ntl * P],
                        qp(0)[:, 0:ntl * P]),
                    ["qp1", "qp0"], ["qp1"], f"D{s}nr{q}")
                for j in range(ntl):
                    tau = tls[j]
                    jk = f"bB{j // 2}"
                    rj = qp(0)[:, j * P:(j + 1) * P]
                    nj = qp(1)[:, j * P:(j + 1) * P]
                    tdst = bigB[:, j * 1024:(j + 1) * 1024].rearrange(
                        "p (c t) -> p c t", c=NCH)
                    vop(lambda tau=tau, rj=rj, tdst=tdst: VEC.tensor_tensor(
                            tdst, xstr(tau),
                            rj.unsqueeze(1).broadcast_to((P, NCH, P)),
                            OP.mult),
                        XK + ["qp0"], [jk], f"D{s}x0{q}_{j}")
                    vop(lambda nj=nj, tdst=tdst: VEC.tensor_tensor(
                            tdst, tdst,
                            nj.unsqueeze(1).broadcast_to((P, NCH, P)),
                            OP.add),
                        [jk, "qp1"], [jk], f"D{s}x1{q}_{j}")
                    tview = bigB[:, j * 1024:(j + 1) * 1024].rearrange(
                        "p (c t) -> p t c", t=P)
                    vop(lambda tview=tview, jk=jk: VEC.reduce_max(
                            scr128[:, 0:P], tview, AX,
                            apply_absolute_value=True),
                        [jk], ["scr128"], f"D{s}am{q}_{j}")
                    for jj in range(4):
                        for kk in range(4):
                            vop(lambda jj=jj, kk=kk: VEC.transpose(
                                    qp(4)[kk * 32:(kk + 1) * 32,
                                          jj * 32:(jj + 1) * 32],
                                    scr128[jj * 32:(jj + 1) * 32,
                                           kk * 32:(kk + 1) * 32]),
                                ["scr128"], ["qp4"], f"D{s}at{q}_{j}_{jj}{kk}")
                    vop(lambda: VEC.reduce_max(scr128[:, 1:2],
                                               qp(4)[:, 0:P], AX),
                        ["qp4"], ["scr128"], f"D{s}ar{q}_{j}")
                    vop(lambda j=j: VEC.tensor_copy(
                            amax_q[:, j:j + 1], scr128[:, 1:2]),
                        ["scr128"], ["amax_q"], f"D{s}ac{q}_{j}")
                vop(lambda ntl=ntl: VEC.tensor_single_scalar(
                        amax_q[:, 0:ntl], amax_q[:, 0:ntl], 1e-5, OP.max),
                    ["amax_q"], ["amax_q"], f"D{s}sc{q}")
                vop(lambda ntl=ntl: VEC.tensor_scalar_mul(
                        den_sb[:, 0:ntl], amax_q[:, 0:ntl], 1.0 / 127.0),
                    ["amax_q"], ["den_sb"], f"D{s}si{q}")
                vop(lambda ntl=ntl: VEC.reciprocal(
                        amax_q[:, 0:ntl], den_sb[:, 0:ntl]),
                    ["den_sb"], ["amax_q"], f"D{s}sr{q}")
                for g4 in range(0, ntl, 4):
                    sub = list(range(g4, min(ntl, g4 + 4)))
                    rep_tiles(lambda j: amax_q[:, j:j + 1], sub, 7,
                              qp(2)[:, 1 + g4 * P:1 + g4 * P + len(sub) * P],
                              ["qp2"], ["amax_q"], f"D{s}sb{q}_{g4}")
                    rep_tiles(lambda j: den_sb[:, j:j + 1], sub, 6,
                              qp(3)[:, 1 + g4 * P:1 + g4 * P + len(sub) * P],
                              ["qp3"], ["den_sb"], f"D{s}ib{q}_{g4}")
                if "amax1" in debug_taps and q == 0:
                    dbg1 = nc.dram_tensor("dbg_amax", [P, 4], F32,
                                          kind="ExternalOutput").ap()
                    dbg2 = nc.dram_tensor("dbg_s", [P, 512], F32,
                                          kind="ExternalOutput").ap()

                    dma("qs", dbg1, amax_q[:, 0:4], ["amax_q"], ["dbgA"],
                        "dbgA")
                    dma("qs", dbg2, qpool[:, 2 * 1024 + 1:2 * 1024 + 513],
                        ["qp2"], ["dbgS"], "dbgS")
                    dma("qs", dbg3, bigB[:, 0:8192], KB, ["dbgX"], "dbgX") if False else None
                for j in range(ntl):
                    jk = f"bB{j // 2}"
                    sj = qp(2)[:, 1 + j * P:1 + (j + 1) * P]
                    t_f = bigB[:, j * 1024:(j + 1) * 1024]
                    tdst = t_f.rearrange("p (c t) -> p c t", c=NCH)
                    t_i = bigB[:, 6144:6144 + NCH * P].bitcast(I32)[
                        :, 0:NCH * P]
                    t_d = bigB[:, 7168:7168 + NCH * P]
                    t_g = bigB[:, 8192:8192 + NCH * P]
                    vop(lambda sj=sj, tdst=tdst: VEC.tensor_tensor(
                            tdst, tdst,
                            sj.unsqueeze(1).broadcast_to((P, NCH, P)),
                            OP.mult),
                        [jk, "qp2"], [jk], f"D{s}c2{q}_{j}")
                    vop(lambda t_i=t_i, t_f=t_f: GPS.tensor_copy(t_i, t_f),
                        [jk], ["bB3"], f"D{s}c3{q}_{j}", eng="gp")
                    vop(lambda t_d=t_d, t_f=t_f, t_i=t_i: GPS.tensor_tensor(
                            t_d, t_f, t_i, OP.subtract),
                        [jk, "bB3"], ["bB3"], f"D{s}c4{q}_{j}", eng="gp")
                    vop(lambda t_g=t_g, t_d=t_d: VEC.tensor_single_scalar(
                            t_g, t_d, 0.5, OP.is_gt),
                        ["bB3"], ["bB4"], f"D{s}c5{q}_{j}")
                    vop(lambda t_d=t_d: VEC.tensor_single_scalar(
                            t_d, t_d, -0.5, OP.is_lt),
                        ["bB3"], ["bB3"], f"D{s}c6{q}_{j}")
                    vop(lambda t_f=t_f, t_g=t_g, t_i=t_i:
                        VEC.scalar_tensor_tensor(t_f, t_g, 1.0, t_i,
                                                 OP.mult, OP.add),
                        ["bB4", "bB3"], [jk], f"D{s}c7{q}_{j}")
                    qdst = qint.rearrange("p (c t) -> p c t",
                                          c=NCH)[:, :, 1 + j * P:
                                                 1 + (j + 1) * P]
                    vop(lambda qdst=qdst, t_f=t_f, t_d=t_d:
                        VEC.scalar_tensor_tensor(
                            qdst, t_d.rearrange("p (c t) -> p c t", c=NCH),
                            -1.0, t_f.rearrange("p (c t) -> p c t", c=NCH),
                            OP.mult, OP.add),
                        ["bB3", jk], ["bC0", "bC1", "bC2"],
                        f"D{s}c8{q}_{j}")
                if "qint" in debug_taps and q == 0:
                    dbg4 = nc.dram_tensor("dbg_qint", [NCH, P, 256], F32,
                                          kind="ExternalOutput").ap()
                    for c_ in range(NCH):
                        dma("qs", dbg4[c_], bigC[:, c_ * 384:c_ * 384 + 256],
                            ["bC0", "bC1", "bC2", "bC3"], ["dbgQ"],
                            f"dbgQ{c_}")
                off_main = 0
                n_edge = 2 if q < NQ - 1 else 1

                def emit_hc_front(hc, q=q, n_edge=n_edge,
                                  off_main=off_main):
                    pp = hc % 2
                    hkey = f"hg{hc}"
                    uk = f"qp{pp}"
                    uvw = qp(pp)[:, 0:514]
                    dma("qs", t1s[:, pp * D:(pp + 1) * D],
                        t1T_d[:, hc * P:(hc + 1) * P].rearrange(
                            "(c p) n -> p c n", p=P),
                        ["t1T_d"], [f"t1s{pp}"], f"D{s}h{q}_{hc}l")
                    bank = hc % 7
                    for c in range(NCH):
                        pg.op("pe", lambda c=c, pp=pp, bank=bank,
                              off_main=off_main: TEN.matmul(
                                pb[bank][:, :],
                                t1s[:, pp * D + c * P:pp * D + (c + 1) * P],
                                qint_c[c][:, off_main:off_main + 512],
                                start=(c == 0), stop=(c == NCH - 1)),
                              reads=[f"t1s{pp}", "bC0", "bC1", "bC2"],
                              writes=[PB[bank]], tag=f"D{s}h{q}_{hc}_{c}")
                    if n_edge > 0:
                        for c in range(NCH):
                            pg.op("pe", lambda c=c, pp=pp, hc=hc,
                                  n_edge=n_edge: TEN.matmul(
                                    pb[7][:, hc * 2:hc * 2 + n_edge],
                                    t1s[:, pp * D + c * P:
                                        pp * D + (c + 1) * P],
                                    qint_c[c][:, 512:512 + n_edge],
                                    start=(c == 0), stop=(c == NCH - 1),
                                    skip_group_check=True),
                                  reads=[f"t1s{pp}", "bC0", "bC1", "bC2"],
                                  writes=["pb7"], tag=f"D{s}he{q}_{hc}_{c}")
                    hblk = hga[:, hc * 512:(hc + 1) * 512]
                    if q == NQ - 1 and hc < 2:
                        vop(lambda uvw=uvw: VEC.memset(uvw[:, 513:514], 0.0),
                            [], [uk], f"D{s}uz{q}_{hc}")
                    vop(lambda uvw=uvw, bank=bank: VEC.tensor_mul(
                            uvw[:, 0:512], pb[bank][:, :],
                            qp(3)[:, 0:512]),
                        [PB[bank], "qp3"], [uk, PB[bank]],
                        f"D{s}hev{q}_{hc}")
                    if n_edge > 0:
                        vop(lambda uvw=uvw, hc=hc, n_edge=n_edge:
                            VEC.tensor_mul(
                                uvw[:, 512:512 + n_edge],
                                pb[7][:, hc * 2:hc * 2 + n_edge],
                                qp(3)[:, 512:512 + n_edge]),
                            ["pb7", "qp3"], [uk, "pb7"],
                            f"D{s}hee{q}_{hc}")
                    gv = hblk
                    w0 = convw_c[:, 3 * hc:3 * hc + 1]
                    w1 = convw_c[:, 3 * hc + 1:3 * hc + 2]
                    w2 = convw_c[:, 3 * hc + 2:3 * hc + 3]
                    vop(lambda gv=gv, uvw=uvw, w0=w0: GPS.tensor_tensor(
                            gv, uvw[:, 0:512],
                            w0.broadcast_to((P, 512)), OP.mult),
                        [uk, "convw_c"], [hkey], f"D{s}cv0{q}_{hc}",
                        eng="gp")
                    vop(lambda gv=gv, hc=hc: GPS.tensor_tensor(
                            gv, gv,
                            convb_c[:, hc:hc + 1].broadcast_to((P, 512)),
                            OP.add),
                        [hkey, "convb_c"], [hkey], f"D{s}cvb{q}_{hc}",
                        eng="gp")
                    vop(lambda gv=gv, uvw=uvw, w1=w1:
                        VEC.scalar_tensor_tensor(
                            gv, uvw[:, 1:513], w1, gv, OP.mult, OP.add),
                        [uk, "convw_c", hkey], [hkey], f"D{s}cv1{q}_{hc}")
                    vop(lambda gv=gv, uvw=uvw, w2=w2:
                        VEC.scalar_tensor_tensor(
                            gv, uvw[:, 2:514], w2, gv, OP.mult, OP.add),
                        [uk, "convw_c", hkey], [hkey], f"D{s}cv2{q}_{hc}")

                def emit_silu_group(g4, q=q):
                    keys = [f"hg{h}" for h in range(4 * g4, 4 * g4 + 4)]
                    gvg = hga[:, g4 * 2048:(g4 + 1) * 2048]
                    pg.op("sc", lambda gvg=gvg: SCA.activation(
                            gvg, gvg, AF.Silu),
                          reads=keys, writes=keys, tag=f"D{s}sl{q}_{g4}")

                def emit_hc_tail(hc, q=q):
                    hkey = f"hg{hc}"
                    gv = hga[:, hc * 512:(hc + 1) * 512]
                    treeB = bC(3)[:, 0:512]
                    if hc == 0:
                        vop(lambda gv=gv: VEC.tensor_copy(
                                qp(4)[:, 0:512], gv),
                            [hkey], ["qp4"], f"D{s}mf{q}_{hc}")
                        vop(lambda gv=gv, treeB=treeB:
                            VEC.tensor_scalar_mul(treeB, gv, -1.0),
                            [hkey], ["bC3"], f"D{s}mg{q}_{hc}")
                    else:
                        vop(lambda gv=gv: VEC.tensor_tensor(
                                qp(4)[:, 0:512], qp(4)[:, 0:512], gv,
                                OP.max),
                            [hkey, "qp4"], ["qp4"], f"D{s}mf{q}_{hc}")
                        vop(lambda gv=gv, treeB=treeB:
                            VEC.scalar_tensor_tensor(
                                treeB, gv, -1.0, treeB, OP.mult, OP.max),
                            [hkey, "bC3"], ["bC3"], f"D{s}mg{q}_{hc}")

                for g4 in range(8):
                    for hc in range(4 * g4, 4 * g4 + 4):
                        emit_hc_front(hc)
                    emit_silu_group(g4)
                    if g4 >= 1:
                        for hc in range(4 * (g4 - 1), 4 * g4 - 4 + 4):
                            emit_hc_tail(hc)
                for hc in range(28, 32):
                    emit_hc_tail(hc)
                if q < NQ - 1:
                    vop(lambda: VEC.tensor_copy(
                            qcarry[:, 0:8],
                            qint.rearrange("p (c t) -> p c t",
                                           c=NCH)[:, :, 512:513]),
                        ["bC0", "bC1", "bC2"], ["qcarry"], f"D{s}qs{q}")
                    vop(lambda: VEC.tensor_copy(icarry[:, :],
                                                qp(3)[:, 512:513]),
                        ["qp3"], ["icarry"], f"D{s}is{q}")
                vop(lambda: VEC.tensor_tensor(
                        qp(4)[:, 0:512], qp(4)[:, 0:512], bC(3)[:, 0:512],
                        OP.max),
                    ["qp4", "bC3"], ["qp4"], f"D{s}mj{q}")
                for j4 in range(4):
                    for jj in range(4):
                        for kk in range(4):
                            vop(lambda j4=j4, jj=jj, kk=kk: VEC.transpose(
                                    scr128[kk * 32:(kk + 1) * 32,
                                           jj * 32:(jj + 1) * 32],
                                    qp(4)[jj * 32:(jj + 1) * 32,
                                          j4 * P + kk * 32:
                                          j4 * P + (kk + 1) * 32]),
                                ["qp4"], ["scr128"],
                                f"D{s}a2{q}_{j4}_{jj}{kk}")
                    vop(lambda: VEC.reduce_max(scr128[:, 1:2],
                                               scr128[:, 0:P], AX),
                        ["scr128"], ["scr128"], f"D{s}a2r{q}_{j4}")
                    vop(lambda j4=j4: VEC.tensor_copy(
                            amax_q[:, j4:j4 + 1], scr128[:, 1:2]),
                        ["scr128"], ["amax_q"], f"D{s}a2c{q}_{j4}")
                vop(lambda: VEC.tensor_single_scalar(
                        amax_q[:, 0:4], amax_q[:, 0:4], 1e-5, OP.max),
                    ["amax_q"], ["amax_q"], f"D{s}2c{q}")
                vop(lambda: VEC.tensor_scalar_mul(
                        den_sb[:, 0:4], amax_q[:, 0:4], swr[:, 0:1]),
                    ["amax_q", "swr"], ["den_sb"], f"D{s}2i{q}")
                vop(lambda: VEC.reciprocal(amax_q[:, 4:8], amax_q[:, 0:4]),
                    ["amax_q"], ["amax_q"], f"D{s}2r{q}")
                vop(lambda: VEC.tensor_scalar_mul(
                        amax_q[:, 4:8], amax_q[:, 4:8], 127.0),
                    ["amax_q"], ["amax_q"], f"D{s}2s{q}")
                rep_tiles(lambda j: amax_q[:, 4 + j:5 + j], [0, 1, 2, 3], 7,
                          qp(2)[:, 0:512], ["qp2"], ["amax_q"], f"D{s}2b{q}")
                rep_tiles(lambda j: den_sb[:, j:j + 1], [0, 1, 2, 3], 6,
                          qp(3)[:, 0:512], ["qp3"], ["den_sb"], f"D{s}2v{q}")
                def qs0(b2):
                    pr = b2 % 2
                    tk = f"qp{pr}"
                    gview = hga[:, b2 * 512:(b2 + 1) * 512]
                    gk = f"hg{b2}"
                    ta = qp(pr)[:, 0:512]
                    ti = qp(pr)[:, 512:1024].bitcast(I32)[:, 0:512]
                    vop(lambda gview=gview, ta=ta: VEC.tensor_mul(
                            ta, gview, qp(2)[:, 0:512]),
                        [gk, "qp2"], [tk], f"D{s}z0{q}_{b2}")
                    vop(lambda ta=ta, ti=ti: VEC.tensor_copy(ti, ta),
                        [tk], [tk], f"D{s}z1{q}_{b2}")

                def qs1(b2):
                    pr = b2 % 2
                    tk = f"qp{pr}"
                    gview = hga[:, b2 * 512:(b2 + 1) * 512]
                    gk = f"hg{b2}"
                    ta = qp(pr)[:, 0:512]
                    ti = qp(pr)[:, 512:1024].bitcast(I32)[:, 0:512]
                    vop(lambda ta=ta, ti=ti, gview=gview: GPS.tensor_tensor(
                            gview, ta, ti, OP.subtract),
                        [tk, gk], [gk], f"D{s}z2{q}_{b2}", eng="gp")

                def qs2(b2):
                    pr = b2 % 2
                    td = hga[:, b2 * 512:(b2 + 1) * 512]
                    gk = f"hg{b2}"
                    tgk = "qp2b" if pr == 0 else "qp3b"
                    tg = (qp(2) if pr == 0 else qp(3))[:, 512:1024]
                    pg.op("sc", lambda td=td, tg=tg: SCA.activation(
                            tg, td, AF.Sign, bias=-0.5, scale=1.0),
                          reads=[gk, "cst2"], writes=[tgk],
                          tag=f"D{s}z3{q}_{b2}")
                    pg.op("sc", lambda tg=tg: SCA.activation(
                            tg, tg, AF.Relu),
                          reads=[tgk], writes=[tgk], tag=f"D{s}z3b{q}_{b2}")
                    pg.op("sc", lambda td=td: SCA.activation(
                            td, td, AF.Sign, bias=0.5, scale=1.0),
                          reads=[gk, "cst2"], writes=[gk],
                          tag=f"D{s}z4{q}_{b2}")
                    pg.op("sc", lambda td=td: SCA.activation(
                            td, td, AF.Relu, bias=0.0, scale=-1.0),
                          reads=[gk, "negone"], writes=[gk],
                          tag=f"D{s}z4b{q}_{b2}")

                def qs3(b2):
                    pr = b2 % 2
                    tk = f"qp{pr}"
                    tl = hga[:, b2 * 512:(b2 + 1) * 512]
                    gk = f"hg{b2}"
                    tgk = "qp2b" if pr == 0 else "qp3b"
                    tg = (qp(2) if pr == 0 else qp(3))[:, 512:1024]
                    ta = qp(pr)[:, 0:512]
                    ti = qp(pr)[:, 512:1024].bitcast(I32)[:, 0:512]
                    vop(lambda ta=ta, tg=tg, ti=ti:
                        VEC.scalar_tensor_tensor(
                            ta, tg, 1.0, ti, OP.mult, OP.add),
                        [tgk, tk], [tk], f"D{s}z5{q}_{b2}")
                    vop(lambda ta=ta, tl=tl, b2=b2:
                        VEC.scalar_tensor_tensor(
                            q2v[:, b2 * 512:(b2 + 1) * 512], tl, -1.0, ta,
                            OP.mult, OP.add),
                        [gk, tk], [f"bC{b2 // 4}"], f"D{s}z6{q}_{b2}")

                for i in range(34):
                    if i >= 2:
                        qs3(i - 2)
                    if i < 32:
                        qs0(i)
                    if 1 <= i <= 32:
                        qs1(i - 1)
                        qs2(i - 1)
                for kg in range(32):
                    wp = kg % 2
                    dma("qs", t2s[:, wp * D:(wp + 1) * D],
                        t2T_d[kg * P:(kg + 1) * P, :],
                        ["t2T_d"], [f"t2s{wp}"], f"D{s}w2l{q}_{kg}")
                    for oc in range(NCH):
                        pg.op("pe", lambda oc=oc, kg=kg, wp=wp:
                              TEN.matmul(
                                pb[oc][:, :],
                                t2s[:, wp * D + oc * P:
                                    wp * D + (oc + 1) * P],
                                q2v[:, kg * 512:(kg + 1) * 512],
                                start=(kg == 0), stop=(kg == 31)),
                              reads=[f"t2s{wp}", f"bC{kg // 4}"],
                              writes=[PB[oc]],
                              tag=f"D{s}w2{q}_{oc}_{kg}")
                sl = slice(q * 512, (q + 1) * 512)
                for oc in range(NCH):
                    pr = oc % 2
                    vop(lambda oc=oc, pr=pr: VEC.tensor_mul(
                            qp(pr)[:, 0:512], pb[oc][:, :],
                            qp(3)[:, 0:512]),
                        [PB[oc], "qp3"], [f"qp{pr}", PB[oc]],
                        f"D{s}w2e{q}_{oc}")
                    vop(lambda oc=oc, sl=sl, pr=pr: VEC.tensor_add(
                            xc(oc)[:, sl], xc(oc)[:, sl],
                            qp(pr)[:, 0:512]),
                        [f"qp{pr}", XK[oc]], [XK[oc]], f"D{s}w2x{q}_{oc}")

        for s in range(B):
            emit_load_x(s)
            if n_blocks >= 1:
                emit_l0_attn(s)
            if n_blocks >= 2:
                emit_l0_ffn(s)
            if n_blocks >= 3:
                emit_l1_attn(s)
            if n_blocks >= 4:
                emit_l1_ffn(s)
            emit_store_x(s)

        pg.op("gp", lambda: GPS.nop(),
              reads=[f"y{s}" for s in range(B)], writes=[], tag="final")

        nc.all_engine_barrier()
        r = nc._kernel_sem_range
        clear = range(r.start + 3, r.stop)
        nc.gpsimd.dma_reset(clear)
        nc.gpsimd.sem_clear(clear)
        nc.all_engine_barrier()
        pg.emit(nc, sems)
        nc.all_engine_barrier()
        nc.gpsimd.dma_reset(clear)
        nc.gpsimd.sem_clear(clear)
        nc.all_engine_barrier()

    return nc


def host_constants():
    ident = np.eye(P, dtype=np.float32)
    ones_col = np.ones((1, P), dtype=np.float32)
    onesT = np.ones((P, P), dtype=np.float32)
    ones128 = np.ones((P, 1), dtype=np.float32)
    sel16 = np.zeros((16, D), dtype=np.float32)
    for c in range(8):
        sel16[2 * c, c * 128:c * 128 + 64] = 1.0
        sel16[2 * c + 1, c * 128 + 64:c * 128 + 128] = 1.0
    identb = np.eye(P, dtype=np.float32).astype(ml_dtypes.bfloat16)
    return {"c_ident": ident, "c_ones_col": ones_col, "c_onesT": onesT,
            "c_ones128": ones128, "c_sel16": sel16, "c_identb": identb}


def kernel(**inputs):
    B_total = inputs["L0_attn_x"].shape[0]
    Bc = B_total // N_CORES
    nc = build_program(B=Bc, T=inputs["L0_attn_x"].shape[1], n_blocks=4)
    consts = host_constants()
    wnames = [k for k in inputs if k != "L0_attn_x"]
    in_maps = []
    for core in range(N_CORES):
        m = {"x": np.ascontiguousarray(
                inputs["L0_attn_x"][core * Bc:(core + 1) * Bc])}
        for k in wnames:
            m[k] = np.asarray(inputs[k])
        m.update(consts)
        in_maps.append(m)
    res = run_bass_kernel_spmd(nc, in_maps, core_ids=list(range(N_CORES)))
    return np.concatenate([r["y"] for r in res.results], axis=0)
